# version 14
# baseline (speedup 1.0000x reference)
import os
import numpy as np
import ml_dtypes
import concourse.bass as bass
import concourse.mybir as mybir
from concourse.bass_utils import run_bass_kernel_spmd

F32 = mybir.dt.float32
BF16 = mybir.dt.bfloat16
F16 = mybir.dt.float16
I32 = mybir.dt.int32
AF = mybir.ActivationFunctionType
ALU = mybir.AluOpType
BF16NP = ml_dtypes.bfloat16

NCORE = 8
TPC = 1024
D = 4096
DI = 11008
KVD = 1024
NBLK = 11


class Ctr:
    def __init__(self, sem):
        self.sem = sem
        self.n = 0

    def inc(self, k=1):
        self.n += k
        return self.n



def build_dequant():
    nc = bass.Bass("TRN2", target_bir_lowering=False)
    gq = nc.dram_tensor("gq", [D, NBLK * 16], I32, kind="ExternalInput").ap()
    gz = nc.dram_tensor("gz", [32, NBLK * 16], I32, kind="ExternalInput").ap()
    gs = nc.dram_tensor("gs", [32, NBLK * 128], F32, kind="ExternalInput").ap()
    uq = nc.dram_tensor("uq", [D, NBLK * 16], I32, kind="ExternalInput").ap()
    uz = nc.dram_tensor("uz", [32, NBLK * 16], I32, kind="ExternalInput").ap()
    us = nc.dram_tensor("us", [32, NBLK * 128], F32, kind="ExternalInput").ap()
    dq = nc.dram_tensor("dq", [DI, 64], I32, kind="ExternalInput").ap()
    dz = nc.dram_tensor("dz", [86, 64], I32, kind="ExternalInput").ap()
    ds = nc.dram_tensor("ds", [86, 512], F32, kind="ExternalInput").ap()

    wg_o = nc.dram_tensor("wg_o", [D, NBLK * 128], BF16, kind="ExternalOutput").ap()
    wu_o = nc.dram_tensor("wu_o", [D, NBLK * 128], BF16, kind="ExternalOutput").ap()
    wd_o = nc.dram_tensor("wd_o", [DI, 512], BF16, kind="ExternalOutput").ap()

    W = NBLK * 128
    from contextlib import ExitStack
    with ExitStack() as _ctx:
        def sbx(*a): return _ctx.enter_context(nc.sbuf_tensor(*a))
        def psx(*a): return _ctx.enter_context(nc.psum_tensor(*a))
        def semx(n): return _ctx.enter_context(nc.semaphore(n))
        ones1 = sbx("ones1", [1, 128], BF16)
        q_sb = sbx("q_sb", [128, 2, NBLK * 16], I32)
        z_sb = sbx("z_sb", [1, 2, NBLK * 16], I32)
        s_sb = sbx("s_sb", [1, 2, W], F32)
        z_bf = sbx("z_bf", [1, 2, W], BF16)
        s_bf = sbx("s_bf", [1, 2, W], BF16)
        znib = sbx("znib", [1, 2, W], I32)
        nib = sbx("nib", [128, 2, W], I32)
        tmp = sbx("tmp", [128, W], F32)
        w_sb = sbx("w_sb", [128, 2, W], BF16)
        ps_z = psx("ps_z", [128, W], F32)
        ps_s = psx("ps_s", [128, W], F32)
        ld_s = semx("ld"); ld1_s = semx("ld1"); st1_s = semx("st1")
        pz_s = semx("pz"); bc_s = semx("bc"); vd_s = semx("vd"); vr_s = semx("vr")
        st_s = semx("st"); ms_s = semx("ms")
        block = _ctx.enter_context(nc.Block())
        ldp = [Ctr(ld_s), Ctr(ld1_s)]; pz = Ctr(pz_s); bc = Ctr(bc_s)
        vd = Ctr(vd_s); stp = [Ctr(st_s), Ctr(st1_s)]; ms = Ctr(ms_s); vr = Ctr(vr_s)

        iters = [("gu", m, kc) for m in range(2) for kc in range(32)]
        iters += [("d", 0, kc) for kc in range(86)]
        NIT = len(iters)

        def widths(it):
            kind = it[0]
            return (W, NBLK * 16) if kind == "gu" else (512, 64)

        @block.vector
        def _(v):
            v.memset(ones1[:], 1.0).then_inc(ms.sem, 1)
            ms.inc()

        @block.gpsimd
        def _(g):
            for i, it in enumerate(iters):
                kind, m, kc = it
                p = i % 2
                if i >= 2:
                    g.wait_ge(vd.sem, i - 1)
                    g.wait_ge(bc.sem, i - 1)
                lsem = ld_s if p == 0 else ld1_s
                if kind == "gu":
                    qt, zt, stn = (gq, gz, gs) if m == 0 else (uq, uz, us)
                    g.dma_start(out=q_sb[:, p], in_=qt[kc * 128:(kc + 1) * 128, :]).then_inc(lsem, 16)
                    g.dma_start(out=z_sb[:, p], in_=zt[kc:kc + 1, :]).then_inc(lsem, 16)
                    g.dma_start(out=s_sb[:, p], in_=stn[kc:kc + 1, :]).then_inc(lsem, 16)
                else:
                    g.dma_start(out=q_sb[:, p, 0:64], in_=dq[kc * 128:(kc + 1) * 128, :]).then_inc(lsem, 16)
                    g.dma_start(out=z_sb[:, p, 0:64], in_=dz[kc:kc + 1, :]).then_inc(lsem, 16)
                    g.dma_start(out=s_sb[:, p, 0:512], in_=ds[kc:kc + 1, :]).then_inc(lsem, 16)
                ldp[p].inc(48)

        @block.vector
        def _(v):
            for i, it in enumerate(iters):
                kind, m, kc = it
                p = i % 2
                wW, wQ = widths(it)
                nb = wW // 128
                v.wait_ge(ld_s if p == 0 else ld1_s, 48 * (i // 2 + 1))
                if i >= 2:
                    v.wait_ge(st_s if p == 0 else st1_s, 16 * ((i - 2) // 2 + 1))
                for j in range(8):
                    zi = z_sb[:, p, 0:wQ].rearrange("o (b w) -> o b w", w=16)
                    zo = znib[:, p].rearrange("o (b n) -> o b n", n=128)[:, 0:nb, j::8]
                    last = v.tensor_scalar(zo, zi, 28 - 4 * j, 28,
                                           ALU.logical_shift_left, ALU.logical_shift_right)
                last.then_inc(vr.sem, 1)
                v.wait_ge(vr.sem, vr.inc())
                v.tensor_copy(z_bf[:, p, 0:wW], znib[:, p, 0:wW])
                v.tensor_copy(s_bf[:, p, 0:wW], s_sb[:, p, 0:wW]).then_inc(pz.sem, 1)
                pz.inc()
                for j in range(8):
                    qi = q_sb[:, p, 0:wQ].rearrange("p (b w) -> p b w", w=16)
                    qo = nib[:, p].rearrange("p (b n) -> p b n", n=128)[:, 0:nb, j::8]
                    last = v.tensor_scalar(qo, qi, 28 - 4 * j, 28,
                                           ALU.logical_shift_left, ALU.logical_shift_right)
                last.then_inc(vr.sem, 1)
                v.wait_ge(bc.sem, i + 1)
                v.wait_ge(vr.sem, vr.inc())
                v.tensor_tensor(tmp[:, 0:wW], nib[:, p, 0:wW], ps_z[:, 0:wW],
                                ALU.subtract).then_inc(vr.sem, 1)
                v.wait_ge(vr.sem, vr.inc())
                v.tensor_tensor(w_sb[:, p, 0:wW], tmp[:, 0:wW], ps_s[:, 0:wW],
                                ALU.mult).then_inc(vd.sem, 1)
                vd.inc()

        @block.tensor
        def _(t):
            t.wait_ge(ms.sem, 1)
            for i, it in enumerate(iters):
                kind, m, kc = it
                p = i % 2
                wW, _ = widths(it)
                t.wait_ge(pz.sem, i + 1)
                if i >= 1:
                    t.wait_ge(vd.sem, i)
                for c0 in range(0, wW, 512):
                    c1 = min(c0 + 512, wW)
                    t.matmul(ps_z[:, c0:c1], ones1[:], z_bf[:, p, c0:c1], start=True, stop=True)
                for c0 in range(0, wW, 512):
                    c1 = min(c0 + 512, wW)
                    mm = t.matmul(ps_s[:, c0:c1], ones1[:], s_bf[:, p, c0:c1], start=True, stop=True)
                mm.then_inc(bc.sem, 1)
                bc.inc()

        @block.sync
        def _(s):
            for i, it in enumerate(iters):
                kind, m, kc = it
                p = i % 2
                wW, _ = widths(it)
                s.wait_ge(vd.sem, i + 1)
                ssem = st_s if p == 0 else st1_s
                if kind == "gu":
                    dst = wg_o if m == 0 else wu_o
                    s.dma_start(out=dst[kc * 128:(kc + 1) * 128, :], in_=w_sb[:, p, 0:wW]).then_inc(ssem, 16)
                else:
                    s.dma_start(out=wd_o[kc * 128:(kc + 1) * 128, :], in_=w_sb[:, p, 0:512]).then_inc(ssem, 16)
                stp[p].inc(16)
            s.wait_ge(st_s, 16 * ((NIT + 1) // 2))
            s.wait_ge(st1_s, 16 * (NIT // 2))
    return nc


def _dequant_inputs(inp):
    maps = []
    for c in range(NCORE):
        blocks = [min(c * NBLK + j, 85) for j in range(NBLK)]
        def gwords(a):
            return np.ascontiguousarray(
                np.concatenate([a[:, b * 16:(b + 1) * 16] for b in blocks], axis=1))
        def gcols(a):
            return np.ascontiguousarray(
                np.concatenate([a[:, b * 128:(b + 1) * 128] for b in blocks], axis=1))
        m = dict(
            gq=gwords(inp["L0_ffn_gate_qweight"]), gz=gwords(inp["L0_ffn_gate_qzeros"]),
            gs=gcols(inp["L0_ffn_gate_scales"]),
            uq=gwords(inp["L0_ffn_up_qweight"]), uz=gwords(inp["L0_ffn_up_qzeros"]),
            us=gcols(inp["L0_ffn_up_scales"]),
            dq=np.ascontiguousarray(inp["L0_ffn_down_qweight"][:, c * 64:(c + 1) * 64]),
            dz=np.ascontiguousarray(inp["L0_ffn_down_qzeros"][:, c * 64:(c + 1) * 64]),
            ds=np.ascontiguousarray(inp["L0_ffn_down_scales"][:, c * 512:(c + 1) * 512]),
        )
        maps.append(m)
    return maps


def run_dequant(inp):
    nc = build_dequant()
    maps = _dequant_inputs(inp)
    res = run_bass_kernel_spmd(nc, maps, core_ids=list(range(NCORE)))
    wg_blocks = np.zeros((86, D, 128), dtype=BF16NP)
    wu_blocks = np.zeros((86, D, 128), dtype=BF16NP)
    wd = np.zeros((DI, D), dtype=BF16NP)
    for c in range(NCORE):
        r = res.results[c]
        gb = r["wg_o"].reshape(D, NBLK, 128).transpose(1, 0, 2)
        ub = r["wu_o"].reshape(D, NBLK, 128).transpose(1, 0, 2)
        for j in range(NBLK):
            b = c * NBLK + j
            if b < 86:
                wg_blocks[b] = gb[j]
                wu_blocks[b] = ub[j]
        wd[:, c * 512:(c + 1) * 512] = r["wd_o"]
    return wg_blocks, wu_blocks, wd


def kernel_unfused(**inputs):
    inp = inputs
    wg_blk, wu_blk, wd = run_dequant(inp)
    kvt, ksum = run_kv_partial(inp)
    sT = run_l0_attn(inp, kvt, ksum)
    x1 = run_l0_ffn(inp, wg_blk, wu_blk, wd, sT)
    qkv = run_l1_qkv(inp, x1)
    s1T = run_l1_attn(inp, qkv)
    out = run_l1_ffn(inp, x1, s1T)
    return out.reshape(2, 4096, 4096)


def kernel(**inputs):
    return kernel_fused(**inputs)



def build_kv_partial():
    nc = bass.Bass("TRN2", target_bir_lowering=False)
    x = nc.dram_tensor("x", [TPC, D], F32, kind="ExternalInput").ap()
    wk = nc.dram_tensor("wk", [KVD, D], F32, kind="ExternalInput").ap()
    wv = nc.dram_tensor("wv", [KVD, D], F32, kind="ExternalInput").ap()
    bk = nc.dram_tensor("bk", [1, KVD], F32, kind="ExternalInput").ap()
    bv = nc.dram_tensor("bv", [1, KVD], F32, kind="ExternalInput").ap()
    eye = nc.dram_tensor("eye", [128, 128], BF16, kind="ExternalInput").ap()
    kvt_o = nc.dram_tensor("kvt_o", [128, 1024], F32, kind="ExternalOutput").ap()
    ksum_o = nc.dram_tensor("ksum_o", [1, 1024], F32, kind="ExternalOutput").ap()

    from contextlib import ExitStack
    with ExitStack() as _ctx:
        def sb(*a): return _ctx.enter_context(nc.sbuf_tensor(*a))
        def ps(*a): return _ctx.enter_context(nc.psum_tensor(*a))
        def sem(n): return _ctx.enter_context(nc.semaphore(n))
        eye_sb = sb("eye_sb", [128, 128], BF16)
        ones_c = sb("ones_c", [128, 128], BF16)
        zrow = sb("zrow", [1, 512], BF16)
        wkt = sb("wkt", [128, 32, KVD], BF16)
        wvt = sb("wvt", [128, 32, KVD], BF16)
        bk_bc = sb("bk_bc", [128, KVD], F32)
        bv_bc = sb("bv_bc", [128, KVD], F32)
        ps_a = ps("ps_a", [128, 512], F32)
        ps_b = ps("ps_b", [128, 512], F32)
        ps_t = ps("ps_t", [128, 512], BF16)
        ps_kv1 = ps("ps_kv1", [128, 512], F32)
        ps_kv2 = ps("ps_kv2", [128, 512], F32)
        ps_ks1 = ps("ps_ks1", [128, 512], F32)
        ps_ks2 = ps("ps_ks2", [128, 512], F32)
        ld_s = sem("ld"); pe_s = sem("pe"); vv_s = sem("vv"); ss_s = sem("ss"); st_s = sem("st")
        block = _ctx.enter_context(nc.Block())
        ld = Ctr(ld_s); pe = Ctr(pe_s); vv = Ctr(vv_s); ss = Ctr(ss_s); st = Ctr(st_s)
        G, Sy, T, V, Sc = block.gpsimd, block.sync, block.tensor, block.vector, block.scalar

        def _(g):
            g.dma_start(out=eye_sb[:], in_=eye).then_inc(ld.sem, 16)
            g.dma_start(out=bk_bc[:], in_=bk.partition_broadcast(128)).then_inc(ld.sem, 16)
            g.dma_start(out=bv_bc[:], in_=bv.partition_broadcast(128)).then_inc(ld.sem, 16)
        G(_); ld.inc(48)
        def _(v):
            v.memset(ones_c[:], 1.0)
            v.memset(zrow[:], 0.0).then_inc(vv.sem, 1)
        V(_); ones_rdy = vv.inc()
        def _(t):
            t.wait_ge(vv.sem, ones_rdy)
            t.matmul(ps_kv1[:], zrow[0:1, 0:128], zrow[:], start=True, stop=True)
            t.matmul(ps_kv2[:], zrow[0:1, 0:128], zrow[:], start=True, stop=True).then_inc(pe.sem, 1)
        T(_); kvclr = pe.inc()

        wrow_cm = nc.sbuf_tensor("wrow", [128, D], BF16)
        wrow = wrow_cm.__enter__()
        for wi in range(2):
            w_in = wk if wi == 0 else wv
            dstT = wkt if wi == 0 else wvt
            for r in range(8):
                wrow_free = pe.n
                def _(g, wrow_free=wrow_free, w_in=w_in, r=r):
                    if wrow_free > 0:
                        g.wait_ge(pe.sem, wrow_free)
                    g.dma_start(out=wrow[:], in_=w_in[r * 128:(r + 1) * 128, :]).then_inc(ld.sem, 16)
                G(_); row_ld = ld.inc(16)
                for grp in range(8):
                    prev_evac = vv.n
                    def _(t, row_ld=row_ld, grp=grp, prev_evac=prev_evac):
                        t.wait_ge(ld.sem, row_ld)
                        t.wait_ge(vv.sem, prev_evac)
                        for j in range(4):
                            cc = grp * 4 + j
                            mm = t.matmul(ps_t[:, j * 128:(j + 1) * 128],
                                          wrow[:, cc * 128:(cc + 1) * 128], eye_sb[:],
                                          is_transpose=True, start=(j == 0), stop=True,
                                          skip_group_check=True)
                        mm.then_inc(pe.sem, 1)
                    T(_); tr = pe.inc()
                    def _(v, tr=tr, dstT=dstT, grp=grp, r=r):
                        v.wait_ge(pe.sem, tr)
                        v.tensor_copy(dstT[:, grp * 4:grp * 4 + 4, r * 128:(r + 1) * 128],
                                      ps_t.ap().rearrange("p (j n) -> p j n", n=128)).then_inc(vv.sem, 1)
                    V(_); vv.inc()

        wrow_cm.__exit__(None, None, None)
        x_t = sb("x_t", [128, D], F32)
        stats = sb("stats", [128, 48], F32)
        mv = sb("mv", [128, 2], F32)
        sc1 = sb("sc1", [128, 4], F32)
        xn_bf = sb("xn_bf", [128, D], BF16)
        xT_t = sb("xT_t", [128, 32, 128], BF16)
        kf = sb("kf", [128, 512], F32)
        k2 = sb("k2", [128, 512], F32)
        k3 = sb("k3", [128, 512], F32)
        k_all = sb("k_all", [128, 8, KVD], BF16)
        v_all = sb("v_all", [128, 8, KVD], BF16)
        prologue_pe = pe.n
        prologue_vv = vv.n
        for i in range(8):
            x_ready_after = vv.n
            def _(g, x_ready_after=x_ready_after, i=i):
                if i == 0:
                    g.wait_ge(pe.sem, prologue_pe)
                    g.wait_ge(vv.sem, prologue_vv)
                else:
                    g.wait_ge(vv.sem, x_ready_after)
                g.dma_start(out=x_t[:], in_=x[i * 128:(i + 1) * 128, :]).then_inc(ld.sem, 16)
            G(_); x_ld = ld.inc(16)

            def _(v, x_ld=x_ld):
                v.wait_ge(ld.sem, x_ld)
                for cch in range(8):
                    op = v.bn_stats(stats[:, cch * 6:(cch + 1) * 6], x_t[:, cch * 512:(cch + 1) * 512])
                op.then_inc(vv.sem, 1)
            V(_); w1 = vv.inc()
            def _(v, w1=w1):
                v.wait_ge(vv.sem, w1)
                v.bn_aggr(mv[:], stats[:]).then_inc(vv.sem, 1)
            V(_); w2 = vv.inc()
            def _(v, w2=w2):
                v.wait_ge(vv.sem, w2)
                v.tensor_scalar(sc1[:, 0:1], mv[:, 1:2], 1.0e-5, None, ALU.add).then_inc(vv.sem, 1)
            V(_); w3 = vv.inc()
            def _(s, w3=w3):
                s.wait_ge(vv.sem, w3)
                s.activation(sc1[:, 3:4], sc1[:, 0:1], AF.Ln).then_inc(ss.sem, 1)
            Sc(_); s1 = ss.inc()
            def _(s, s1=s1):
                s.wait_ge(ss.sem, s1)
                s.activation(sc1[:, 1:2], sc1[:, 3:4], AF.Exp, scale=-0.5).then_inc(ss.sem, 1)
            Sc(_); s2 = ss.inc()
            def _(v, s2=s2):
                v.wait_ge(ss.sem, s2)
                v.tensor_scalar(sc1[:, 2:3], mv[:, 0:1], sc1[:, 1:2], -1.0, ALU.mult, ALU.mult).then_inc(vv.sem, 1)
            V(_); w4 = vv.inc()
            def _(v, w4=w4):
                v.wait_ge(vv.sem, w4)
                v.tensor_scalar(xn_bf[:], x_t[:], sc1[:, 1:2], sc1[:, 2:3], ALU.mult, ALU.add).then_inc(vv.sem, 1)
            V(_); xn_rdy = vv.inc()

            for grp in range(8):
                prev_evac = vv.n
                def _(t, xn_rdy=xn_rdy, grp=grp, prev_evac=prev_evac):
                    t.wait_ge(vv.sem, xn_rdy)
                    t.wait_ge(vv.sem, prev_evac)
                    for j in range(4):
                        cc = grp * 4 + j
                        mm = t.matmul(ps_t[:, j * 128:(j + 1) * 128],
                                      xn_bf[:, cc * 128:(cc + 1) * 128], eye_sb[:],
                                      is_transpose=True, start=(j == 0), stop=True,
                                      skip_group_check=True)
                    mm.then_inc(pe.sem, 1)
                T(_); tr = pe.inc()
                def _(v, tr=tr, grp=grp):
                    v.wait_ge(pe.sem, tr)
                    v.tensor_copy(xT_t[:, grp * 4:grp * 4 + 4, :],
                                  ps_t.ap().rearrange("p (j n) -> p j n", n=128)).then_inc(vv.sem, 1)
                V(_); vv.inc()
            xT_rdy = vv.n

            psums = [ps_a, ps_b]
            psum_free = [vv.n, vv.n]
            for which in range(2):
                wt = wkt if which == 0 else wvt
                for half in range(2):
                    slot = (which * 2 + half) % 2
                    dst = psums[slot]
                    def _(t, dst=dst, wt=wt, half=half, xT_rdy=xT_rdy, free=psum_free[slot]):
                        t.wait_ge(vv.sem, max(xT_rdy, free))
                        for cc in range(32):
                            mm = t.matmul(dst[:], xT_t[:, cc], wt[:, cc, half * 512:(half + 1) * 512],
                                          start=(cc == 0), stop=(cc == 31))
                        mm.then_inc(pe.sem, 1)
                    T(_); mm_done = pe.inc()
                    if which == 1:
                        def _(v, mm_done=mm_done, dst=dst, half=half, i=i):
                            v.wait_ge(pe.sem, mm_done)
                            v.tensor_tensor(v_all[:, i, half * 512:(half + 1) * 512], dst[:],
                                            bv_bc[:, half * 512:(half + 1) * 512], ALU.add).then_inc(vv.sem, 1)
                        V(_); psum_free[slot] = vv.inc()
                    else:
                        def _(v, mm_done=mm_done, dst=dst, half=half):
                            v.wait_ge(pe.sem, mm_done)
                            v.tensor_tensor(kf[:], dst[:], bk_bc[:, half * 512:(half + 1) * 512],
                                            ALU.add).then_inc(vv.sem, 1)
                        V(_); a1 = vv.inc()
                        psum_free[slot] = a1
                        def _(v, a1=a1):
                            v.wait_ge(vv.sem, a1)
                            v.tensor_scalar(k2[:], kf[:], 0.0, None, ALU.min).then_inc(vv.sem, 1)
                        V(_); a2 = vv.inc()
                        def _(s, a2=a2):
                            s.wait_ge(vv.sem, a2)
                            s.activation(k3[:], k2[:], AF.Exp).then_inc(ss.sem, 1)
                        Sc(_); a3 = ss.inc()
                        def _(v, a3=a3, half=half, i=i):
                            v.wait_ge(ss.sem, a3)
                            v.tensor_scalar(k2[:], kf[:], 0.0, None, ALU.max).then_inc(vv.sem, 1)
                        V(_); a4 = vv.inc()
                        def _(v, a3=a3, a4=a4, half=half, i=i):
                            v.wait_ge(ss.sem, a3)
                            v.wait_ge(vv.sem, a4)
                            v.tensor_tensor(k_all[:, i, half * 512:(half + 1) * 512], k3[:],
                                            k2[:], ALU.add).then_inc(vv.sem, 1)
                        V(_); vv.inc()
            kv_rdy = vv.n

            def _(t, kv_rdy=kv_rdy, i=i):
                t.wait_ge(vv.sem, kv_rdy)
                t.wait_ge(vv.sem, ones_rdy)
                for h in range(8):
                    dst = ps_kv1 if h < 4 else ps_kv2
                    t.matmul(dst[:, (h % 4) * 128:(h % 4) * 128 + 128],
                             k_all[:, i, h * 128:(h + 1) * 128],
                             v_all[:, i, h * 128:(h + 1) * 128],
                             start=False, stop=(i == 7),
                             skip_group_check=True)
                for half in range(2):
                    dstk = ps_ks1 if half == 0 else ps_ks2
                    mm = t.matmul(dstk[:], ones_c[:], k_all[:, i, half * 512:(half + 1) * 512],
                                  start=(i == 0), stop=(i == 7), skip_group_check=True)
                mm.then_inc(pe.sem, 1)
            T(_); acc_done = pe.inc()

        def _(v, acc_done=acc_done):
            v.wait_ge(pe.sem, acc_done)
            v.tensor_copy(kf[:], ps_kv1[:])
            v.tensor_copy(k2[:], ps_kv2[:])
            v.tensor_copy(k3[0:1, :], ps_ks1[0:1, :]).then_inc(vv.sem, 1)
        V(_); ev = vv.inc()
        def _(s, ev=ev):
            s.wait_ge(vv.sem, ev)
            s.dma_start(out=kvt_o[:, 0:512], in_=kf[:]).then_inc(st.sem, 16)
            s.dma_start(out=kvt_o[:, 512:1024], in_=k2[:]).then_inc(st.sem, 16)
            s.dma_start(out=ksum_o[:, 0:512], in_=k3[0:1, :]).then_inc(st.sem, 16)
        Sy(_); st.inc(48)
        def _(v):
            v.wait_ge(st.sem, st.n)
            v.tensor_copy(k3[0:1, :], ps_ks2[0:1, :]).then_inc(vv.sem, 1)
        V(_); ev2 = vv.inc()
        def _(s, ev2=ev2):
            s.wait_ge(vv.sem, ev2)
            s.dma_start(out=ksum_o[:, 512:1024], in_=k3[0:1, :]).then_inc(st.sem, 16)
            s.wait_ge(st.sem, st.n + 16)
        Sy(_); st.inc(16)
    return nc


def run_kv_partial(inp):
    nc = build_kv_partial()
    x = inp["L0_attn_hidden_states"].reshape(8192, D)
    eye = np.eye(128, dtype=BF16NP)
    maps = []
    for c in range(NCORE):
        maps.append(dict(
            x=np.ascontiguousarray(x[c * TPC:(c + 1) * TPC]),
            wk=inp["L0_attn_Wk"], wv=inp["L0_attn_Wv"],
            bk=inp["L0_attn_bk"].reshape(1, KVD), bv=inp["L0_attn_bv"].reshape(1, KVD),
            eye=eye))
    res = run_bass_kernel_spmd(nc, maps, core_ids=list(range(NCORE)))
    kvt = np.stack([res.results[c]["kvt_o"] for c in range(NCORE)])
    ksum = np.stack([res.results[c]["ksum_o"] for c in range(NCORE)])
    return kvt, ksum



def build_l0_attn():
    nc = bass.Bass("TRN2", target_bir_lowering=False)
    x = nc.dram_tensor("x", [TPC, D], F32, kind="ExternalInput").ap()
    wq = nc.dram_tensor("wq", [D, D], F32, kind="ExternalInput").ap()
    bq = nc.dram_tensor("bq", [1, D], F32, kind="ExternalInput").ap()
    wo = nc.dram_tensor("wo", [D, D], F32, kind="ExternalInput").ap()
    bo_pp = nc.dram_tensor("bo_pp", [128, 32], F32, kind="ExternalInput").ap()
    kvt_p = nc.dram_tensor("kvt_p", [4 * 128, 1024], F32, kind="ExternalInput").ap()
    ksum_p = nc.dram_tensor("ksum_p", [1, 4 * 1024], F32, kind="ExternalInput").ap()
    eye = nc.dram_tensor("eye", [128, 128], BF16, kind="ExternalInput").ap()
    eyef = nc.dram_tensor("eyef", [128, 128], F32, kind="ExternalInput").ap()
    sT_o = nc.dram_tensor("sT_o", [D, TPC], F32, kind="ExternalOutput").ap()
    xt_scr = nc.dram_tensor("xt_scr", [8, 128, D], BF16).ap()

    from contextlib import ExitStack
    with ExitStack() as _ctx:
        def sb(*a): return _ctx.enter_context(nc.sbuf_tensor(*a))
        def psm(*a): return _ctx.enter_context(nc.psum_tensor(*a))
        def sem(n): return _ctx.enter_context(nc.semaphore(n))
        eye_sb = sb("eye_sb", [128, 128], BF16)
        eyef_sb = sb("eyef_sb", [128, 128], F32)
        kvt_bf = sb("kvt_bf", [128, 1024], BF16)
        ksr = sb("ksr", [1, 2048], F32)
        ks_in = sb("ks_in", [1, 1024], F32)
        recip_pp = sb("recip_pp", [128, 8], F32)
        bo_sb = sb("bo_sb", [128, 32], F32)
        x_t = sb("x_t", [128, 2048], F32)
        stats = sb("stats", [128, 48], F32)
        mv = sb("mv", [128, 2], F32)
        sc1 = sb("sc1", [128, 4], F32)
        xn_bf = sb("xn_bf", [128, 2048], BF16)
        xT_t = sb("xT_t", [128, 32, 128], BF16)
        xts = sb("xts", [128, 2, 32, 128], BF16)
        wrow = sb("wrow", [128, D], BF16)
        wt_slab = sb("wt_slab", [128, 32, 512], BF16)
        bq_bc = sb("bq_bc", [128, 512], F32)
        qf = sb("qf", [128, 512], F32)
        q2 = sb("q2", [128, 512], F32)
        q3 = sb("q3", [128, 512], F32)
        qphi = sb("qphi", [128, 512], BF16)
        qphiT = sb("qphiT", [128, 4, 8, 128], BF16)
        attnT = sb("attnT", [128, 32, 1024], BF16)
        s_st = sb("s_st", [128, 2, 512], F32)
        ps_t = psm("ps_t", [128, 512], BF16)
        ps_tf = psm("ps_tf", [128, 512], F32)
        ps_q = psm("ps_q", [128, 512], F32)
        ps_q2 = psm("ps_q2", [128, 512], F32)
        ps_at = psm("ps_at", [128, 512], F32)
        ps_s = psm("ps_s", [128, 512], F32)
        ld_s = sem("ld"); pe_s = sem("pe"); vv_s = sem("vv"); ss_s = sem("ss"); st_s = sem("st")
        xtl_s = [sem("xtl0"), sem("xtl1")]
        bq_s = sem("bq")
        block = _ctx.enter_context(nc.Block())
        ld = Ctr(ld_s); pe = Ctr(pe_s); vv = Ctr(vv_s); ss = Ctr(ss_s); st = Ctr(st_s)
        xtl = [Ctr(xtl_s[0]), Ctr(xtl_s[1])]
        bqc = Ctr(bq_s)
        G, Sy, T, V, Sc = block.gpsimd, block.sync, block.tensor, block.vector, block.scalar

        def _(g):
            g.dma_start(out=eye_sb[:], in_=eye).then_inc(ld.sem, 16)
            g.dma_start(out=eyef_sb[:], in_=eyef).then_inc(ld.sem, 16)
            g.dma_start(out=bo_sb[:], in_=bo_pp).then_inc(ld.sem, 16)
        G(_); ld.inc(48)
        def _(g):
            g.dma_start(out=x_t[:, 0:1024], in_=kvt_p[0:128, :]).then_inc(ld.sem, 16)
        G(_); ld.inc(16)
        for j in range(1, 4):
            def _(g, j=j, w=vv.n):
                g.wait_ge(vv.sem, w)
                g.dma_start(out=x_t[:, 1024:2048], in_=kvt_p[j * 128:(j + 1) * 128, :]).then_inc(ld.sem, 16)
            G(_); ld.inc(16)
            def _(v):
                v.wait_ge(ld.sem, ld.n)
                v.tensor_tensor(x_t[:, 0:1024], x_t[:, 0:1024], x_t[:, 1024:2048], ALU.add).then_inc(vv.sem, 1)
            V(_); vv.inc()
        def _(v):
            v.wait_ge(vv.sem, vv.n)
            v.tensor_copy(kvt_bf[:], x_t[:, 0:1024]).then_inc(vv.sem, 1)
        V(_); kv_rdy = vv.inc()
        ksl_s = sem("ksl"); ksl = Ctr(ksl_s)
        def _(g):
            g.dma_start(out=ksr[:, 0:1024], in_=ksum_p[:, 0:1024]).then_inc(ksl_s, 16)
        G(_); ksl.inc(16)
        for j in range(1, 4):
            def _(g, j=j, w=vv.n):
                g.wait_ge(vv.sem, w)
                g.dma_start(out=ks_in[:], in_=ksum_p[:, j * 1024:(j + 1) * 1024]).then_inc(ksl_s, 16)
            G(_); ksl.inc(16)
            def _(v):
                v.wait_ge(ksl_s, ksl.n)
                v.tensor_tensor(ksr[:, 0:1024], ksr[:, 0:1024], ks_in[:], ALU.add).then_inc(vv.sem, 1)
            V(_); vv.inc()
        def _(v):
            v.wait_ge(vv.sem, vv.n)
            v.tensor_scalar(ksr[:, 0:1024], ksr[:, 0:1024], 1.0e-10, None, ALU.add).then_inc(vv.sem, 1)
            v.wait_ge(vv.sem, vv.n + 1)
            v.reciprocal(ksr[:, 1024:2048], ksr[:, 0:1024]).then_inc(vv.sem, 1)
        V(_); rec_rdy = vv.inc(2)
        def _(t, rec_rdy=rec_rdy):
            t.wait_ge(vv.sem, rec_rdy)
            for h in range(8):
                mm = t.matmul(ps_tf[:, h:h + 1], ksr[0:1, 1024 + h * 128:1024 + (h + 1) * 128],
                              eyef_sb[0:1, 0:1], start=(h == 0), stop=True,
                              skip_group_check=True)
            mm.then_inc(pe.sem, 1)
            t.drain()
        T(_); rt = pe.inc()
        def _(v, rt=rt):
            v.wait_ge(pe.sem, rt)
            v.tensor_copy(recip_pp[:], ps_tf[:, 0:8]).then_inc(vv.sem, 1)
        V(_); vv.inc()

        for i in range(8):
            prev = vv.n
            def _(g, i=i, prev=prev):
                g.wait_ge(vv.sem, prev)
                g.dma_start(out=x_t[:], in_=x[i * 128:(i + 1) * 128, 0:2048]).then_inc(ld.sem, 16)
            G(_); xl1 = ld.inc(16)
            def _(v, xl1=xl1):
                v.wait_ge(ld.sem, xl1)
                for cch in range(4):
                    op = v.bn_stats(stats[:, cch * 6:(cch + 1) * 6], x_t[:, cch * 512:(cch + 1) * 512])
                op.then_inc(vv.sem, 1)
            V(_); h1 = vv.inc()
            def _(g, i=i, h1=h1):
                g.wait_ge(vv.sem, h1)
                g.dma_start(out=x_t[:], in_=x[i * 128:(i + 1) * 128, 2048:4096]).then_inc(ld.sem, 16)
            G(_); xl2 = ld.inc(16)
            def _(v, xl2=xl2):
                v.wait_ge(ld.sem, xl2)
                for cch in range(4):
                    op = v.bn_stats(stats[:, 24 + cch * 6:24 + (cch + 1) * 6], x_t[:, cch * 512:(cch + 1) * 512])
                op.then_inc(vv.sem, 1)
                v.wait_ge(vv.sem, vv.n + 1)
                v.bn_aggr(mv[:], stats[:]).then_inc(vv.sem, 1)
                v.wait_ge(vv.sem, vv.n + 2)
                v.tensor_scalar(sc1[:, 0:1], mv[:, 1:2], 1.0e-5, None, ALU.add).then_inc(vv.sem, 1)
            V(_); h2 = vv.inc(3)
            def _(s, h2=h2):
                s.wait_ge(vv.sem, h2)
                s.activation(sc1[:, 3:4], sc1[:, 0:1], AF.Ln).then_inc(ss.sem, 1)
                s.wait_ge(ss.sem, ss.n + 1)
                s.activation(sc1[:, 1:2], sc1[:, 3:4], AF.Exp, scale=-0.5).then_inc(ss.sem, 1)
            Sc(_); s2 = ss.inc(2)
            def _(v, s2=s2):
                v.wait_ge(ss.sem, s2)
                v.tensor_scalar(sc1[:, 2:3], mv[:, 0:1], sc1[:, 1:2], -1.0, ALU.mult, ALU.mult).then_inc(vv.sem, 1)
                v.wait_ge(vv.sem, vv.n + 1)
                v.tensor_scalar(xn_bf[:], x_t[:], sc1[:, 1:2], sc1[:, 2:3], ALU.mult, ALU.add).then_inc(vv.sem, 1)
            V(_); xn_rdy = vv.inc(2)
            for half in range(2):
                if half == 1:
                    def _(g, i=i, w=vv.n):
                        g.wait_ge(vv.sem, w)
                        g.dma_start(out=x_t[:], in_=x[i * 128:(i + 1) * 128, 0:2048]).then_inc(ld.sem, 16)
                    G(_); xl3 = ld.inc(16)
                    def _(v, xl3=xl3):
                        v.wait_ge(ld.sem, xl3)
                        v.tensor_scalar(xn_bf[:], x_t[:], sc1[:, 1:2], sc1[:, 2:3], ALU.mult, ALU.add).then_inc(vv.sem, 1)
                    V(_); xn_rdy = vv.inc()
                cbase = 16 if half == 0 else 0
                for grp in range(4):
                    prev_evac = vv.n
                    def _(t, xn_rdy=xn_rdy, grp=grp, prev_evac=prev_evac):
                        t.wait_ge(vv.sem, xn_rdy)
                        t.wait_ge(vv.sem, prev_evac)
                        for j in range(4):
                            cc = grp * 4 + j
                            mm = t.matmul(ps_t[:, j * 128:(j + 1) * 128],
                                          xn_bf[:, cc * 128:(cc + 1) * 128], eye_sb[:],
                                          is_transpose=True, start=(j == 0), stop=True,
                                          skip_group_check=True)
                        mm.then_inc(pe.sem, 1)
                    T(_); tr = pe.inc()
                    def _(v, tr=tr, grp=grp, cbase=cbase):
                        v.wait_ge(pe.sem, tr)
                        v.tensor_copy(xT_t[:, cbase + grp * 4:cbase + grp * 4 + 4, :],
                                      ps_t.ap().rearrange("p (j n) -> p j n", n=128)).then_inc(vv.sem, 1)
                    V(_); vv.inc()
            def _(s, i=i, w=vv.n):
                s.wait_ge(vv.sem, w)
                s.dma_start(out=xt_scr[i], in_=xT_t.ap().rearrange("p a b -> p (a b)")).then_inc(st.sem, 16)
            Sy(_); st.inc(16)
            def _(t, w=st.n):
                pass
            def _(v, w=st.n):
                v.wait_ge(st.sem, w)
            V(_)
        ph1_done_st = st.n

        def emit_build_slab(w_src, slab):
            nonlocal_marker = None
            for sub in range(4):
                wq_used = pe.n
                def _(g, slab=slab, sub=sub, wq_used=wq_used, w_src=w_src):
                    g.wait_ge(pe.sem, wq_used)
                    g.dma_start(out=wrow[:], in_=w_src[slab * 512 + sub * 128: slab * 512 + (sub + 1) * 128, :]).then_inc(ld.sem, 16)
                G(_); wl = ld.inc(16)
                for grp in range(8):
                    prev_evac = vv.n
                    def _(t, wl=wl, grp=grp, prev_evac=prev_evac):
                        t.wait_ge(ld.sem, wl)
                        t.wait_ge(vv.sem, prev_evac)
                        for j in range(4):
                            cc = grp * 4 + j
                            mm = t.matmul(ps_t[:, j * 128:(j + 1) * 128],
                                          wrow[:, cc * 128:(cc + 1) * 128], eye_sb[:],
                                          is_transpose=True, start=(j == 0), stop=True,
                                          skip_group_check=True)
                        mm.then_inc(pe.sem, 1)
                    T(_); tr = pe.inc()
                    def _(v, tr=tr, grp=grp, sub=sub):
                        v.wait_ge(pe.sem, tr)
                        v.tensor_copy(wt_slab[:, grp * 4:grp * 4 + 4, sub * 128:(sub + 1) * 128],
                                      ps_t.ap().rearrange("p (j n) -> p j n", n=128)).then_inc(vv.sem, 1)
                    V(_); vv.inc()
            return vv.n

        for slab in range(8):
            slab_rdy = emit_build_slab(wq, slab)
            def _(g, slab=slab):
                g.dma_start(out=bq_bc[:], in_=bq[:, slab * 512:(slab + 1) * 512].partition_broadcast(128)).then_inc(bq_s, 16)
            G(_); bq_ld = bqc.inc(16)
            psq = [ps_q, ps_q2]
            psq_free = [vv.n, vv.n]
            xts_used = [pe.n, pe.n]
            for i in range(8):
                slot = i % 2
                def _(g, i=i, slot=slot, used=xts_used[slot], first=(slab == 0)):
                    if first and i < 2:
                        g.wait_ge(st.sem, ph1_done_st)
                    g.wait_ge(pe.sem, used)
                    g.dma_start(out=xts[:, slot].rearrange("p a b -> p (a b)"), in_=xt_scr[i]).then_inc(xtl_s[slot], 16)
                G(_); xtv = xtl[slot].inc(16)
                def _(t, i=i, slab_rdy=slab_rdy, slot=slot, free=psq_free[slot], xtv=xtv):
                    t.wait_ge(vv.sem, max(slab_rdy, free))
                    t.wait_ge(xtl_s[slot], xtv)
                    for cc in range(32):
                        mm = t.matmul(psq[slot][:], xts[:, slot, cc], wt_slab[:, cc],
                                      start=(cc == 0), stop=(cc == 31))
                    mm.then_inc(pe.sem, 1)
                T(_); qd = pe.inc()
                xts_used[slot] = qd
                def _(v, qd=qd, slot=slot, bq_ld=bq_ld):
                    v.wait_ge(pe.sem, qd)
                    v.wait_ge(bq_s, bq_ld)
                    v.tensor_tensor(qf[:], psq[slot][:], bq_bc[:], ALU.add).then_inc(vv.sem, 1)
                V(_); a1 = vv.inc()
                psq_free[slot] = a1
                def _(v, a1=a1):
                    v.wait_ge(vv.sem, a1)
                    v.tensor_scalar(q2[:], qf[:], 0.0, None, ALU.min).then_inc(vv.sem, 1)
                V(_); a2 = vv.inc()
                def _(s, a2=a2):
                    s.wait_ge(vv.sem, a2)
                    s.activation(q3[:], q2[:], AF.Exp).then_inc(ss.sem, 1)
                Sc(_); a3 = ss.inc()
                def _(v, a3=a3):
                    v.wait_ge(ss.sem, a3)
                    v.tensor_scalar(q2[:], qf[:], 0.0, None, ALU.max).then_inc(vv.sem, 1)
                V(_); a4 = vv.inc()
                def _(v, a3=a3, a4=a4):
                    v.wait_ge(ss.sem, a3)
                    v.wait_ge(vv.sem, a4)
                    v.tensor_tensor(qphi[:], q3[:], q2[:], ALU.add).then_inc(vv.sem, 1)
                V(_); a5 = vv.inc()
                prev_evac = vv.n
                def _(t, a5=a5, prev_evac=prev_evac):
                    t.wait_ge(vv.sem, a5)
                    t.wait_ge(vv.sem, prev_evac)
                    for j in range(4):
                        mm = t.matmul(ps_t[:, j * 128:(j + 1) * 128],
                                      qphi[:, j * 128:(j + 1) * 128], eye_sb[:],
                                      is_transpose=True, start=(j == 0), stop=True,
                                      skip_group_check=True)
                    mm.then_inc(pe.sem, 1)
                T(_); qt = pe.inc()
                def _(v, qt=qt, i=i):
                    v.wait_ge(pe.sem, qt)
                    v.tensor_copy(qphiT[:, 0:4, i, :],
                                  ps_t.ap().rearrange("p (j n) -> p j n", n=128)).then_inc(vv.sem, 1)
                V(_); vv.inc()
            q_rdy = vv.n
            for j in range(4):
                h = slab * 4 + j
                hkv = h // 4
                for tq in range(2):
                    attn_free = vv.n
                    def _(t, q_rdy=q_rdy, j=j, h=h, hkv=hkv, tq=tq, attn_free=attn_free):
                        t.wait_ge(vv.sem, max(q_rdy, attn_free))
                        t.matmul(ps_at[:], kvt_bf[:, hkv * 128:(hkv + 1) * 128],
                                 qphiT[:, j, tq * 4:(tq + 1) * 4, :].rearrange("p a b -> p (a b)"),
                                 start=True, stop=True).then_inc(pe.sem, 1)
                    T(_); ad = pe.inc()
                    def _(v, ad=ad, h=h, hkv=hkv, tq=tq):
                        v.wait_ge(pe.sem, ad)
                        v.tensor_scalar(attnT[:, h, tq * 512:(tq + 1) * 512], ps_at[:],
                                        recip_pp[:, hkv:hkv + 1], None, ALU.mult).then_inc(vv.sem, 1)
                    V(_); vv.inc()
        attn_rdy = vv.n

        sto_s = [sem("sto0"), sem("sto1")]; sto = [Ctr(sto_s[0]), Ctr(sto_s[1])]
        for slab in range(8):
            slab_rdy = emit_build_slab(wo, slab)
            pss = [ps_s, ps_q]
            pss_free = [vv.n, vv.n]
            for sub in range(4):
                o_chunk = slab * 4 + sub
                for tq in range(2):
                    slot = (sub * 2 + tq) % 2
                    def _(t, slab_rdy=slab_rdy, sub=sub, tq=tq, slot=slot, free=pss_free[slot]):
                        t.wait_ge(vv.sem, max(slab_rdy, attn_rdy, free))
                        for cc in range(32):
                            mm = t.matmul(pss[slot][:], wt_slab[:, cc, sub * 128:(sub + 1) * 128],
                                          attnT[:, cc, tq * 512:(tq + 1) * 512],
                                          start=(cc == 0), stop=(cc == 31))
                        mm.then_inc(pe.sem, 1)
                    T(_); sd = pe.inc()
                    k3i = (slab * 8 + sub * 2 + tq)
                    s_free = sto[k3i % 2].n
                    def _(v, sd=sd, o_chunk=o_chunk, slot=slot, s_free=s_free, k3i=k3i):
                        v.wait_ge(pe.sem, sd)
                        if k3i >= 2:
                            v.wait_ge(sto_s[k3i % 2], s_free)
                        v.tensor_scalar(s_st[:, k3i % 2], pss[slot][:], bo_sb[:, o_chunk:o_chunk + 1],
                                        None, ALU.add).then_inc(vv.sem, 1)
                    V(_); ev = vv.inc()
                    pss_free[slot] = ev
                    def _(s, ev=ev, o_chunk=o_chunk, tq=tq, k3i=k3i):
                        s.wait_ge(vv.sem, ev)
                        s.dma_start(out=sT_o[o_chunk * 128:(o_chunk + 1) * 128, tq * 512:(tq + 1) * 512],
                                    in_=s_st[:, k3i % 2]).then_inc(sto_s[k3i % 2], 16)
                    Sy(_); sto[k3i % 2].inc(16)
        def _(s):
            s.wait_ge(st.sem, st.n)
            s.wait_ge(sto_s[0], sto[0].n)
            s.wait_ge(sto_s[1], sto[1].n)
        Sy(_)
    return nc


def run_l0_attn(inp, kvt, ksum):
    nc = build_l0_attn()
    x = inp["L0_attn_hidden_states"].reshape(8192, D)
    eye = np.eye(128, dtype=BF16NP)
    eyef = np.eye(128, dtype=np.float32)
    maps = []
    for c in range(NCORE):
        b = c // 4
        kvtg = np.concatenate([kvt[b * 4 + j] for j in range(4)], axis=0)
        ksg = np.concatenate([ksum[b * 4 + j][0] for j in range(4)])
        maps.append(dict(
            x=np.ascontiguousarray(x[c * TPC:(c + 1) * TPC]),
            wq=inp["L0_attn_Wq"], bq=inp["L0_attn_bq"].reshape(1, D),
            wo=inp["L0_attn_Wo"],
            bo_pp=np.ascontiguousarray(inp["L0_attn_bo"].reshape(32, 128).T),
            kvt_p=np.ascontiguousarray(kvtg), ksum_p=np.ascontiguousarray(ksg.reshape(1, 4096)),
            eye=eye, eyef=eyef))
    res = run_bass_kernel_spmd(nc, maps, core_ids=list(range(NCORE)))
    return np.stack([res.results[c]["sT_o"] for c in range(NCORE)])



def build_l0_ffn():
    nc = bass.Bass("TRN2", target_bir_lowering=False)
    x = nc.dram_tensor("x", [TPC, D], F32, kind="ExternalInput").ap()
    wg_blk = nc.dram_tensor("wg_blk", [86, D, 128], BF16, kind="ExternalInput").ap()
    wu_blk = nc.dram_tensor("wu_blk", [86, D, 128], BF16, kind="ExternalInput").ap()
    wd = nc.dram_tensor("wd", [DI, D], BF16, kind="ExternalInput").ap()
    sT = nc.dram_tensor("sT", [D, TPC], F32, kind="ExternalInput").ap()
    eye = nc.dram_tensor("eye", [128, 128], BF16, kind="ExternalInput").ap()
    eyef = nc.dram_tensor("eyef", [128, 128], F32, kind="ExternalInput").ap()
    x1_o = nc.dram_tensor("x1_o", [TPC, D], F32, kind="ExternalOutput").ap()
    xt_scr = nc.dram_tensor("xt_scr", [8, 128, D], BF16).ap()

    SLABS = [list(range(s * 8, min(s * 8 + 8, 86))) for s in range(11)]

    from contextlib import ExitStack
    with ExitStack() as _ctx:
        def sb(*a): return _ctx.enter_context(nc.sbuf_tensor(*a))
        def psm(*a): return _ctx.enter_context(nc.psum_tensor(*a))
        def sem(n): return _ctx.enter_context(nc.semaphore(n))
        eye_sb = sb("eye_sb", [128, 128], BF16)
        eyef_sb = sb("eyef_sb", [128, 128], F32)
        x_t = sb("x_t", [128, 2048], F32)
        stats = sb("stats", [128, 48], F32)
        mv = sb("mv", [128, 2], F32)
        sc1 = sb("sc1", [128, 4], F32)
        xn_bf = sb("xn_bf", [128, 2048], BF16)
        xT_t = sb("xT_t", [128, 32, 128], BF16)
        xtg = sb("xtg", [128, 4, 32, 128], BF16)
        acc = sb("acc", [128, 4, 4096], F16)
        hsl = sb("hsl", [128, 8, 512], BF16)
        gw = sb("gw", [128, 2, 32, 128], BF16)
        uw = sb("uw", [128, 2, 32, 128], BF16)
        dw = sb("dw", [128, 2, 8, 512], BF16)
        hg = sb("hg", [128, 512], F32)
        s_t = sb("s_t", [128, 2048], F32)
        s_ld = sb("s_ld", [128, 16, 128], F32)
        g_t = sb("g_t", [128, 2048], F32)
        tmp = sb("tmp", [128, 2048], F32)
        ps_t = psm("ps_t", [128, 512], BF16)
        ps_tf = psm("ps_tf", [128, 512], F32)
        ps_g = psm("ps_g", [128, 512], F32)
        ps_u = psm("ps_u", [128, 512], F32)
        ps_d = psm("ps_d", [128, 512], F32)
        ps_d2 = psm("ps_d2", [128, 512], F32)
        ld_s = sem("ld"); pe_s = sem("pe"); vv_s = sem("vv"); ss_s = sem("ss"); st_s = sem("st")
        wl_s = [sem("wl0"), sem("wl1")]
        dl_s = [sem("dl0"), sem("dl1")]
        xg_s = sem("xg")
        block = _ctx.enter_context(nc.Block())
        ld = Ctr(ld_s); pe = Ctr(pe_s); vv = Ctr(vv_s); ss = Ctr(ss_s); st = Ctr(st_s)
        wl = [Ctr(wl_s[0]), Ctr(wl_s[1])]
        dl = [Ctr(dl_s[0]), Ctr(dl_s[1])]
        xg = Ctr(xg_s)
        G, Sy, T, V, Sc = block.gpsimd, block.sync, block.tensor, block.vector, block.scalar

        def _(g):
            g.dma_start(out=eye_sb[:], in_=eye).then_inc(ld.sem, 16)
            g.dma_start(out=eyef_sb[:], in_=eyef).then_inc(ld.sem, 16)
        G(_); ld.inc(32)

        for i in range(8):
            prev = vv.n
            def _(g, i=i, prev=prev):
                g.wait_ge(vv.sem, prev)
                g.dma_start(out=x_t[:], in_=x[i * 128:(i + 1) * 128, 0:2048]).then_inc(ld.sem, 16)
            G(_); xl1 = ld.inc(16)
            def _(v, xl1=xl1):
                v.wait_ge(ld.sem, xl1)
                for cch in range(4):
                    op = v.bn_stats(stats[:, cch * 6:(cch + 1) * 6], x_t[:, cch * 512:(cch + 1) * 512])
                op.then_inc(vv.sem, 1)
            V(_); h1 = vv.inc()
            def _(g, i=i, h1=h1):
                g.wait_ge(vv.sem, h1)
                g.dma_start(out=x_t[:], in_=x[i * 128:(i + 1) * 128, 2048:4096]).then_inc(ld.sem, 16)
            G(_); xl2 = ld.inc(16)
            def _(v, xl2=xl2):
                v.wait_ge(ld.sem, xl2)
                for cch in range(4):
                    op = v.bn_stats(stats[:, 24 + cch * 6:24 + (cch + 1) * 6], x_t[:, cch * 512:(cch + 1) * 512])
                op.then_inc(vv.sem, 1)
                v.wait_ge(vv.sem, vv.n + 1)
                v.bn_aggr(mv[:], stats[:]).then_inc(vv.sem, 1)
                v.wait_ge(vv.sem, vv.n + 2)
                v.tensor_scalar(sc1[:, 0:1], mv[:, 1:2], 1.0e-5, None, ALU.add).then_inc(vv.sem, 1)
            V(_); h2 = vv.inc(3)
            def _(s, h2=h2):
                s.wait_ge(vv.sem, h2)
                s.activation(sc1[:, 3:4], sc1[:, 0:1], AF.Ln).then_inc(ss.sem, 1)
                s.wait_ge(ss.sem, ss.n + 1)
                s.activation(sc1[:, 1:2], sc1[:, 3:4], AF.Exp, scale=-0.5).then_inc(ss.sem, 1)
            Sc(_); s2 = ss.inc(2)
            def _(v, s2=s2):
                v.wait_ge(ss.sem, s2)
                v.tensor_scalar(sc1[:, 2:3], mv[:, 0:1], sc1[:, 1:2], -1.0, ALU.mult, ALU.mult).then_inc(vv.sem, 1)
                v.wait_ge(vv.sem, vv.n + 1)
                v.tensor_scalar(xn_bf[:], x_t[:], sc1[:, 1:2], sc1[:, 2:3], ALU.mult, ALU.add).then_inc(vv.sem, 1)
            V(_); xn_rdy = vv.inc(2)
            for half in range(2):
                if half == 1:
                    def _(g, i=i, w=vv.n):
                        g.wait_ge(vv.sem, w)
                        g.dma_start(out=x_t[:], in_=x[i * 128:(i + 1) * 128, 0:2048]).then_inc(ld.sem, 16)
                    G(_); xl3 = ld.inc(16)
                    def _(v, xl3=xl3):
                        v.wait_ge(ld.sem, xl3)
                        v.tensor_scalar(xn_bf[:], x_t[:], sc1[:, 1:2], sc1[:, 2:3], ALU.mult, ALU.add).then_inc(vv.sem, 1)
                    V(_); xn_rdy = vv.inc()
                cbase = 16 if half == 0 else 0
                for grp in range(4):
                    prev_evac = vv.n
                    def _(t, xn_rdy=xn_rdy, grp=grp, prev_evac=prev_evac):
                        t.wait_ge(vv.sem, xn_rdy)
                        t.wait_ge(vv.sem, prev_evac)
                        for j in range(4):
                            cc = grp * 4 + j
                            mm = t.matmul(ps_t[:, j * 128:(j + 1) * 128],
                                          xn_bf[:, cc * 128:(cc + 1) * 128], eye_sb[:],
                                          is_transpose=True, start=(j == 0), stop=True,
                                          skip_group_check=True)
                        mm.then_inc(pe.sem, 1)
                    T(_); tr = pe.inc()
                    def _(v, tr=tr, grp=grp, cbase=cbase):
                        v.wait_ge(pe.sem, tr)
                        v.tensor_copy(xT_t[:, cbase + grp * 4:cbase + grp * 4 + 4, :],
                                      ps_t.ap().rearrange("p (j n) -> p j n", n=128)).then_inc(vv.sem, 1)
                    V(_); vv.inc()
            def _(s, i=i, w=vv.n):
                s.wait_ge(vv.sem, w)
                s.dma_start(out=xt_scr[i], in_=xT_t.ap().rearrange("p a b -> p (a b)")).then_inc(st.sem, 16)
            Sy(_); st.inc(16)
            def _(v, w=st.n):
                v.wait_ge(st.sem, w)
            V(_)
        ph1_st = st.n

        for g2 in range(2):
            xg_prev = pe.n
            def _(g, g2=g2, xg_prev=xg_prev):
                g.wait_ge(pe.sem, xg_prev)
                for i in range(4):
                    g.dma_start(out=xtg[:, i].rearrange("p a b -> p (a b)"),
                                in_=xt_scr[g2 * 4 + i]).then_inc(xg_s, 16)
            G(_); xg_ld = xg.inc(64)
            def _(v, w=vv.n):
                v.memset(acc.ap().rearrange("p a b -> p (a b)"), 0.0).then_inc(vv.sem, 1)
            V(_); acc_zero = vv.inc()

            prev_down_pe = pe.n
            prev_sil = ss.n
            prev_mult = vv.n
            for si, slab in enumerate(SLABS):
                for sj, ncg in enumerate(slab):
                    p = (si * 8 + sj) % 2
                    wfree = pe.n if (si * 8 + sj) >= 2 else 0
                    def _(g, ncg=ncg, p=p, wfree=wfree, g2=g2):
                        if wfree > 0:
                            g.wait_ge(pe.sem, wfree)
                        g.dma_start(out=gw[:, p],
                                    in_=wg_blk[ncg].rearrange("(c p) n -> p c n", p=128)).then_inc(wl_s[p], 16)
                        g.dma_start(out=uw[:, p],
                                    in_=wu_blk[ncg].rearrange("(c p) n -> p c n", p=128)).then_inc(wl_s[p], 16)
                    G(_); wv_t = wl[p].inc(32)
                    def _(t, p=p, wv_t=wv_t, xg_ld=xg_ld, ps_free_s=prev_sil, ps_free_v=prev_mult):
                        t.wait_ge(wl_s[p], wv_t)
                        t.wait_ge(xg_s, xg_ld)
                        t.wait_ge(ss.sem, ps_free_s)
                        t.wait_ge(vv.sem, ps_free_v)
                        for cc in range(32):
                            t.matmul(ps_g[:], gw[:, p, cc], xtg[:, 0:4, cc, :],
                                     start=(cc == 0), stop=(cc == 31))
                        for cc in range(32):
                            mm = t.matmul(ps_u[:], uw[:, p, cc], xtg[:, 0:4, cc, :],
                                          start=(cc == 0), stop=(cc == 31))
                        mm.then_inc(pe.sem, 1)
                    T(_); gu_done = pe.inc()
                    def _(s, gu_done=gu_done):
                        s.wait_ge(pe.sem, gu_done)
                        s.activation(hg[:], ps_g[:], AF.Silu).then_inc(ss.sem, 1)
                    Sc(_); sil = ss.inc()
                    prev_sil = sil
                    def _(v, sil=sil, sj=sj, prev_down_pe=prev_down_pe):
                        v.wait_ge(ss.sem, sil)
                        v.wait_ge(pe.sem, prev_down_pe)
                        v.tensor_tensor(hsl[:, sj], hg[:], ps_u[:], ALU.mult).then_inc(vv.sem, 1)
                    V(_); prev_mult = vv.inc()
                h_rdy = vv.n
                S = len(slab)
                psd = [ps_d, ps_d2]
                psd_free = [vv.n, vv.n]
                for o2 in range(8):
                    p = o2 % 2
                    dfree = pe.n if o2 >= 2 else 0
                    def _(g, slab=slab, o2=o2, p=p, dfree=dfree):
                        if dfree > 0:
                            g.wait_ge(pe.sem, dfree)
                        for sj, ncg in enumerate(slab):
                            g.dma_start(out=dw[:, p, sj],
                                        in_=wd[ncg * 128:(ncg + 1) * 128, o2 * 512:(o2 + 1) * 512]).then_inc(dl_s[p], 16)
                    G(_); dv_t = dl[p].inc(16 * len(slab))
                    for t4 in range(4):
                        slot = (o2 * 4 + t4) % 2
                        def _(t, p=p, dv_t=dv_t, t4=t4, S=S, slot=slot, h_rdy=h_rdy, free=psd_free[slot]):
                            t.wait_ge(dl_s[p], dv_t)
                            t.wait_ge(vv.sem, max(h_rdy, free))
                            for sj in range(S):
                                mm = t.matmul(psd[slot][:], hsl[:, sj, t4 * 128:(t4 + 1) * 128],
                                              dw[:, p, sj], start=(sj == 0), stop=(sj == S - 1))
                            mm.then_inc(pe.sem, 1)
                        T(_); dd = pe.inc()
                        def _(v, dd=dd, t4=t4, o2=o2, slot=slot):
                            v.wait_ge(pe.sem, dd)
                            v.tensor_tensor(acc[:, t4, o2 * 512:(o2 + 1) * 512],
                                            acc[:, t4, o2 * 512:(o2 + 1) * 512],
                                            psd[slot][:], ALU.add).then_inc(vv.sem, 1)
                        V(_); psd_free[slot] = vv.inc()
                prev_down_pe = pe.n
            ffn_done = vv.n

            for t4 in range(4):
                i = g2 * 4 + t4
                for half in range(2):
                    w0 = vv.n
                    def _(g, i=i, half=half, w0=w0):
                        g.wait_ge(vv.sem, w0)
                        g.dma_start(out=x_t[:], in_=x[i * 128:(i + 1) * 128, half * 2048:(half + 1) * 2048]).then_inc(ld.sem, 16)
                        g.dma_start(out=s_ld.ap(),
                                    in_=sT[half * 2048:(half + 1) * 2048, i * 128:(i + 1) * 128]
                                    .rearrange("(a p) t -> p a t", p=128)).then_inc(ld.sem, 16)
                    G(_); xl = ld.inc(32)
                    for grp in range(4):
                        prev_evac = vv.n
                        def _(t, grp=grp, prev_evac=prev_evac, ffn_done=ffn_done, xl=xl):
                            t.wait_ge(vv.sem, max(prev_evac, ffn_done))
                            t.wait_ge(ld.sem, xl)
                            for j in range(4):
                                oc = grp * 4 + j
                                mm = t.matmul(ps_tf[:, j * 128:(j + 1) * 128],
                                              s_ld[:, oc, :],
                                              eyef_sb[:], is_transpose=True, start=(j == 0), stop=True,
                                              skip_group_check=True)
                            mm.then_inc(pe.sem, 1)
                        T(_); tr = pe.inc()
                        def _(v, tr=tr, grp=grp):
                            v.wait_ge(pe.sem, tr)
                            v.tensor_copy(s_t[:, grp * 512:(grp + 1) * 512], ps_tf[:]).then_inc(vv.sem, 1)
                        V(_); vv.inc()
                    s_rdy = vv.n
                    def _(s, t4=t4, half=half, w=s_rdy):
                        s.wait_ge(vv.sem, w)
                        s.activation(g_t[:], acc[:, t4, half * 2048:(half + 1) * 2048], AF.Silu).then_inc(ss.sem, 1)
                    Sc(_); gsil = ss.inc()
                    def _(v, gsil=gsil, xl=xl, i=i, half=half):
                        v.wait_ge(ss.sem, gsil)
                        v.wait_ge(ld.sem, xl)
                        v.tensor_tensor(tmp[:], s_t[:], g_t[:], ALU.mult).then_inc(vv.sem, 1)
                        v.wait_ge(vv.sem, vv.n + 1)
                        v.tensor_tensor(tmp[:], x_t[:], tmp[:], ALU.add).then_inc(vv.sem, 1)
                    V(_); cmb = vv.inc(2)
                    st_prev = st.n
                    def _(s, cmb=cmb, i=i, half=half):
                        s.wait_ge(vv.sem, cmb)
                        s.dma_start(out=x1_o[i * 128:(i + 1) * 128, half * 2048:(half + 1) * 2048],
                                    in_=tmp[:]).then_inc(st.sem, 16)
                    Sy(_); st.inc(16)
                    def _(v, w=st.n):
                        v.wait_ge(st.sem, w)
                    V(_)
        def _(s):
            s.wait_ge(st.sem, st.n)
        Sy(_)
    return nc


def run_l0_ffn(inp, wg_blk, wu_blk, wd, sT):
    nc = build_l0_ffn()
    x = inp["L0_attn_hidden_states"].reshape(8192, D)
    eye = np.eye(128, dtype=BF16NP)
    eyef = np.eye(128, dtype=np.float32)
    maps = []
    for c in range(NCORE):
        maps.append(dict(
            x=np.ascontiguousarray(x[c * TPC:(c + 1) * TPC]),
            wg_blk=wg_blk, wu_blk=wu_blk, wd=wd,
            sT=np.ascontiguousarray(sT[c]),
            eye=eye, eyef=eyef))
    res = run_bass_kernel_spmd(nc, maps, core_ids=list(range(NCORE)))
    return np.stack([res.results[c]["x1_o"] for c in range(NCORE)])



def build_l1_qkv():
    nc = bass.Bass("TRN2", target_bir_lowering=False)
    x = nc.dram_tensor("x", [TPC, D], F32, kind="ExternalInput").ap()
    wq = nc.dram_tensor("wq", [D, D], F32, kind="ExternalInput").ap()
    wk = nc.dram_tensor("wk", [D, KVD], F32, kind="ExternalInput").ap()
    wv = nc.dram_tensor("wv", [D, KVD], F32, kind="ExternalInput").ap()
    cosr = nc.dram_tensor("cosr", [TPC, 256], F32, kind="ExternalInput").ap()
    sinr = nc.dram_tensor("sinr", [TPC, 256], F32, kind="ExternalInput").ap()
    eye = nc.dram_tensor("eye", [128, 128], BF16, kind="ExternalInput").ap()
    eyef = nc.dram_tensor("eyef", [128, 128], F32, kind="ExternalInput").ap()
    q_o = nc.dram_tensor("q_o", [TPC, D], BF16, kind="ExternalOutput").ap()
    k_o = nc.dram_tensor("k_o", [TPC, KVD], BF16, kind="ExternalOutput").ap()
    v_o = nc.dram_tensor("v_o", [TPC, KVD], BF16, kind="ExternalOutput").ap()
    qn2_o = nc.dram_tensor("qn2_o", [TPC, 32], F32, kind="ExternalOutput").ap()
    knm_o = nc.dram_tensor("knm_o", [1, 1], F32, kind="ExternalOutput").ap()
    xt_scr = nc.dram_tensor("xt_scr", [8, 128, D], BF16).ap()

    from contextlib import ExitStack
    with ExitStack() as _ctx:
        def sb(*a): return _ctx.enter_context(nc.sbuf_tensor(*a))
        def psm(*a): return _ctx.enter_context(nc.psum_tensor(*a))
        def sem(n): return _ctx.enter_context(nc.semaphore(n))
        eye_sb = sb("eye_sb", [128, 128], BF16)
        eyef_sb = sb("eyef_sb", [128, 128], F32)
        x_t = sb("x_t", [128, 2048], F32)
        stats = sb("stats", [128, 48], F32)
        mv = sb("mv", [128, 2], F32)
        sc1 = sb("sc1", [128, 4], F32)
        xn_bf = sb("xn_bf", [128, 2048], BF16)
        xT_t = sb("xT_t", [128, 32, 128], BF16)
        xts = sb("xts", [128, 2, 32, 128], BF16)
        wsl = sb("wsl", [128, 2, 32, 512], BF16)
        cos_t = sb("cos_t", [128, 2, 256], F32)
        sin_t = sb("sin_t", [128, 2, 256], F32)
        qf = sb("qf", [128, 512], F32)
        qr = sb("qr", [128, 512], F32)
        t1b = sb("t1b", [128, 256], F32)
        t2b = sb("t2b", [128, 256], F32)
        sqb = sb("sqb", [128, 512], F32)
        nrm = sb("nrm", [128, 4], F32)
        knacc = sb("knacc", [128, 8], F32)
        kn1 = sb("kn1", [128, 1], F32)
        out_bf = sb("out_bf", [128, 2, 512], BF16)
        ps_t = psm("ps_t", [128, 512], BF16)
        ps_tf = psm("ps_tf", [128, 512], F32)
        ps_a = psm("ps_a", [128, 512], F32)
        ps_b = psm("ps_b", [128, 512], F32)
        ld_s = sem("ld"); pe_s = sem("pe"); vv_s = sem("vv"); ss_s = sem("ss"); st_s = sem("st")
        wl_s = [sem("wl0"), sem("wl1")]
        xl_s = [sem("xl0"), sem("xl1")]
        cl_s = [sem("cl0"), sem("cl1")]
        so_s = [sem("so0"), sem("so1")]
        qn_s = sem("qn")
        block = _ctx.enter_context(nc.Block())
        ld = Ctr(ld_s); pe = Ctr(pe_s); vv = Ctr(vv_s); ss = Ctr(ss_s); st = Ctr(st_s)
        wl = [Ctr(wl_s[0]), Ctr(wl_s[1])]
        xl = [Ctr(xl_s[0]), Ctr(xl_s[1])]
        cl = [Ctr(cl_s[0]), Ctr(cl_s[1])]
        so = [Ctr(so_s[0]), Ctr(so_s[1])]
        qn = Ctr(qn_s)
        G, Sy, T, V, Sc = block.gpsimd, block.sync, block.tensor, block.vector, block.scalar

        def _(g):
            g.dma_start(out=eye_sb[:], in_=eye).then_inc(ld.sem, 16)
            g.dma_start(out=eyef_sb[:], in_=eyef).then_inc(ld.sem, 16)
        G(_); ld.inc(32)
        def _(v):
            v.memset(knacc[:], 0.0).then_inc(vv.sem, 1)
        V(_); kn_zero = vv.inc()

        for i in range(8):
            prev = vv.n
            def _(g, i=i, prev=prev):
                g.wait_ge(vv.sem, prev)
                g.dma_start(out=x_t[:], in_=x[i * 128:(i + 1) * 128, 0:2048]).then_inc(ld.sem, 16)
            G(_); xl1 = ld.inc(16)
            def _(v, xl1=xl1):
                v.wait_ge(ld.sem, xl1)
                for cch in range(4):
                    op = v.bn_stats(stats[:, cch * 6:(cch + 1) * 6], x_t[:, cch * 512:(cch + 1) * 512])
                op.then_inc(vv.sem, 1)
            V(_); h1 = vv.inc()
            def _(g, i=i, h1=h1):
                g.wait_ge(vv.sem, h1)
                g.dma_start(out=x_t[:], in_=x[i * 128:(i + 1) * 128, 2048:4096]).then_inc(ld.sem, 16)
            G(_); xl2 = ld.inc(16)
            def _(v, xl2=xl2):
                v.wait_ge(ld.sem, xl2)
                for cch in range(4):
                    op = v.bn_stats(stats[:, 24 + cch * 6:24 + (cch + 1) * 6], x_t[:, cch * 512:(cch + 1) * 512])
                op.then_inc(vv.sem, 1)
                v.wait_ge(vv.sem, vv.n + 1)
                v.bn_aggr(mv[:], stats[:]).then_inc(vv.sem, 1)
                v.wait_ge(vv.sem, vv.n + 2)
                v.tensor_scalar(sc1[:, 0:1], mv[:, 1:2], 1.0e-5, None, ALU.add).then_inc(vv.sem, 1)
            V(_); h2 = vv.inc(3)
            def _(s, h2=h2):
                s.wait_ge(vv.sem, h2)
                s.activation(sc1[:, 3:4], sc1[:, 0:1], AF.Ln).then_inc(ss.sem, 1)
                s.wait_ge(ss.sem, ss.n + 1)
                s.activation(sc1[:, 1:2], sc1[:, 3:4], AF.Exp, scale=-0.5).then_inc(ss.sem, 1)
            Sc(_); s2 = ss.inc(2)
            def _(v, s2=s2):
                v.wait_ge(ss.sem, s2)
                v.tensor_scalar(sc1[:, 2:3], mv[:, 0:1], sc1[:, 1:2], -1.0, ALU.mult, ALU.mult).then_inc(vv.sem, 1)
                v.wait_ge(vv.sem, vv.n + 1)
                v.tensor_scalar(xn_bf[:], x_t[:], sc1[:, 1:2], sc1[:, 2:3], ALU.mult, ALU.add).then_inc(vv.sem, 1)
            V(_); xn_rdy = vv.inc(2)
            for half in range(2):
                if half == 1:
                    def _(g, i=i, w=vv.n):
                        g.wait_ge(vv.sem, w)
                        g.dma_start(out=x_t[:], in_=x[i * 128:(i + 1) * 128, 0:2048]).then_inc(ld.sem, 16)
                    G(_); xl3 = ld.inc(16)
                    def _(v, xl3=xl3):
                        v.wait_ge(ld.sem, xl3)
                        v.tensor_scalar(xn_bf[:], x_t[:], sc1[:, 1:2], sc1[:, 2:3], ALU.mult, ALU.add).then_inc(vv.sem, 1)
                    V(_); xn_rdy = vv.inc()
                cbase = 16 if half == 0 else 0
                for grp in range(4):
                    prev_evac = vv.n
                    def _(t, xn_rdy=xn_rdy, grp=grp, prev_evac=prev_evac):
                        t.wait_ge(vv.sem, xn_rdy)
                        t.wait_ge(vv.sem, prev_evac)
                        for j in range(4):
                            cc = grp * 4 + j
                            mm = t.matmul(ps_t[:, j * 128:(j + 1) * 128],
                                          xn_bf[:, cc * 128:(cc + 1) * 128], eye_sb[:],
                                          is_transpose=True, start=(j == 0), stop=True,
                                          skip_group_check=True)
                        mm.then_inc(pe.sem, 1)
                    T(_); tr = pe.inc()
                    def _(v, tr=tr, grp=grp, cbase=cbase):
                        v.wait_ge(pe.sem, tr)
                        v.tensor_copy(xT_t[:, cbase + grp * 4:cbase + grp * 4 + 4, :],
                                      ps_t.ap().rearrange("p (j n) -> p j n", n=128)).then_inc(vv.sem, 1)
                    V(_); vv.inc()
            def _(s, i=i, w=vv.n):
                s.wait_ge(vv.sem, w)
                s.dma_start(out=xt_scr[i], in_=xT_t.ap().rearrange("p a b -> p (a b)")).then_inc(st.sem, 16)
            Sy(_); st.inc(16)
            def _(v, w=st.n):
                v.wait_ge(st.sem, w)
            V(_)
        ph1_st = st.n

        slabs = [("q", wq, s) for s in range(8)] + [("k", wk, s) for s in range(2)] + [("v", wv, s) for s in range(2)]
        psq = [ps_a, ps_b]
        psq_free = [0, 0]
        xts_used = [pe.n, pe.n]
        first_iter = True
        for kind, wt, sfb in slabs:
            sp = (slabs.index((kind, wt, sfb))) % 2
            wfree = pe.n
            def _(g, wt=wt, sfb=sfb, sp=sp, wfree=wfree, first=first_iter):
                if not first:
                    g.wait_ge(pe.sem, wfree)
                for cc in range(32):
                    g.dma_start(out=wsl[:, sp, cc],
                                in_=wt[cc * 128:(cc + 1) * 128, sfb * 512:(sfb + 1) * 512]).then_inc(wl_s[sp], 16)
            G(_); wv_t = wl[sp].inc(16 * 32)
            for i in range(8):
                slot = i % 2
                def _(g, i=i, slot=slot, used=xts_used[slot], first=first_iter, ph1_st=ph1_st):
                    if first and i < 2:
                        g.wait_ge(st.sem, ph1_st)
                    g.wait_ge(pe.sem, used)
                    g.dma_start(out=xts[:, slot].rearrange("p a b -> p (a b)"), in_=xt_scr[i]).then_inc(xl_s[slot], 16)
                G(_); xtv = xl[slot].inc(16)
                if kind in ("q", "k"):
                    cfree = vv.n
                    def _(g, i=i, slot=slot, cfree=cfree):
                        g.wait_ge(vv.sem, cfree)
                        g.dma_start(out=cos_t[:, slot], in_=cosr[i * 128:(i + 1) * 128, :]).then_inc(cl_s[slot], 16)
                        g.dma_start(out=sin_t[:, slot], in_=sinr[i * 128:(i + 1) * 128, :]).then_inc(cl_s[slot], 16)
                    G(_); clv = cl[slot].inc(32)
                else:
                    clv = cl[slot].n
                def _(t, sp=sp, wv_t=wv_t, slot=slot, xtv=xtv, free=psq_free[slot]):
                    t.wait_ge(wl_s[sp], wv_t)
                    t.wait_ge(xl_s[slot], xtv)
                    if free:
                        t.wait_ge(vv.sem, free)
                    for cc in range(32):
                        mm = t.matmul(psq[slot][:], xts[:, slot, cc], wsl[:, sp, cc],
                                      start=(cc == 0), stop=(cc == 31))
                    mm.then_inc(pe.sem, 1)
                T(_); pd = pe.inc()
                xts_used[slot] = pd
                if kind == "v":
                    def _(v, pd=pd, slot=slot, sofree=so[slot].n):
                        v.wait_ge(pe.sem, pd)
                        v.wait_ge(so_s[slot], sofree)
                        v.tensor_copy(out_bf[:, slot], psq[slot][:]).then_inc(vv.sem, 1)
                    V(_); psq_free[slot] = vv.inc()
                    ev = psq_free[slot]
                    def _(s, ev=ev, i=i, sfb=sfb, slot=slot):
                        s.wait_ge(vv.sem, ev)
                        s.dma_start(out=v_o[i * 128:(i + 1) * 128, sfb * 512:(sfb + 1) * 512],
                                    in_=out_bf[:, slot]).then_inc(so_s[slot], 16)
                    Sy(_); so[slot].inc(16)
                else:
                    def _(v, pd=pd, slot=slot, clv=clv, kind=kind):
                        v.wait_ge(pe.sem, pd)
                        v.wait_ge(cl_s[slot], clv)
                        v.tensor_copy(qf[:], psq[slot][:]).then_inc(vv.sem, 1)
                    V(_); e0 = vv.inc()
                    psq_free[slot] = e0
                    ncos = 256 if kind == "q" else 256
                    def _(v, e0=e0, slot=slot):
                        v.wait_ge(vv.sem, e0)
                        qe = qf[:, 0::2]
                        qo = qf[:, 1::2]
                        v.tensor_tensor(t1b[:], qe, cos_t[:, slot], ALU.mult)
                        v.tensor_tensor(t2b[:], qo, sin_t[:, slot], ALU.mult).then_inc(vv.sem, 1)
                        v.wait_ge(vv.sem, vv.n + 1)
                        v.tensor_tensor(qr[:, 0::2], t1b[:], t2b[:], ALU.subtract).then_inc(vv.sem, 1)
                        v.wait_ge(vv.sem, vv.n + 2)
                        v.tensor_tensor(t1b[:], qe, sin_t[:, slot], ALU.mult)
                        v.tensor_tensor(t2b[:], qo, cos_t[:, slot], ALU.mult).then_inc(vv.sem, 1)
                        v.wait_ge(vv.sem, vv.n + 3)
                        v.tensor_tensor(qr[:, 1::2], t1b[:], t2b[:], ALU.add).then_inc(vv.sem, 1)
                    V(_); rope_done = vv.inc(4)
                    def _(v, rope_done=rope_done, slot=slot, sofree=so[slot].n, qnfree=qn.n):
                        v.wait_ge(vv.sem, rope_done)
                        v.wait_ge(so_s[slot], sofree)
                        v.tensor_copy(out_bf[:, slot], qr[:]).then_inc(vv.sem, 1)
                        v.tensor_tensor(sqb[:], qr[:], qr[:], ALU.mult).then_inc(vv.sem, 1)
                        v.wait_ge(vv.sem, vv.n + 2)
                        v.wait_ge(qn_s, qnfree)
                        v.tensor_reduce(nrm[:], sqb.ap().rearrange("p (h d) -> p h d", d=128),
                                        mybir.AxisListType.X, ALU.add).then_inc(vv.sem, 1)
                    V(_); nrm_done = vv.inc(3)
                    if kind == "q":
                        def _(s2_, nrm_done=nrm_done, i=i, sfb=sfb):
                            s2_.wait_ge(vv.sem, nrm_done)
                            s2_.dma_start(out=qn2_o[i * 128:(i + 1) * 128, sfb * 4:(sfb + 1) * 4],
                                          in_=nrm[:]).then_inc(qn_s, 16)
                        Sy(_); qn.inc(16)
                    else:
                        def _(v, nrm_done=nrm_done, sfb=sfb):
                            v.wait_ge(vv.sem, nrm_done)
                            v.tensor_tensor(knacc[:, sfb * 4:(sfb + 1) * 4],
                                            knacc[:, sfb * 4:(sfb + 1) * 4], nrm[:], ALU.max).then_inc(vv.sem, 1)
                        V(_); vv.inc()
                    out_dst = q_o if kind == "q" else k_o
                    ob_done = vv.n
                    def _(s, ob_done=ob_done, i=i, sfb=sfb, slot=slot, out_dst=out_dst):
                        s.wait_ge(vv.sem, ob_done)
                        s.dma_start(out=out_dst[i * 128:(i + 1) * 128, sfb * 512:(sfb + 1) * 512],
                                    in_=out_bf[:, slot]).then_inc(so_s[slot], 16)
                    Sy(_); so[slot].inc(16)
            first_iter = False
        def _(v, w=vv.n):
            v.wait_ge(vv.sem, w)
            v.reduce_max(kn1[:], knacc[:], mybir.AxisListType.X).then_inc(vv.sem, 1)
        V(_); r1 = vv.inc()
        def _(t, r1=r1):
            t.wait_ge(vv.sem, r1)
            t.drain()
            t.matmul(ps_tf[0:1, 0:128], kn1[:], eyef_sb[:], is_transpose=True,
                     start=True, stop=True, skip_group_check=True).then_inc(pe.sem, 1)
            t.drain()
        T(_); r2 = pe.inc()
        def _(v, r2=r2):
            v.wait_ge(pe.sem, r2)
            v.reduce_max(kn1[0:1, 0:1], ps_tf[0:1, 0:128], mybir.AxisListType.X).then_inc(vv.sem, 1)
        V(_); r3 = vv.inc()
        def _(s, r3=r3):
            s.wait_ge(vv.sem, r3)
            s.dma_start(out=knm_o, in_=kn1[0:1, 0:1]).then_inc(st.sem, 16)
            s.wait_ge(st.sem, st.n + 16)
        Sy(_); st.inc(16)
        def _(s):
            s.wait_ge(st.sem, st.n)
            s.wait_ge(so_s[0], so[0].n)
            s.wait_ge(so_s[1], so[1].n)
            s.wait_ge(qn_s, qn.n)
        Sy(_)
    return nc


def rope_tables(l0):
    half = 64
    inv = 1.0 / (10000.0 ** (np.arange(half, dtype=np.float64) / half))
    ang = (np.arange(l0, l0 + TPC, dtype=np.float64)[:, None] * inv[None, :])
    cos = np.cos(ang).astype(np.float32)
    sin = np.sin(ang).astype(np.float32)
    cosr = np.tile(cos, (1, 4))
    sinr = np.tile(sin, (1, 4))
    return np.ascontiguousarray(cosr), np.ascontiguousarray(sinr)


def run_l1_qkv(inp, x1):
    nc = build_l1_qkv()
    eye = np.eye(128, dtype=BF16NP)
    eyef = np.eye(128, dtype=np.float32)
    maps = []
    for c in range(NCORE):
        cosr, sinr = rope_tables((c % 4) * TPC)
        maps.append(dict(
            x=np.ascontiguousarray(x1[c]),
            wq=inp["L1_attn_wq"], wk=inp["L1_attn_wk"], wv=inp["L1_attn_wv"],
            cosr=cosr, sinr=sinr, eye=eye, eyef=eyef))
    res = run_bass_kernel_spmd(nc, maps, core_ids=list(range(NCORE)))
    out = {k: np.stack([res.results[c][k] for c in range(NCORE)]) for k in
           ["q_o", "k_o", "v_o", "qn2_o", "knm_o"]}
    return out



def build_l1_attn(core_quarter):
    nc = bass.Bass("TRN2", target_bir_lowering=False)
    q_in = nc.dram_tensor("q_in", [TPC, D], BF16, kind="ExternalInput").ap()
    k_all = nc.dram_tensor("k_all", [4096, KVD], BF16, kind="ExternalInput").ap()
    v_all = nc.dram_tensor("v_all", [4096, KVD], BF16, kind="ExternalInput").ap()
    qn2 = nc.dram_tensor("qn2", [TPC, 32], F32, kind="ExternalInput").ap()
    knm4 = nc.dram_tensor("knm4", [1, 4], F32, kind="ExternalInput").ap()
    wo = nc.dram_tensor("wo", [D, D], F32, kind="ExternalInput").ap()
    eye = nc.dram_tensor("eye", [128, 128], BF16, kind="ExternalInput").ap()
    mask = nc.dram_tensor("mask", [128, 128], F32, kind="ExternalInput").ap()
    onesf_in = nc.dram_tensor("onesf_in", [1, 128], F32, kind="ExternalInput").ap()
    s1T_o = nc.dram_tensor("s1T_o", [D, TPC], F32, kind="ExternalOutput").ap()

    from contextlib import ExitStack
    with ExitStack() as _ctx:
        def sb(*a): return _ctx.enter_context(nc.sbuf_tensor(*a))
        def psm(*a): return _ctx.enter_context(nc.psum_tensor(*a))
        def sem(n): return _ctx.enter_context(nc.semaphore(n))
        eye_sb = sb("eye_sb", [128, 128], BF16)
        mask_sb = sb("mask_sb", [128, 128], F32)
        onesf = sb("onesf", [1, 128], F32)
        kn4 = sb("kn4", [1, 4], F32)
        cbias = sb("cbias", [128, 1], F32)
        knm2c = sb("knm2c", [128, 1], F32)
        qn2_t = sb("qn2_t", [128, 32], F32)
        t1 = sb("t1", [128, 32], F32)
        nb_all = sb("nb_all", [128, 8, 32], F32)
        kT_h = sb("kT_h", [128, 4096], BF16)
        kstg = sb("kstg", [128, 128], BF16)
        v_aug = sb("v_aug", [128, 32, 132], BF16)
        probs = sb("probs", [128, 2, 512], BF16)
        probsT = sb("probsT", [128, 2, 512], BF16)
        rl = sb("rl", [128, 1], F32)
        ctx_bf = sb("ctx_bf", [128, 128], BF16)
        ctxT_all = sb("ctxT_all", [128, 32, 1024], BF16)
        ps_t = psm("ps_t", [128, 512], BF16)
        ps_tf = psm("ps_tf", [128, 128], F32)
        ps_sc = [psm("ps_sc0", [128, 512], F32), psm("ps_sc1", [128, 512], F32)]
        ps_pt = psm("ps_pt", [128, 512], BF16)
        ps_ctx = psm("ps_ctx", [128, 132], F32)
        ps_o = [psm("ps_o0", [128, 512], F32), psm("ps_o1", [128, 512], F32)]
        ld_s = sem("ld"); pe_s = sem("pe"); vv_s = sem("vv"); ss_s = sem("ss"); st_s = sem("st")
        ql_s = sem("ql"); qtl_s = sem("qtl"); kl_s = sem("kl"); wl_s = [sem("wl0"), sem("wl1")]
        so_s = [sem("so0"), sem("so1")]
        block = _ctx.enter_context(nc.Block())
        ld = Ctr(ld_s); pe = Ctr(pe_s); vv = Ctr(vv_s); ss = Ctr(ss_s); st = Ctr(st_s)
        ql = Ctr(ql_s); qtl = Ctr(qtl_s); kl = Ctr(kl_s)
        wl = [Ctr(wl_s[0]), Ctr(wl_s[1])]
        so = [Ctr(so_s[0]), Ctr(so_s[1])]
        G, Sy, T, V, Sc = block.gpsimd, block.sync, block.tensor, block.vector, block.scalar

        qt_cm = nc.sbuf_tensor("q_t", [128, D], BF16)
        q_t = qt_cm.__enter__()
        qta_cm = nc.sbuf_tensor("qT_all", [128, 32, 8, 128], BF16)
        qT_all = qta_cm.__enter__()
        def _(g):
            g.dma_start(out=eye_sb[:], in_=eye).then_inc(ld.sem, 16)
            g.dma_start(out=mask_sb[:], in_=mask).then_inc(ld.sem, 16)
            g.dma_start(out=onesf[:], in_=onesf_in).then_inc(ld.sem, 16)
            g.dma_start(out=kn4[:], in_=knm4).then_inc(ld.sem, 16)
        G(_); ld.inc(64)
        def _(v):
            v.wait_ge(ld.sem, ld.n)
            v.memset(cbias[:], -2.4260151319598084)
            v.reduce_max(kn4[0:1, 0:1], kn4[:], mybir.AxisListType.X).then_inc(vv.sem, 1)
        V(_); km = vv.inc()
        def _(t, km=km):
            t.wait_ge(vv.sem, km)
            t.matmul(ps_tf[:, 0:1], onesf[:], kn4[0:1, 0:1], start=True, stop=True,
                     skip_group_check=True).then_inc(pe.sem, 1)
            t.drain()
        T(_); kb = pe.inc()
        def _(v, kb=kb):
            v.wait_ge(pe.sem, kb)
            v.tensor_copy(knm2c[:], ps_tf[:, 0:1]).then_inc(vv.sem, 1)
        V(_); kc_rdy = vv.inc()
        for qt in range(8):
            prev = vv.n
            def _(g, qt=qt, prev=prev):
                g.wait_ge(vv.sem, prev)
                g.dma_start(out=qn2_t[:], in_=qn2[qt * 128:(qt + 1) * 128, :]).then_inc(ql_s, 16)
            G(_); qv = ql.inc(16)
            def _(v, qv=qv, kc_rdy=kc_rdy):
                v.wait_ge(ql_s, qv)
                v.wait_ge(vv.sem, kc_rdy)
                v.tensor_scalar(t1[:], qn2_t[:], knm2c[:, 0:1], None, ALU.mult).then_inc(vv.sem, 1)
            V(_); b1 = vv.inc()
            def _(s, b1=b1):
                s.wait_ge(vv.sem, b1)
                s.activation(t1[:], t1[:], AF.Ln).then_inc(ss.sem, 1)
                s.wait_ge(ss.sem, ss.n + 1)
                s.activation(t1[:], t1[:], AF.Exp, scale=0.5, bias=cbias[:, 0:1]).then_inc(ss.sem, 1)
            Sc(_); b2 = ss.inc(2)
            def _(v, b2=b2, qt=qt):
                v.wait_ge(ss.sem, b2)
                v.tensor_scalar(nb_all[:, qt, :], t1[:], -1.0, None, ALU.mult).then_inc(vv.sem, 1)
            V(_); vv.inc()
        for i in range(8):
            qfree = pe.n
            def _(g, i=i, qfree=qfree):
                g.wait_ge(pe.sem, qfree)
                g.dma_start(out=q_t[:], in_=q_in[i * 128:(i + 1) * 128, :]).then_inc(qtl_s, 16)
            G(_); qv = qtl.inc(16)
            for grp in range(8):
                prev_evac = vv.n
                def _(t, qv=qv, grp=grp, prev_evac=prev_evac):
                    t.wait_ge(qtl_s, qv)
                    t.wait_ge(vv.sem, prev_evac)
                    for j in range(4):
                        h = grp * 4 + j
                        mm = t.matmul(ps_t[:, j * 128:(j + 1) * 128],
                                      q_t[:, h * 128:(h + 1) * 128], eye_sb[:],
                                      is_transpose=True, start=(j == 0), stop=True,
                                      skip_group_check=True)
                    mm.then_inc(pe.sem, 1)
                T(_); tr = pe.inc()
                def _(v, tr=tr, grp=grp, i=i):
                    v.wait_ge(pe.sem, tr)
                    v.tensor_copy(qT_all[:, grp * 4:grp * 4 + 4, i, :],
                                  ps_t.ap().rearrange("p (j n) -> p j n", n=128)).then_inc(vv.sem, 1)
                V(_); vv.inc()
        prologue_done = vv.n

        ones_set = False
        for h in range(8):
            kstg_free = pe.n
            vload_free = pe.n
            def _(g, h=h, vload_free=vload_free, ones_set=ones_set):
                g.wait_ge(pe.sem, vload_free)
                for kt in range(32):
                    g.dma_start(out=v_aug[:, kt, 0:128],
                                in_=v_all[kt * 128:(kt + 1) * 128, h * 128:(h + 1) * 128]).then_inc(kl_s, 16)
            G(_); vl = kl.inc(16 * 32)
            if not ones_set:
                def _(v, w=vl):
                    v.memset(v_aug.ap().rearrange("p a b -> p (a b)")[:, :].tensor[...] if False else v_aug[:, 0:32, 128:129], 1.0).then_inc(vv.sem, 1)
                V(_); ones_rdy = vv.inc()
                ones_set = True
            for kt in range(32):
                ks_free = pe.n
                def _(g, h=h, kt=kt, ks_free=ks_free):
                    g.wait_ge(pe.sem, ks_free)
                    g.dma_start(out=kstg[:], in_=k_all[kt * 128:(kt + 1) * 128, h * 128:(h + 1) * 128]).then_inc(kl_s, 16)
                G(_); klv = kl.inc(16)
                prev_evac = vv.n
                def _(t, klv=klv, kt=kt, prev_evac=prev_evac):
                    t.wait_ge(kl_s, klv)
                    if kt % 4 == 0:
                        t.wait_ge(vv.sem, prev_evac)
                    mm = t.matmul(ps_t[:, (kt % 4) * 128:(kt % 4 + 1) * 128], kstg[:], eye_sb[:],
                                  is_transpose=True, start=(kt % 4 == 0), stop=True,
                                  skip_group_check=True)
                    mm.then_inc(pe.sem, 1)
                T(_); tr = pe.inc()
                if kt % 4 == 3:
                    def _(v, tr=tr, kt=kt):
                        v.wait_ge(pe.sem, tr)
                        v.tensor_copy(kT_h[:, (kt - 3) * 128:(kt + 1) * 128], ps_t.ap()).then_inc(vv.sem, 1)
                    V(_); vv.inc()
            kvh_rdy = vv.n

            if h == 0:
                last_exp = {0: 0, 1: 0}
                last_tr = {0: 0, 1: 0}
            for qh in range(4):
                h_q = h * 4 + qh
                for qt in range(8):
                    gq = core_quarter * 8 + qt
                    P = (gq + 1) * 128
                    nkc = (P + 511) // 512
                    for kc in range(nkc):
                        pp = kc % 2
                        W = min(512, P - kc * 512)
                        is_last = (kc == nkc - 1)
                        sc_free = last_exp[pp]
                        def _(t, pp=pp, W=W, kc=kc, h_q=h_q, qt=qt, sc_free=sc_free, kvh=kvh_rdy, pro=prologue_done):
                            t.wait_ge(vv.sem, max(kvh, pro))
                            if sc_free:
                                t.wait_ge(ss.sem, sc_free)
                            t.matmul(ps_sc[pp][:, 0:W], qT_all[:, h_q, qt, :],
                                     kT_h[:, kc * 512:kc * 512 + W],
                                     start=True, stop=True).then_inc(pe.sem, 1)
                        T(_); sc_done = pe.inc()
                        mask_done = 0
                        if is_last:
                            def _(v, pp=pp, W=W, sc_done=sc_done):
                                v.wait_ge(pe.sem, sc_done)
                                v.tensor_tensor(ps_sc[pp][:, W - 128:W], ps_sc[pp][:, W - 128:W],
                                                mask_sb[:], ALU.add).then_inc(vv.sem, 1)
                            V(_); mask_done = vv.inc()
                        pfree = last_tr[pp]
                        def _(s, pp=pp, W=W, qt=qt, h_q=h_q, sc_done=sc_done, mask_done=mask_done, pfree=pfree):
                            s.wait_ge(pe.sem, sc_done)
                            if mask_done:
                                s.wait_ge(vv.sem, mask_done)
                            if pfree:
                                s.wait_ge(pe.sem, pfree)
                            s.activation(probs[:, pp, 0:W], ps_sc[pp][:, 0:W], AF.Exp,
                                         scale=0.08838834764831845,
                                         bias=nb_all[:, qt, h_q:h_q + 1]).then_inc(ss.sem, 1)
                        Sc(_); exp_done = ss.inc()
                        last_exp[pp] = exp_done
                        ptfree = vv.n
                        def _(t, pp=pp, W=W, exp_done=exp_done, ptfree=ptfree):
                            t.wait_ge(ss.sem, exp_done)
                            t.wait_ge(vv.sem, ptfree)
                            for j in range(W // 128):
                                mm = t.matmul(ps_pt[:, j * 128:(j + 1) * 128],
                                              probs[:, pp, j * 128:(j + 1) * 128], eye_sb[:],
                                              is_transpose=True, start=(j == 0), stop=True,
                                              skip_group_check=True)
                            mm.then_inc(pe.sem, 1)
                        T(_); tr_done = pe.inc()
                        last_tr[pp] = tr_done
                        def _(v, pp=pp, W=W, tr_done=tr_done):
                            v.wait_ge(pe.sem, tr_done)
                            v.tensor_copy(probsT[:, pp, 0:W], ps_pt[:, 0:W]).then_inc(vv.sem, 1)
                        V(_); ptv = vv.inc()
                        def _(t, pp=pp, W=W, kc=kc, ptv=ptv, first=(kc == 0)):
                            t.wait_ge(vv.sem, ptv)
                            for j in range(W // 128):
                                mm = t.matmul(ps_ctx[:, 0:129], probsT[:, pp, j * 128:(j + 1) * 128],
                                              v_aug[:, kc * 4 + j, 0:129],
                                              start=(first and j == 0), stop=False,
                                              skip_group_check=True)
                            mm.then_inc(pe.sem, 1)
                        T(_); ctx_done = pe.inc()
                    def _(v, ctx_done=ctx_done):
                        v.wait_ge(pe.sem, ctx_done)
                        v.reciprocal(rl[:], ps_ctx[:, 128:129]).then_inc(vv.sem, 1)
                    V(_); rld = vv.inc()
                    def _(s, rld=rld):
                        s.wait_ge(vv.sem, rld)
                        s.activation(ctx_bf[:], ps_ctx[:, 0:128], AF.Copy, scale=rl[:, 0:1]).then_inc(ss.sem, 1)
                    Sc(_); cev = ss.inc()
                    ctfree = vv.n
                    def _(t, cev=cev, ctfree=ctfree):
                        t.wait_ge(ss.sem, cev)
                        t.wait_ge(vv.sem, ctfree)
                        t.matmul(ps_pt[:, 0:128], ctx_bf[:], eye_sb[:], is_transpose=True,
                                 start=True, stop=True, skip_group_check=True).then_inc(pe.sem, 1)
                    T(_); ctr = pe.inc()
                    def _(v, ctr=ctr, h_q=h_q, qt=qt):
                        v.wait_ge(pe.sem, ctr)
                        v.tensor_copy(ctxT_all[:, h_q, qt * 128:(qt + 1) * 128], ps_pt[:, 0:128]).then_inc(vv.sem, 1)
                    V(_); vv.inc()
            vload_free = pe.n
        attn_done = vv.n
        qta_cm.__exit__(None, None, None)
        qt_cm.__exit__(None, None, None)
        wsl = sb("wsl", [128, 2, 32, 512], BF16)
        s_st = sb("s_st", [128, 2, 512], F32)

        for slab in range(8):
            sp = slab % 2
            wfree = pe.n
            def _(g, slab=slab, sp=sp, wfree=wfree, first=(slab < 2)):
                if not first:
                    g.wait_ge(pe.sem, wfree)
                for cc in range(32):
                    g.dma_start(out=wsl[:, sp, cc],
                                in_=wo[cc * 128:(cc + 1) * 128, slab * 512:(slab + 1) * 512]).then_inc(wl_s[sp], 16)
            G(_); wv_t = wl[sp].inc(16 * 32)
            for sub in range(4):
                o_chunk = slab * 4 + sub
                for tq in range(2):
                    k3i = slab * 8 + sub * 2 + tq
                    slot = k3i % 2
                    ofree = vv.n if k3i >= 2 else 0
                    def _(t, sp=sp, wv_t=wv_t, sub=sub, tq=tq, slot=slot, ofree=ofree, attn_done=attn_done):
                        t.wait_ge(wl_s[sp], wv_t)
                        t.wait_ge(vv.sem, max(attn_done, ofree))
                        for cc in range(32):
                            mm = t.matmul(ps_o[slot][:], wsl[:, sp, cc, sub * 128:(sub + 1) * 128],
                                          ctxT_all[:, cc, tq * 512:(tq + 1) * 512],
                                          start=(cc == 0), stop=(cc == 31))
                        mm.then_inc(pe.sem, 1)
                    T(_); sd = pe.inc()
                    s_free = so[k3i % 2].n
                    def _(v, sd=sd, slot=slot, k3i=k3i, s_free=s_free):
                        v.wait_ge(pe.sem, sd)
                        if k3i >= 2:
                            v.wait_ge(so_s[k3i % 2], s_free)
                        v.tensor_copy(s_st[:, k3i % 2], ps_o[slot][:]).then_inc(vv.sem, 1)
                    V(_); ev = vv.inc()
                    def _(s, ev=ev, o_chunk=o_chunk, tq=tq, k3i=k3i):
                        s.wait_ge(vv.sem, ev)
                        s.dma_start(out=s1T_o[o_chunk * 128:(o_chunk + 1) * 128, tq * 512:(tq + 1) * 512],
                                    in_=s_st[:, k3i % 2]).then_inc(so_s[k3i % 2], 16)
                    Sy(_); so[k3i % 2].inc(16)
        def _(s):
            s.wait_ge(so_s[0], so[0].n)
            s.wait_ge(so_s[1], so[1].n)
        Sy(_)
    return nc


def run_l1_attn(inp, qkv):
    eye = np.eye(128, dtype=BF16NP)
    onesf = np.ones((1, 128), dtype=np.float32)
    mask = np.triu(np.full((128, 128), -1.0e30, dtype=np.float32), k=1)
    outs = []
    for quarter in range(4):
        nc = build_l1_attn(quarter)
        maps = []
        for b in range(2):
            c = b * 4 + quarter
            k_all = np.concatenate([qkv["k_o"][b * 4 + j] for j in range(4)], axis=0)
            v_all = np.concatenate([qkv["v_o"][b * 4 + j] for j in range(4)], axis=0)
            knm4 = np.concatenate([qkv["knm_o"][b * 4 + j] for j in range(4)], axis=1)
            maps.append(dict(q_in=qkv["q_o"][c], k_all=np.ascontiguousarray(k_all),
                             v_all=np.ascontiguousarray(v_all), qn2=qkv["qn2_o"][c],
                             knm4=np.ascontiguousarray(knm4),
                             wo=inp["L1_attn_wo"], eye=eye, mask=mask, onesf_in=onesf))
        res = run_bass_kernel_spmd(nc, maps, core_ids=[quarter, 4 + quarter])
        outs.append((res.results[0]["s1T_o"], res.results[1]["s1T_o"]))
    s1T = np.zeros((8, D, TPC), np.float32)
    for quarter in range(4):
        s1T[quarter] = outs[quarter][0]
        s1T[4 + quarter] = outs[quarter][1]
    return s1T



def build_l1_ffn():
    nc = bass.Bass("TRN2", target_bir_lowering=False)
    x = nc.dram_tensor("x", [TPC, D], F32, kind="ExternalInput").ap()
    s1T = nc.dram_tensor("s1T", [D, TPC], F32, kind="ExternalInput").ap()
    wres = nc.dram_tensor("wres", [16, D], F32, kind="ExternalInput").ap()
    wpre = nc.dram_tensor("wpre", [4, D], F32, kind="ExternalInput").ap()
    wpost = nc.dram_tensor("wpost", [4, D], F32, kind="ExternalInput").ap()
    bases = nc.dram_tensor("bases", [1, 24], F32, kind="ExternalInput").ap()
    gate = nc.dram_tensor("gate", [1, 1], F32, kind="ExternalInput").ap()
    wb = nc.dram_tensor("wb", [1024, 1024], F32, kind="ExternalInput").ap()
    eye4 = nc.dram_tensor("eye4", [1, 16], F32, kind="ExternalInput").ap()
    eye = nc.dram_tensor("eye", [128, 128], BF16, kind="ExternalInput").ap()
    eyef = nc.dram_tensor("eyef", [128, 128], F32, kind="ExternalInput").ap()
    onesf_in = nc.dram_tensor("onesf_in", [1, 128], F32, kind="ExternalInput").ap()
    out_o = nc.dram_tensor("out_o", [TPC, D], F32, kind="ExternalOutput").ap()
    xn_scr = nc.dram_tensor("xn_scr", [8, 128, D], F32).ap()

    from contextlib import ExitStack
    with ExitStack() as _ctx:
        def sb(*a): return _ctx.enter_context(nc.sbuf_tensor(*a))
        def psm(*a): return _ctx.enter_context(nc.psum_tensor(*a))
        def sem(n): return _ctx.enter_context(nc.semaphore(n))
        eye_sb = sb("eye_sb", [128, 128], BF16)
        eyef_sb = sb("eyef_sb", [128, 128], F32)
        onesf = sb("onesf", [1, 128], F32)
        wrow = sb("wrow", [16, D], F32)
        wallT = sb("wallT", [128, 32, 128], BF16)
        wallf = sb("wallf", [128, 24], F32)
        wb_bf = sb("wb_bf", [128, 8, 1024], BF16)
        small = sb("small", [1, 32], F32)
        g_bc = sb("g_bc", [128, 2], F32)
        base_bc = sb("base_bc", [128, 24], F32)
        base16r = sb("base16r", [128, 128], F32)
        base4r = sb("base4r", [128, 2, 32], F32)
        ig_rep = sb("ig_rep", [128, 128], F32)
        x_t = sb("x_t", [128, 2048], F32)
        stats = sb("stats", [128, 48], F32)
        mv = sb("mv", [128, 2], F32)
        sc1 = sb("sc1", [128, 6], F32)
        xn_f = sb("xn_f", [128, 2048], F32)
        xr_bf = sb("xr_bf", [128, 2048], BF16)
        xTr = sb("xTr", [128, 32, 128], BF16)
        a_sb = sb("a_sb", [32, 128], F32)
        z_all = sb("z_all", [128, 8, 24], F32)
        M = sb("M", [128, 128], F32)
        E = sb("E", [128, 128], F32)
        S = sb("S", [128, 32], F32)
        L = sb("L", [128, 32], F32)
        Pl = sb("Pl", [128, 2, 32], F32)
        Hr = sb("Hr", [128, 128], F32)
        Hpre = sb("Hpre", [128, 2, 32], F32)
        xn_t = sb("xn_t", [128, D], F32)
        xp = sb("xp", [128, 2, 1024], F32)
        xpT = sb("xpT", [128, 8, 128], BF16)
        accm = sb("accm", [128, 2, 1024], F32)
        g1f = sb("g1f", [128, D], F32)
        s_ld = sb("s_ld", [128, 16, 128], F32)
        s_t = sb("s_t", [128, 2048], F32)
        gt = sb("gt", [128, 2048], F32)
        tmp = sb("tmp", [128, 2048], F32)
        ps_t = psm("ps_t", [128, 512], BF16)
        ps_tf = psm("ps_tf", [128, 512], F32)
        ps_a = psm("ps_a", [128, 128], F32)
        ps_y = [psm("ps_y0", [128, 512], F32), psm("ps_y1", [128, 512], F32)]
        ld_s = sem("ld"); pe_s = sem("pe"); vv_s = sem("vv"); ss_s = sem("ss"); st_s = sem("st")
        sl_s = sem("sl"); xl_s = sem("xl"); wll_s = sem("wll"); wbl_s = sem("wbl")
        block = _ctx.enter_context(nc.Block())
        ld = Ctr(ld_s); pe = Ctr(pe_s); vv = Ctr(vv_s); ss = Ctr(ss_s); st = Ctr(st_s)
        sl = Ctr(sl_s); xlc = Ctr(xl_s); wll = Ctr(wll_s); wbl = Ctr(wbl_s)
        G, Sy, T, V, Sc = block.gpsimd, block.sync, block.tensor, block.vector, block.scalar

        def _(g):
            g.dma_start(out=eye_sb[:], in_=eye).then_inc(ld.sem, 16)
            g.dma_start(out=eyef_sb[:], in_=eyef).then_inc(ld.sem, 16)
            g.dma_start(out=onesf[:], in_=onesf_in).then_inc(ld.sem, 16)
            g.dma_start(out=small[0:1, 0:24], in_=bases).then_inc(ld.sem, 16)
            g.dma_start(out=small[0:1, 24:25], in_=gate).then_inc(ld.sem, 16)
            g.dma_start(out=E[0:1, 0:16], in_=eye4).then_inc(ld.sem, 16)
        G(_); ld.inc(96)
        def _(s):
            s.wait_ge(ld.sem, ld.n)
            s.activation(small[0:1, 30:31], small[0:1, 24:25], AF.Exp, scale=-1.0).then_inc(ss.sem, 1)
        Sc(_); e1 = ss.inc()
        def _(v, e1=e1):
            v.wait_ge(ss.sem, e1)
            v.tensor_scalar(small[0:1, 30:31], small[0:1, 30:31], 1.0, None, ALU.add).then_inc(vv.sem, 1)
            v.wait_ge(vv.sem, vv.n + 1)
            v.reciprocal(small[0:1, 31:32], small[0:1, 30:31]).then_inc(vv.sem, 1)
        V(_); gs_rdy = vv.inc(2)
        def _(t, gs_rdy=gs_rdy, w=ld.n):
            t.wait_ge(vv.sem, gs_rdy)
            t.wait_ge(ld.sem, w)
            t.matmul(ps_tf[:, 0:1], onesf[:], small[0:1, 31:32], start=True, stop=True, skip_group_check=True)
            t.matmul(ps_tf[:, 8:32], onesf[:], small[0:1, 0:24], start=False, stop=True, skip_group_check=True)
            t.matmul(ps_tf[:, 32:48], onesf[:], E[0:1, 0:16], start=False, stop=True, skip_group_check=True).then_inc(pe.sem, 1)
            t.drain()
        T(_); bc1 = pe.inc()
        def _(v, bc1=bc1):
            v.wait_ge(pe.sem, bc1)
            v.tensor_copy(g_bc[:, 0:1], ps_tf[:, 0:1])
            v.tensor_copy(base_bc[:], ps_tf[:, 8:32]).then_inc(vv.sem, 1)
            v.wait_ge(vv.sem, vv.n + 1)
            v.tensor_scalar(g_bc[:, 1:2], g_bc[:, 0:1], -1.0, 1.0, ALU.mult, ALU.add).then_inc(vv.sem, 1)
            v.wait_ge(vv.sem, vv.n + 2)
            for r in range(8):
                op = v.tensor_scalar(ig_rep[:, r * 16:(r + 1) * 16], ps_tf[:, 32:48], g_bc[:, 1:2], None, ALU.mult)
            op.then_inc(vv.sem, 1)
            for r in range(8):
                op = v.tensor_copy(base16r[:, r * 16:(r + 1) * 16], base_bc[:, 0:16])
            for r in range(8):
                op2 = v.tensor_copy(base4r[:, 0, r * 4:(r + 1) * 4], base_bc[:, 16:20])
                op2 = v.tensor_copy(base4r[:, 1, r * 4:(r + 1) * 4], base_bc[:, 20:24])
            op2.then_inc(vv.sem, 1)
        V(_); bc_rdy = vv.inc(4)
        def _(v):
            v.memset(wallT.ap().rearrange("p a b -> p (a b)"), 0.0).then_inc(vv.sem, 1)
        V(_); wz = vv.inc()
        srcs = [(wres, 16, 0), (wpre, 4, 16), (wpost, 4, 20)]
        for wsrc, nr, coff in srcs:
            wfree = pe.n
            def _(g, wsrc=wsrc, nr=nr, wfree=wfree):
                if wfree:
                    g.wait_ge(pe.sem, wfree)
                g.dma_start(out=wrow[0:nr, :], in_=wsrc).then_inc(wll_s, 16)
            G(_); wld = wll.inc(16)
            for cc in range(32):
                prev = vv.n
                def _(t, cc=cc, nr=nr, wld=wld, prev=prev, wz=wz):
                    t.wait_ge(wll_s, wld)
                    t.wait_ge(vv.sem, max(prev, wz))
                    t.matmul(ps_tf[:, 0:nr], wrow[0:nr, cc * 128:(cc + 1) * 128],
                             eyef_sb[0:nr, 0:nr], is_transpose=True, start=True, stop=True,
                             skip_group_check=True).then_inc(pe.sem, 1)
                T(_); tr = pe.inc()
                def _(v, tr=tr, cc=cc, coff=coff, nr=nr):
                    v.wait_ge(pe.sem, tr)
                    v.tensor_copy(wallf[:, 0:nr], ps_tf[:, 0:nr]).then_inc(vv.sem, 1)
                    v.wait_ge(vv.sem, vv.n + 1)
                    v.tensor_copy(wallT[:, cc, coff:coff + nr], wallf[:, 0:nr]).then_inc(vv.sem, 1)
                V(_); vv.inc(2)
        def _(t):
            t.drain()
            t.nop().then_inc(pe.sem, 1)
        T(_); pe.inc()
        wall_rdy = vv.n
        def _(g):
            for cc in range(8):
                g.dma_start(out=wb_bf[:, cc], in_=wb[cc * 128:(cc + 1) * 128, :]).then_inc(wbl_s, 16)
        G(_); wb_ld = wbl.inc(16 * 8)

        for i in range(8):
            prev = vv.n
            def _(g, i=i, prev=prev):
                g.wait_ge(vv.sem, prev)
                g.dma_start(out=x_t[:], in_=x[i * 128:(i + 1) * 128, 0:2048]).then_inc(xl_s, 16)
            G(_); xl1 = xlc.inc(16)
            def _(v, xl1=xl1):
                v.wait_ge(xl_s, xl1)
                for cch in range(4):
                    op = v.bn_stats(stats[:, cch * 6:(cch + 1) * 6], x_t[:, cch * 512:(cch + 1) * 512])
                op.then_inc(vv.sem, 1)
            V(_); h1 = vv.inc()
            def _(g, i=i, h1=h1):
                g.wait_ge(vv.sem, h1)
                g.dma_start(out=x_t[:], in_=x[i * 128:(i + 1) * 128, 2048:4096]).then_inc(xl_s, 16)
            G(_); xl2 = xlc.inc(16)
            def _(v, xl2=xl2):
                v.wait_ge(xl_s, xl2)
                for cch in range(4):
                    op = v.bn_stats(stats[:, 24 + cch * 6:24 + (cch + 1) * 6], x_t[:, cch * 512:(cch + 1) * 512])
                op.then_inc(vv.sem, 1)
                v.wait_ge(vv.sem, vv.n + 1)
                v.bn_aggr(mv[:], stats[:]).then_inc(vv.sem, 1)
                v.wait_ge(vv.sem, vv.n + 2)
                v.tensor_scalar(sc1[:, 0:1], mv[:, 1:2], 1.0e-5, None, ALU.add).then_inc(vv.sem, 1)
            V(_); h2 = vv.inc(3)
            def _(s, h2=h2):
                s.wait_ge(vv.sem, h2)
                s.activation(sc1[:, 3:4], sc1[:, 0:1], AF.Ln).then_inc(ss.sem, 1)
                s.wait_ge(ss.sem, ss.n + 1)
                s.activation(sc1[:, 1:2], sc1[:, 3:4], AF.Exp, scale=-0.5).then_inc(ss.sem, 1)
            Sc(_); s2 = ss.inc(2)
            def _(v, s2=s2):
                v.wait_ge(ss.sem, s2)
                v.tensor_scalar(sc1[:, 2:3], mv[:, 0:1], sc1[:, 1:2], -1.0, ALU.mult, ALU.mult).then_inc(vv.sem, 1)
                v.wait_ge(vv.sem, vv.n + 1)
                v.reciprocal(sc1[:, 4:5], sc1[:, 0:1]).then_inc(vv.sem, 1)
                v.wait_ge(vv.sem, vv.n + 2)
                v.tensor_scalar(sc1[:, 4:5], sc1[:, 4:5], mv[:, 1:2], 1.1920929e-07, ALU.mult, ALU.add).then_inc(vv.sem, 1)
            V(_); rm1 = vv.inc(3)
            def _(s, rm1=rm1):
                s.wait_ge(vv.sem, rm1)
                s.activation(sc1[:, 5:6], sc1[:, 4:5], AF.Ln).then_inc(ss.sem, 1)
                s.wait_ge(ss.sem, ss.n + 1)
                s.activation(sc1[:, 4:5], sc1[:, 5:6], AF.Exp, scale=-0.5).then_inc(ss.sem, 1)
            Sc(_); rm2 = ss.inc(2)
            for half in (1, 0):
                if half == 0:
                    def _(g, i=i, w=vv.n):
                        g.wait_ge(vv.sem, w)
                        g.dma_start(out=x_t[:], in_=x[i * 128:(i + 1) * 128, 0:2048]).then_inc(xl_s, 16)
                    G(_); xl3 = xlc.inc(16)
                else:
                    xl3 = xlc.n
                def _(v, xl3=xl3, rm2=rm2):
                    v.wait_ge(xl_s, xl3)
                    v.wait_ge(ss.sem, rm2)
                    v.tensor_scalar(xn_f[:], x_t[:], sc1[:, 1:2], sc1[:, 2:3], ALU.mult, ALU.add).then_inc(vv.sem, 1)
                    v.wait_ge(vv.sem, vv.n + 1)
                    v.tensor_scalar(xr_bf[:], xn_f[:], sc1[:, 4:5], None, ALU.mult).then_inc(vv.sem, 1)
                V(_); xh = vv.inc(2)
                def _(s, xh=xh, i=i, half=half):
                    s.wait_ge(vv.sem, xh)
                    s.dma_start(out=xn_scr[i][:, half * 2048:(half + 1) * 2048], in_=xn_f.ap()).then_inc(st.sem, 16)
                Sy(_); st.inc(16)
                cbase = 16 * half
                for grp in range(4):
                    prev_evac = vv.n
                    def _(t, xh=xh, grp=grp, prev_evac=prev_evac):
                        t.wait_ge(vv.sem, xh)
                        t.wait_ge(vv.sem, prev_evac)
                        for j in range(4):
                            cc = grp * 4 + j
                            mm = t.matmul(ps_t[:, j * 128:(j + 1) * 128],
                                          xr_bf[:, cc * 128:(cc + 1) * 128], eye_sb[:],
                                          is_transpose=True, start=(j == 0), stop=True,
                                          skip_group_check=True)
                        mm.then_inc(pe.sem, 1)
                    T(_); tr = pe.inc()
                    def _(v, tr=tr, grp=grp, cbase=cbase, w=st.n):
                        v.wait_ge(pe.sem, tr)
                        v.wait_ge(st.sem, w)
                        v.tensor_copy(xTr[:, cbase + grp * 4:cbase + grp * 4 + 4, :],
                                      ps_t.ap().rearrange("p (j n) -> p j n", n=128)).then_inc(vv.sem, 1)
                    V(_); vv.inc()
            xtr_rdy = vv.n
            def _(t, xtr_rdy=xtr_rdy, wall_rdy=wall_rdy):
                t.wait_ge(vv.sem, max(xtr_rdy, wall_rdy))
                for cc in range(32):
                    mm = t.matmul(ps_a[:], wallT[:, cc, :], xTr[:, cc, :],
                                  start=(cc == 0), stop=(cc == 31))
                mm.then_inc(pe.sem, 1)
            T(_); ad = pe.inc()
            def _(v, ad=ad):
                v.wait_ge(pe.sem, ad)
                v.tensor_copy(a_sb[0:24, :], ps_a[0:24, :]).then_inc(vv.sem, 1)
            V(_); aev = vv.inc()
            def _(t, aev=aev):
                t.wait_ge(vv.sem, aev)
                t.matmul(ps_tf[:, 0:24], a_sb[0:24, :], eyef_sb[0:24, 0:24], is_transpose=True,
                         start=True, stop=True, skip_group_check=True).then_inc(pe.sem, 1)
            T(_); atr = pe.inc()
            def _(v, atr=atr, i=i):
                v.wait_ge(pe.sem, atr)
                v.tensor_copy(z_all[:, i, :], ps_tf[:, 0:24]).then_inc(vv.sem, 1)
            V(_); vv.inc()
        z_rdy = vv.n

        def _(v, z_rdy=z_rdy, bc_rdy=bc_rdy):
            v.wait_ge(vv.sem, max(z_rdy, bc_rdy))
            v.tensor_tensor(M.ap().rearrange("p (g j) -> p g j", j=16), z_all[:, :, 0:16],
                            base16r.ap().rearrange("p (g j) -> p g j", j=16), ALU.add).then_inc(vv.sem, 1)
            v.wait_ge(vv.sem, vv.n + 1)
            v.tensor_scalar(M[:], M[:], 10.0, None, ALU.mult).then_inc(vv.sem, 1)
            v.tensor_tensor(Pl.ap().rearrange("p a (g j) -> p a g j", j=4)[:, 0],
                            z_all[:, :, 16:20],
                            base4r.ap().rearrange("p a (g j) -> p a g j", j=4)[:, 0], ALU.add)
            v.tensor_tensor(Pl.ap().rearrange("p a (g j) -> p a g j", j=4)[:, 1],
                            z_all[:, :, 20:24],
                            base4r.ap().rearrange("p a (g j) -> p a g j", j=4)[:, 1], ALU.add).then_inc(vv.sem, 1)
        V(_); m0 = vv.inc(3)
        cur_v = m0
        for it in range(50):
            for phase in range(2):
                def _(s, cur_v=cur_v):
                    s.wait_ge(vv.sem, cur_v)
                    s.activation(E[:], M[:], AF.Exp).then_inc(ss.sem, 1)
                Sc(_); ex = ss.inc()
                def _(v, ex=ex, phase=phase):
                    v.wait_ge(ss.sem, ex)
                    if phase == 0:
                        v.tensor_reduce(S[:], E.ap().rearrange("p (a j) -> p a j", j=4),
                                        mybir.AxisListType.X, ALU.add).then_inc(vv.sem, 1)
                    else:
                        ev = E.ap().rearrange("p (g i j) -> p g j i", i=4, j=4)
                        v.tensor_reduce(S[:], ev, mybir.AxisListType.X, ALU.add).then_inc(vv.sem, 1)
                V(_); sv = vv.inc()
                def _(s, sv=sv):
                    s.wait_ge(vv.sem, sv)
                    s.activation(L[:], S[:], AF.Ln).then_inc(ss.sem, 1)
                Sc(_); lg = ss.inc()
                def _(v, lg=lg, phase=phase):
                    v.wait_ge(ss.sem, lg)
                    if phase == 0:
                        mview = M.ap().rearrange("p (a j) -> p j a", j=4)
                        for j in range(4):
                            op = v.tensor_tensor(mview[:, j], mview[:, j], L[:], ALU.subtract)
                    else:
                        mview = M.ap().rearrange("p (g i j) -> p i g j", i=4, j=4)
                        lview = L.ap().rearrange("p (g j) -> p g j", j=4)
                        for j in range(4):
                            op = v.tensor_tensor(mview[:, j], mview[:, j], lview, ALU.subtract)
                    op.then_inc(vv.sem, 1)
                V(_); cur_v = vv.inc()
        def _(s, cur_v=cur_v):
            s.wait_ge(vv.sem, cur_v)
            s.activation(Hr[:], M[:], AF.Exp).then_inc(ss.sem, 1)
        Sc(_); hx = ss.inc()
        def _(v, hx=hx):
            v.wait_ge(ss.sem, hx)
            v.tensor_scalar(Hr[:], Hr[:], g_bc[:, 0:1], None, ALU.mult).then_inc(vv.sem, 1)
            v.wait_ge(vv.sem, vv.n + 1)
            v.tensor_tensor(Hr[:], Hr[:], ig_rep[:], ALU.add).then_inc(vv.sem, 1)
        V(_); hr_rdy = vv.inc(2)
        def _(s, m0=m0):
            s.wait_ge(vv.sem, m0)
            s.activation(Pl[:], Pl[:], AF.Exp).then_inc(ss.sem, 1)
        Sc(_); px = ss.inc()
        def _(v, px=px):
            v.wait_ge(ss.sem, px)
            v.tensor_reduce(S[:, 0:16], Pl.ap().rearrange("p a (g j) -> p (a g) j", j=4),
                            mybir.AxisListType.X, ALU.add).then_inc(vv.sem, 1)
            v.wait_ge(vv.sem, vv.n + 1)
            v.reciprocal(S[:, 16:32], S[:, 0:16]).then_inc(vv.sem, 1)
            v.wait_ge(vv.sem, vv.n + 2)
            for j in range(4):
                pv = Hpre.ap().rearrange("p a (g j) -> p j (a g)", j=4)
                ev2 = Pl.ap().rearrange("p a (g j) -> p j (a g)", j=4)
                op = v.tensor_tensor(pv[:, j], ev2[:, j], S[:, 16:32], ALU.mult)
            op.then_inc(vv.sem, 1)
        V(_); hp_rdy = vv.inc(3)

        for i in range(8):
            w0 = vv.n
            def _(g, i=i, w0=w0, st0=st.n):
                g.wait_ge(vv.sem, w0)
                g.wait_ge(st.sem, st0)
                g.dma_start(out=xn_t[:], in_=xn_scr[i]).then_inc(xl_s, 16)
            G(_); xnl = xlc.inc(16)
            def _(v, xnl=xnl, i=i, hp_rdy=hp_rdy):
                v.wait_ge(xl_s, xnl)
                v.wait_ge(vv.sem, hp_rdy)
                v.tensor_scalar(xp[:, 0], xn_t[:, 0:1024], Hpre[:, 0, i * 4:i * 4 + 1], None, ALU.mult).then_inc(vv.sem, 1)
                for j in range(1, 4):
                    v.wait_ge(vv.sem, vv.n + j)
                    v.scalar_tensor_tensor(xp[:, j % 2], xn_t[:, j * 1024:(j + 1) * 1024],
                                           Hpre[:, 0, i * 4 + j:i * 4 + j + 1], xp[:, (j + 1) % 2],
                                           ALU.mult, ALU.add).then_inc(vv.sem, 1)
                v.wait_ge(vv.sem, vv.n + 4)
                v.tensor_copy(xpT.ap().rearrange("p a b -> p (a b)")[:, 0:1024] if False else xp[:, 0, 0:1024], xp[:, 1]).then_inc(vv.sem, 1)
            V(_); xp_rdy = vv.inc(5)
            def _(v, xp_rdy=xp_rdy):
                v.wait_ge(vv.sem, xp_rdy)
                v.tensor_copy(xr_bf[:, 0:1024], xp[:, 0]).then_inc(vv.sem, 1)
            V(_); xpb = vv.inc()
            for grp in range(2):
                prev_evac = vv.n
                def _(t, xpb=xpb, grp=grp, prev_evac=prev_evac):
                    t.wait_ge(vv.sem, xpb)
                    t.wait_ge(vv.sem, prev_evac)
                    for j in range(4):
                        cc = grp * 4 + j
                        mm = t.matmul(ps_t[:, j * 128:(j + 1) * 128],
                                      xr_bf[:, cc * 128:(cc + 1) * 128], eye_sb[:],
                                      is_transpose=True, start=(j == 0), stop=True,
                                      skip_group_check=True)
                    mm.then_inc(pe.sem, 1)
                T(_); tr = pe.inc()
                def _(v, tr=tr, grp=grp):
                    v.wait_ge(pe.sem, tr)
                    v.tensor_copy(xpT[:, grp * 4:grp * 4 + 4, :],
                                  ps_t.ap().rearrange("p (j n) -> p j n", n=128)).then_inc(vv.sem, 1)
                V(_); vv.inc()
            xpt_rdy = vv.n
            ydone = []
            for half in range(2):
                def _(t, half=half, xpt_rdy=xpt_rdy, wb_ld=wb_ld, yfree=vv.n):
                    t.wait_ge(vv.sem, xpt_rdy)
                    t.wait_ge(wbl_s, wb_ld)
                    t.wait_ge(vv.sem, yfree)
                    for cc in range(8):
                        mm = t.matmul(ps_y[half][:], xpT[:, cc, :],
                                      wb_bf[:, cc, half * 512:(half + 1) * 512],
                                      start=(cc == 0), stop=(cc == 7))
                    mm.then_inc(pe.sem, 1)
                T(_); ydone.append(pe.inc())
            mix_done = []
            for istr in range(4):
                def _(v, istr=istr, i=i, xnl=xnl, ydone=ydone, hr_rdy=hr_rdy):
                    v.wait_ge(pe.sem, ydone[1])
                    v.wait_ge(vv.sem, hr_rdy)
                    v.tensor_scalar(accm[:, 0], xn_t[:, 0:1024],
                                    Hr[:, i * 16 + istr * 4:i * 16 + istr * 4 + 1], None, ALU.mult).then_inc(vv.sem, 1)
                    for j in range(1, 4):
                        v.wait_ge(vv.sem, vv.n + j)
                        v.scalar_tensor_tensor(accm[:, j % 2], xn_t[:, j * 1024:(j + 1) * 1024],
                                               Hr[:, i * 16 + istr * 4 + j:i * 16 + istr * 4 + j + 1],
                                               accm[:, (j + 1) % 2], ALU.mult, ALU.add).then_inc(vv.sem, 1)
                    for half in range(2):
                        v.wait_ge(vv.sem, vv.n + 4 + half)
                        v.scalar_tensor_tensor(g1f[:, istr * 1024 + half * 512: istr * 1024 + (half + 1) * 512],
                                               ps_y[half][:],
                                               Hpre[:, 1, i * 4 + istr:i * 4 + istr + 1],
                                               accm[:, 1, half * 512:(half + 1) * 512], ALU.mult, ALU.add).then_inc(vv.sem, 1)
                V(_); mix_done.append(vv.inc(6))
            for half in range(2):
                w1 = vv.n
                def _(g, i=i, half=half, w1=w1):
                    g.wait_ge(vv.sem, w1)
                    g.dma_start(out=x_t[:], in_=x[i * 128:(i + 1) * 128, half * 2048:(half + 1) * 2048]).then_inc(xl_s, 16)
                    g.dma_start(out=s_ld.ap(),
                                in_=s1T[half * 2048:(half + 1) * 2048, i * 128:(i + 1) * 128]
                                .rearrange("(a p) t -> p a t", p=128)).then_inc(xl_s, 16)
                G(_); xl4 = xlc.inc(32)
                for grp in range(4):
                    prev_evac = vv.n
                    def _(t, grp=grp, prev_evac=prev_evac, xl4=xl4):
                        t.wait_ge(xl_s, xl4)
                        t.wait_ge(vv.sem, prev_evac)
                        for j in range(4):
                            oc = grp * 4 + j
                            mm = t.matmul(ps_tf[:, j * 128:(j + 1) * 128], s_ld[:, oc, :],
                                          eyef_sb[:], is_transpose=True, start=(j == 0), stop=True,
                                          skip_group_check=True)
                        mm.then_inc(pe.sem, 1)
                    T(_); tr = pe.inc()
                    def _(v, tr=tr, grp=grp):
                        v.wait_ge(pe.sem, tr)
                        v.tensor_copy(s_t[:, grp * 512:(grp + 1) * 512], ps_tf[:]).then_inc(vv.sem, 1)
                    V(_); vv.inc()
                s_rdy = vv.n
                def _(s, half=half, w=mix_done[3], s_rdy=s_rdy):
                    s.wait_ge(vv.sem, max(w, s_rdy))
                    s.activation(gt[:], g1f[:, half * 2048:(half + 1) * 2048], AF.Silu).then_inc(ss.sem, 1)
                Sc(_); gsil = ss.inc()
                def _(v, gsil=gsil, xl4=xl4):
                    v.wait_ge(ss.sem, gsil)
                    v.wait_ge(xl_s, xl4)
                    v.tensor_tensor(tmp[:], s_t[:], gt[:], ALU.mult).then_inc(vv.sem, 1)
                    v.wait_ge(vv.sem, vv.n + 1)
                    v.tensor_tensor(tmp[:], x_t[:], tmp[:], ALU.add).then_inc(vv.sem, 1)
                V(_); cmb = vv.inc(2)
                def _(s, cmb=cmb, i=i, half=half):
                    s.wait_ge(vv.sem, cmb)
                    s.dma_start(out=out_o[i * 128:(i + 1) * 128, half * 2048:(half + 1) * 2048],
                                in_=tmp[:]).then_inc(st.sem, 16)
                Sy(_); st.inc(16)
                def _(v, w=st.n):
                    v.wait_ge(st.sem, w)
                V(_)
        def _(s):
            s.wait_ge(st.sem, st.n)
        Sy(_)
    return nc


def run_l1_ffn(inp, x1, s1T):
    nc = build_l1_ffn()
    eye = np.eye(128, dtype=BF16NP)
    eyef = np.eye(128, dtype=np.float32)
    eye4 = np.eye(4, dtype=np.float32).reshape(1, 16)
    onesf = np.ones((1, 128), dtype=np.float32)
    bases = np.concatenate([inp["L1_ffn_H_res_base"].reshape(-1),
                            inp["L1_ffn_H_pre_base"].reshape(-1),
                            inp["L1_ffn_H_post_base"].reshape(-1)]).reshape(1, 24)
    maps = []
    for c in range(NCORE):
        maps.append(dict(
            x=np.ascontiguousarray(x1[c]), s1T=np.ascontiguousarray(s1T[c]),
            wres=inp["L1_ffn_W_res"], wpre=inp["L1_ffn_W_pre"], wpost=inp["L1_ffn_W_post"],
            bases=np.ascontiguousarray(bases), gate=inp["L1_ffn_gate"].reshape(1, 1),
            wb=inp["L1_ffn_W_branch"], eye4=eye4, eye=eye, eyef=eyef, onesf_in=onesf))
    res = run_bass_kernel_spmd(nc, maps, core_ids=list(range(NCORE)))
    return np.stack([res.results[c]["out_o"] for c in range(NCORE)])



def build_fused():
    nc = bass.Bass("TRN2", target_bir_lowering=False)
    AP_IN = lambda n, s, d: nc.dram_tensor(n, s, d, kind="ExternalInput").ap()
    x0 = AP_IN("x0", [TPC, D], F32)
    wq0 = AP_IN("wq0", [D, D], F32)
    bq0 = AP_IN("bq0", [1, D], F32)
    wk0 = AP_IN("wk0", [KVD, D], F32)
    wv0 = AP_IN("wv0", [KVD, D], F32)
    bk0 = AP_IN("bk0", [1, KVD], F32)
    bv0 = AP_IN("bv0", [1, KVD], F32)
    wo0 = AP_IN("wo0", [D, D], F32)
    bo_pp = AP_IN("bo_pp", [128, 32], F32)
    gq = AP_IN("gq", [D, NBLK * 16], I32)
    gz = AP_IN("gz", [32, NBLK * 16], I32)
    gs = AP_IN("gs", [32, NBLK * 128], F32)
    uq = AP_IN("uq", [D, NBLK * 16], I32)
    uz = AP_IN("uz", [32, NBLK * 16], I32)
    us = AP_IN("us", [32, NBLK * 128], F32)
    dq = AP_IN("dq", [DI, 64], I32)
    dz = AP_IN("dz", [96, 64], I32)
    ds = AP_IN("ds", [96, 512], F32)
    wq1 = AP_IN("wq1", [D, D], F32)
    wk1 = AP_IN("wk1", [D, KVD], F32)
    wv1 = AP_IN("wv1", [D, KVD], F32)
    wo1 = AP_IN("wo1", [D, D], F32)
    wres = AP_IN("wres", [16, D], F32)
    wpre = AP_IN("wpre", [4, D], F32)
    wpost = AP_IN("wpost", [4, D], F32)
    bases = AP_IN("bases", [1, 24], F32)
    gate = AP_IN("gate", [1, 1], F32)
    wb = AP_IN("wb", [1024, 1024], F32)
    eye = AP_IN("eye", [128, 128], BF16)
    sel_in = AP_IN("sel", [32, 4096], BF16)
    eyef = AP_IN("eyef", [128, 128], F32)
    onesf_in = AP_IN("onesf_in", [1, 128], F32)
    eye4 = AP_IN("eye4", [1, 16], F32)
    mask1 = AP_IN("mask1", [128, 512], F32)
    cosr = AP_IN("cosr", [TPC, 256], F32)
    sinr = AP_IN("sinr", [TPC, 256], F32)
    xidx = AP_IN("xidx", [128, 8], mybir.dt.uint32)
    out_q = nc.dram_tensor("out_q", [TPC, D], F32, kind="ExternalOutput").ap()

    _dbg = os.environ.get("DBG_OUT", "").split(",") if "DBG_OUT" in os.environ else []
    DR = lambda n, s, d: nc.dram_tensor(n, s, d).ap()
    wg_l = DR("wg_l", [NBLK, D, 128], BF16)
    wu_l = DR("wu_l", [NBLK, D, 128], BF16)
    wd_l = DR("wd_l", [DI, 512], BF16)
    wg_all = DR("wg_all", [8 * NBLK, D, 128], BF16)
    wu_all = DR("wu_all", [8 * NBLK, D, 128], BF16)
    wd_all = DR("wd_all", [8 * DI, 512], BF16)
    kvk_l = DR("kvk_l", [129, 1024], F32)
    kvk_s = DR("kvk_s", [129, 1024], F32)
    xt_scr = DR("xt_scr", [8, 128, D], BF16)
    sT_scr = DR("sT_scr", [D, TPC], F32)
    x1_la = DR("x1_la", [512, D], F32)
    x1_lb = DR("x1_lb", [512, D], F32)
    x1_h1 = DR("x1_h1", [2048, D], F32)
    x1_h2 = DR("x1_h2", [2048, D], F32)
    xt1_scr = DR("xt1_scr", [8, 128, D], BF16)
    xn1_scr = DR("xn1_scr", [8, 128, D], F32)
    q_scr = DR("q_scr", [TPC, D], BF16)
    k_l = DR("k_l", [TPC, KVD], BF16)
    v_l = DR("v_l", [TPC, KVD], BF16)
    k_all = DR("k_all", [4096, KVD], BF16)
    v_all = DR("v_all", [4096, KVD], BF16)
    qn2_l = DR("qn2_l", [TPC, 32], F32)
    knm_l = DR("knm_l", [1, 1], F32)
    knm_s = DR("knm_s", [1, 1], F32)
    s1T_scr = DR("s1T_scr", [D, TPC], F32)
    rms_scr = DR("rms_scr", [8, 128, 1], F32)

    GROUPS = [[0, 1, 2, 3], [4, 5, 6, 7]]
    ALL8 = [list(range(8))]

    from contextlib import ExitStack
    with ExitStack() as _ctx:
        def semx(n): return _ctx.enter_context(nc.semaphore(n))
        names = ["ld", "pe", "vv", "ss", "st", "wl0", "wl1", "xl0", "xl1",
                 "dl0", "dl1", "so0", "so1", "cl0", "cl1", "kl", "ql", "qtl",
                 "qn", "xg", "bqs", "ksl", "wll", "wbl", "cc", "sl", "st1",
                 "ul0", "ul1"]
        SEM = {n: semx(n) for n in names}
        C = {n: Ctr(SEM[n]) for n in names}
        ld = C["ld"]; pe = C["pe"]; vv = C["vv"]; ss = C["ss"]; st = C["st"]
        cc = C["cc"]
        block = _ctx.enter_context(nc.Block())
        G, Sy, T, V, Sc = block.gpsimd, block.sync, block.tensor, block.vector, block.scalar

        g_eye = _ctx.enter_context(nc.sbuf_tensor("g_eye", [128, 128], BF16))
        g_eyef = _ctx.enter_context(nc.sbuf_tensor("g_eyef", [128, 128], F32))
        g_onesf = _ctx.enter_context(nc.sbuf_tensor("g_onesf", [1, 128], F32))
        def _(g):
            g.dma_start(out=g_eye[:], in_=eye).then_inc(SEM["ld"], 16)
            g.dma_start(out=g_eyef[:], in_=eyef).then_inc(SEM["ld"], 16)
            g.dma_start(out=g_onesf[:], in_=onesf_in).then_inc(SEM["ld"], 16)
        G(_); ld.inc(48)
        consts_ld = ld.n

        def snap():
            return {n: C[n].n for n in names}

        def wait_all(eng, sn):
            for n in ("pe", "vv", "ss", "bqs", "xg", "qn", "ksl", "sl"):
                if sn[n]:
                    eng.wait_ge(SEM[n], sn[n])

        _pa = ExitStack()
        def sbA(*a): return _pa.enter_context(nc.sbuf_tensor(*a))
        q_pp = sbA("q_pp", [128, 2, 176], I32)
        zs_g = sbA("zs_g", [32, 2, 1408], BF16)
        zs_u = sbA("zs_u", [32, 2, 1408], BF16)
        zs_d = sbA("zs_d", [32, 2, 3, 512], BF16)
        nibA = sbA("nibA", [128, 512], I32)
        tmpA = sbA("tmpA", [128, 512], F32)
        w_sb = sbA("w_sb", [128, 4, 512], BF16)
        selA = sbA("selA", [32, 4096], BF16)
        ps_zs = _pa.enter_context(nc.psum_tensor("ps_zs", [128, 2, 512], F32))
        _pb = ExitStack()
        def sbB(*a): return _pb.enter_context(nc.sbuf_tensor(*a))
        def psmB(*a): return _pb.enter_context(nc.psum_tensor(*a))
        ones_c = sbB("ones_c", [128, 128], BF16)
        bk_bc = sbB("bk_bc", [128, KVD], F32)
        bv_bc = sbB("bv_bc", [128, KVD], F32)
        wt_sh = sbB("wt_sh", [128, 32, KVD], BF16)
        zrow = sbB("zrow", [1, 512], BF16)
        wrow = sbB("wrowB", [128, D], BF16)
        x_t = sbB("x_tB", [128, 2048], F32)
        stats = sbB("statsB", [128, 48], F32)
        mvb = sbB("mvB", [128, 2], F32)
        sc1 = sbB("sc1B", [128, 4], F32)
        xn_bf = sbB("xn_bfB", [128, D], BF16)
        xT_t = sbB("xT_tB", [128, 32, 128], BF16)
        kf = sbB("kfB", [128, 512], F32)
        k2 = sbB("k2B", [128, 512], F32)
        k3 = sbB("k3B", [128, 512], F32)
        k_allb = sbB("k_allB", [128, 8, KVD], BF16)
        v_pp = sbB("v_ppB", [128, 2, KVD], BF16)
        ps_a = psmB("ps_a", [128, 512], F32)
        ps_b = psmB("ps_b", [128, 512], F32)
        ps_t = psmB("ps_t", [128, 512], BF16)
        ps_kv1 = psmB("ps_kv1", [128, 512], F32)
        ps_kv2 = psmB("ps_kv2", [128, 512], F32)

        _pas = ExitStack()
        def sbAs(*a): return _pas.enter_context(nc.sbuf_tensor(*a))
        z_raw_g = sbAs("z_raw_g", [32, 176], I32)
        z_raw_u = sbAs("z_raw_u", [32, 176], I32)
        z_raw_d = sbAs("z_raw_d", [96, 64], I32)
        s_raw_g = sbAs("s_raw_g", [32, 1408], F32)
        s_raw_u = sbAs("s_raw_u", [32, 1408], F32)
        s_raw_d = sbAs("s_raw_d", [96, 512], F32)
        znib_a = sbAs("znib_a", [96, 1408], I32)

        def _(g):
            g.dma_start(out=z_raw_g[:], in_=gz).then_inc(SEM["qtl"], 16)
            g.dma_start(out=z_raw_u[:], in_=uz).then_inc(SEM["qtl"], 16)
            g.dma_start(out=z_raw_d[:], in_=dz).then_inc(SEM["qtl"], 16)
            g.dma_start(out=s_raw_g[:], in_=gs).then_inc(SEM["qtl"], 16)
            g.dma_start(out=s_raw_u[:], in_=us).then_inc(SEM["qtl"], 16)
            g.dma_start(out=s_raw_d[:], in_=ds).then_inc(SEM["qtl"], 16)
            g.dma_start(out=selA[:], in_=sel_in).then_inc(SEM["qtl"], 16)
        G(_); au_ld = C["qtl"].inc(112)

        def _(g):
            g.dma_start(out=bk_bc[:], in_=bk0.partition_broadcast(128)).then_inc(SEM["cl0"], 16)
            g.dma_start(out=bv_bc[:], in_=bv0.partition_broadcast(128)).then_inc(SEM["cl1"], 16)
        G(_)
        bld = (C["cl0"].inc(16), C["cl1"].inc(16))
        def _(v):
            v.wait_ge(SEM["ld"], consts_ld)
            v.memset(ones_c[:], 1.0)
            v.memset(zrow[:], 0.0).then_inc(SEM["vv"], 1)
        V(_); ones_rdy = vv.inc()
        def _(t):
            t.wait_ge(SEM["vv"], ones_rdy)
            t.matmul(ps_kv1[:], zrow[0:1, 0:128], zrow[:], start=True, stop=True)
            t.matmul(ps_kv2[:], zrow[0:1, 0:128], zrow[:], start=True, stop=True).then_inc(SEM["pe"], 1)
        T(_); pe.inc()

        def emit_wtrans(w_in):
            for r in range(8):
                wrow_free = pe.n
                def _(g, wrow_free=wrow_free, w_in=w_in, r=r):
                    g.wait_ge(SEM["pe"], wrow_free)
                    g.dma_start(out=wrow[:], in_=w_in[r * 128:(r + 1) * 128, :]).then_inc(SEM["kl"], 16)
                G(_); row_ld = C["kl"].inc(16)
                for grp in range(8):
                    prev_evac = vv.n
                    def _(t, row_ld=row_ld, grp=grp, prev_evac=prev_evac):
                        t.wait_ge(SEM["kl"], row_ld)
                        t.wait_ge(SEM["vv"], prev_evac)
                        for j in range(4):
                            ccj = grp * 4 + j
                            mm = t.matmul(ps_t[:, j * 128:(j + 1) * 128],
                                          wrow[:, ccj * 128:(ccj + 1) * 128], g_eye[:],
                                          is_transpose=True, start=(j == 0), stop=True,
                                          skip_group_check=True)
                        mm.then_inc(SEM["pe"], 1)
                    T(_); tr = pe.inc()
                    def _(v, tr=tr, grp=grp, r=r):
                        v.wait_ge(SEM["pe"], tr)
                        v.tensor_copy(wt_sh[:, grp * 4:grp * 4 + 4, r * 128:(r + 1) * 128],
                                      ps_t.ap().rearrange("p (j n) -> p j n", n=128)).then_inc(SEM["vv"], 1)
                    V(_); vv.inc()
            return vv.n

        wkT_done = emit_wtrans(wk0)
        prologue_pe = pe.n
        prologue_vv = vv.n
        psums = [ps_a, ps_b]
        psum_free = [vv.n, vv.n]
        prev_xt_store = 0
        for i in range(8):
            x_ready_after = vv.n
            def _(g, x_ready_after=x_ready_after, i=i):
                if i == 0:
                    g.wait_ge(SEM["pe"], prologue_pe)
                    g.wait_ge(SEM["vv"], prologue_vv)
                else:
                    g.wait_ge(SEM["vv"], x_ready_after)
                g.dma_start(out=x_t[:], in_=x0[i * 128:(i + 1) * 128, 0:2048]).then_inc(SEM["xl0"], 16)
            G(_); x_ld = C["xl0"].inc(16)
            def _(v, x_ld=x_ld):
                v.wait_ge(SEM["xl0"], x_ld)
                for cch in range(4):
                    op = v.bn_stats(stats[:, cch * 6:(cch + 1) * 6], x_t[:, cch * 512:(cch + 1) * 512])
                op.then_inc(SEM["vv"], 1)
            V(_); w1 = vv.inc()
            def _(g, i=i, w1=w1):
                g.wait_ge(SEM["vv"], w1)
                g.dma_start(out=x_t[:], in_=x0[i * 128:(i + 1) * 128, 2048:4096]).then_inc(SEM["xl0"], 16)
            G(_); x_ld2 = C["xl0"].inc(16)
            def _(v, x_ld2=x_ld2):
                v.wait_ge(SEM["xl0"], x_ld2)
                for cch in range(4):
                    op = v.bn_stats(stats[:, 24 + cch * 6:24 + (cch + 1) * 6], x_t[:, cch * 512:(cch + 1) * 512])
                op.then_inc(SEM["vv"], 1)
                v.wait_ge(SEM["vv"], vv.n + 1)
                v.bn_aggr(mvb[:], stats[:]).then_inc(SEM["vv"], 1)
                v.wait_ge(SEM["vv"], vv.n + 2)
                v.tensor_scalar(sc1[:, 0:1], mvb[:, 1:2], 1.0e-5, None, ALU.add).then_inc(SEM["vv"], 1)
            V(_); w3 = vv.inc(3)
            def _(s, w3=w3):
                s.wait_ge(SEM["vv"], w3)
                s.activation(sc1[:, 3:4], sc1[:, 0:1], AF.Ln).then_inc(SEM["ss"], 1)
                s.wait_ge(SEM["ss"], ss.n + 1)
                s.activation(sc1[:, 1:2], sc1[:, 3:4], AF.Exp, scale=-0.5).then_inc(SEM["ss"], 1)
            Sc(_); s2v = ss.inc(2)
            def _(v, s2v=s2v):
                v.wait_ge(SEM["ss"], s2v)
                v.tensor_scalar(sc1[:, 2:3], mvb[:, 0:1], sc1[:, 1:2], -1.0, ALU.mult, ALU.mult).then_inc(SEM["vv"], 1)
                v.wait_ge(SEM["vv"], vv.n + 1)
                v.tensor_scalar(xn_bf[:, 2048:4096], x_t[:], sc1[:, 1:2], sc1[:, 2:3], ALU.mult, ALU.add).then_inc(SEM["vv"], 1)
            V(_); xnh2 = vv.inc(2)
            def _(g, i=i, xnh2=xnh2):
                g.wait_ge(SEM["vv"], xnh2)
                g.dma_start(out=x_t[:], in_=x0[i * 128:(i + 1) * 128, 0:2048]).then_inc(SEM["xl0"], 16)
            G(_); x_ld3 = C["xl0"].inc(16)
            def _(v, x_ld3=x_ld3):
                v.wait_ge(SEM["xl0"], x_ld3)
                v.tensor_scalar(xn_bf[:, 0:2048], x_t[:], sc1[:, 1:2], sc1[:, 2:3], ALU.mult, ALU.add).then_inc(SEM["vv"], 1)
            V(_); xn_rdy = vv.inc()
            for grp in range(8):
                prev_evac = vv.n
                def _(t, xn_rdy=xn_rdy, grp=grp, prev_evac=prev_evac, pxs=prev_xt_store):
                    t.wait_ge(SEM["vv"], xn_rdy)
                    t.wait_ge(SEM["vv"], prev_evac)
                    if grp == 0 and pxs:
                        t.wait_ge(SEM["sl"], pxs)
                    for j in range(4):
                        ccj = grp * 4 + j
                        mm = t.matmul(ps_t[:, j * 128:(j + 1) * 128],
                                      xn_bf[:, ccj * 128:(ccj + 1) * 128], g_eye[:],
                                      is_transpose=True, start=(j == 0), stop=True,
                                      skip_group_check=True)
                    mm.then_inc(SEM["pe"], 1)
                T(_); tr = pe.inc()
                def _(v, tr=tr, grp=grp):
                    v.wait_ge(SEM["pe"], tr)
                    v.tensor_copy(xT_t[:, grp * 4:grp * 4 + 4, :],
                                  ps_t.ap().rearrange("p (j n) -> p j n", n=128)).then_inc(SEM["vv"], 1)
                V(_); vv.inc()
            xT_rdy = vv.n
            def _(s, i=i, w=xT_rdy):
                s.wait_ge(SEM["vv"], w)
                s.dma_start(out=xt_scr[i], in_=xT_t.ap().rearrange("p a b -> p (a b)")).then_inc(SEM["sl"], 16)
            Sy(_); prev_xt_store = C["sl"].inc(16)
            for half in range(2):
                dst = psums[half]
                def _(t, dst=dst, half=half, xT_rdy=xT_rdy, free=psum_free[half],
                      wkT_done=wkT_done, bld=bld):
                    t.wait_ge(SEM["vv"], max(xT_rdy, free, wkT_done))
                    t.wait_ge(SEM["cl0"], bld[0])
                    t.wait_ge(SEM["cl1"], bld[1])
                    for ccj in range(32):
                        mm = t.matmul(dst[:], xT_t[:, ccj], wt_sh[:, ccj, half * 512:(half + 1) * 512],
                                      start=(ccj == 0), stop=(ccj == 31))
                    mm.then_inc(SEM["pe"], 1)
                T(_); mm_done = pe.inc()
                def _(v, mm_done=mm_done, dst=dst, half=half):
                    v.wait_ge(SEM["pe"], mm_done)
                    v.tensor_tensor(kf[:], dst[:], bk_bc[:, half * 512:(half + 1) * 512],
                                    ALU.add).then_inc(SEM["vv"], 1)
                V(_); a1 = vv.inc()
                psum_free[half] = a1
                def _(v, a1=a1):
                    v.wait_ge(SEM["vv"], a1)
                    v.tensor_scalar(k2[:], kf[:], 0.0, None, ALU.min).then_inc(SEM["vv"], 1)
                V(_); a2 = vv.inc()
                def _(s, a2=a2):
                    s.wait_ge(SEM["vv"], a2)
                    s.activation(k3[:], k2[:], AF.Exp).then_inc(SEM["ss"], 1)
                Sc(_); a3 = ss.inc()
                def _(v, a3=a3):
                    v.wait_ge(SEM["ss"], a3)
                    v.tensor_scalar(k2[:], kf[:], 0.0, None, ALU.max).then_inc(SEM["vv"], 1)
                V(_); a4 = vv.inc()
                def _(v, a3=a3, a4=a4, half=half, i=i):
                    v.wait_ge(SEM["ss"], a3)
                    v.wait_ge(SEM["vv"], a4)
                    v.tensor_tensor(k_allb[:, i, half * 512:(half + 1) * 512], k3[:],
                                    k2[:], ALU.add).then_inc(SEM["vv"], 1)
                V(_); vv.inc()
        kphase_sl = C["sl"].n
        wvT_done = emit_wtrans(wv0)
        psv_free = list(psum_free)
        kv_acc_done = [0, 0]
        for i in range(8):
            slot = i % 2
            xt_free = pe.n
            def _(g, i=i, kphase_sl=kphase_sl, xt_free=xt_free):
                g.wait_ge(SEM["sl"], kphase_sl)
                g.wait_ge(SEM["pe"], xt_free)
                g.dma_start(out=xT_t.ap().rearrange("p a b -> p (a b)"), in_=xt_scr[i]).then_inc(SEM["xl0"], 16)
            G(_); xrld = C["xl0"].inc(16)
            for half in range(2):
                dst = psums[half]
                def _(t, dst=dst, half=half, xrld=xrld, free=psv_free[half], wvT_done=wvT_done):
                    t.wait_ge(SEM["vv"], max(wvT_done, free))
                    t.wait_ge(SEM["xl0"], xrld)
                    for ccj in range(32):
                        mm = t.matmul(dst[:], xT_t[:, ccj], wt_sh[:, ccj, half * 512:(half + 1) * 512],
                                      start=(ccj == 0), stop=(ccj == 31))
                    mm.then_inc(SEM["pe"], 1)
                T(_); vmm = pe.inc()
                def _(v, vmm=vmm, dst=dst, slot=slot, half=half, kvfree=kv_acc_done[slot]):
                    v.wait_ge(SEM["pe"], max(vmm, kvfree))
                    v.tensor_tensor(v_pp[:, slot, half * 512:(half + 1) * 512], dst[:],
                                    bv_bc[:, half * 512:(half + 1) * 512], ALU.add).then_inc(SEM["vv"], 1)
                V(_); psv_free[half] = vv.inc()
            v_rdy = vv.n
            def _(t, v_rdy=v_rdy, i=i, slot=slot):
                t.wait_ge(SEM["vv"], v_rdy)
                for h in range(8):
                    dstp = ps_kv1 if h < 4 else ps_kv2
                    mm = t.matmul(dstp[:, (h % 4) * 128:(h % 4) * 128 + 128],
                                  k_allb[:, i, h * 128:(h + 1) * 128],
                                  v_pp[:, slot, h * 128:(h + 1) * 128],
                                  start=False, stop=(i == 7), skip_group_check=True)
                mm.then_inc(SEM["pe"], 1)
            T(_); kv_acc_done[slot] = pe.inc()
        ks_wait = max(psv_free)
        def _(t, ks_wait=ks_wait):
            t.wait_ge(SEM["vv"], ks_wait)
            for half in range(2):
                dst = psums[half]
                for i in range(8):
                    mm = t.matmul(dst[:], ones_c[:], k_allb[:, i, half * 512:(half + 1) * 512],
                                  start=(i == 0), stop=(i == 7), skip_group_check=True)
            mm.then_inc(SEM["pe"], 1)
        T(_); ks_done = pe.inc()
        def _(v, ks_done=ks_done):
            v.wait_ge(SEM["pe"], ks_done)
            v.tensor_copy(kf[:], ps_kv1[:])
            v.tensor_copy(k2[:], ps_kv2[:])
            v.tensor_copy(k3[0:1, :], ps_a[0:1, :]).then_inc(SEM["vv"], 1)
        V(_); ev = vv.inc()
        def _(s, ev=ev):
            s.wait_ge(SEM["vv"], ev)
            s.dma_start(out=kvk_l.tensor.ap()[0:128, 0:512], in_=kf[:]).then_inc(SEM["sl"], 16)
            s.dma_start(out=kvk_l.tensor.ap()[0:128, 512:1024], in_=k2[:]).then_inc(SEM["sl"], 16)
            s.dma_start(out=kvk_l.tensor.ap()[128:129, 0:512], in_=k3[0:1, :]).then_inc(SEM["sl"], 16)
        Sy(_); C["sl"].inc(48)
        def _(v, w=C["sl"].n):
            v.wait_ge(SEM["sl"], w)
            v.tensor_copy(k3[0:1, :], ps_b[0:1, :]).then_inc(SEM["vv"], 1)
        V(_); ev2 = vv.inc()
        def _(s, ev2=ev2):
            s.wait_ge(SEM["vv"], ev2)
            s.dma_start(out=kvk_l.tensor.ap()[128:129, 512:1024], in_=k3[0:1, :]).then_inc(SEM["sl"], 16)
        Sy(_); C["sl"].inc(16)
        B_sl_total = C["sl"].n

        vr = C["ksl"]; qnA = C["qn"]; xgA = C["xg"]
        def _(v):
            v.wait_ge(SEM["qtl"], au_ld)
            for (zr, sr, zs_x, nkc, nblk) in ((z_raw_g, s_raw_g, zs_g, 32, 11),
                                              (z_raw_u, s_raw_u, zs_u, 32, 11),
                                              (z_raw_d, s_raw_d, zs_d, 96, 4)):
                Wz = nblk * 128
                if vr.n:
                    v.wait_ge(SEM["ksl"], vr.n)
                for j in range(8):
                    zi = zr[:].rearrange("p (b w) -> p b w", w=16)
                    zo = znib_a[0:nkc, 0:Wz].rearrange("p (b n) -> p b n", n=128)[:, :, j::8]
                    last = v.tensor_scalar(zo, zi, 28 - 4 * j, 28,
                                           ALU.logical_shift_left, ALU.logical_shift_right)
                last.then_inc(SEM["ksl"], 1)
                v.wait_ge(SEM["ksl"], vr.inc())
                if nkc == 96:
                    for b in range(3):
                        r1 = 32
                        v.tensor_copy(zs_x[0:r1, 0, b, :], znib_a[b * 32:b * 32 + r1, 0:512])
                        last = v.tensor_copy(zs_x[0:r1, 1, b, :], sr[b * 32:b * 32 + r1, :])
                    last.then_inc(SEM["qn"], 1)
                else:
                    v.tensor_copy(zs_x[:, 0, :], znib_a[0:nkc, 0:Wz])
                    v.tensor_copy(zs_x[:, 1, :], sr[:]).then_inc(SEM["ksl"], 1)
                    vr.inc()
        V(_); U = qnA.inc()

        itemsA = ([("g", kc) for kc in range(32)] + [("u", kc) for kc in range(32)]
                  + [("d", kc) for kc in range(86)])
        chunksA = []
        for ki, (kind, kc) in enumerate(itemsA):
            for ch in range(3 if kind != "d" else 1):
                chunksA.append((ki, kind, kc, ch))
        NCHA = len(chunksA)
        item_last_ci = {}
        for ci2, (ki, kind, kc, ch) in enumerate(chunksA):
            item_last_ci[ki] = ci2
        stS = [C["st"], C["st1"], C["so0"], C["so1"]]
        stNames = ["st", "st1", "so0", "so1"]
        sthist = {}
        _sv = [s.n for s in stS]
        boundsA = {}
        for ci2 in range(NCHA):
            _sv[ci2 % 4] += 16
            sthist[ci2] = _sv[ci2 % 4]
            if ci2 == 95:
                boundsA["g"] = list(_sv)
            if ci2 == 191:
                boundsA["u"] = list(_sv)
        boundsA["d"] = list(_sv)

        qlp = [C["wl0"], C["wl1"]]
        QSRC = {"g": gq, "u": uq, "d": dq}
        ci = 0
        for ki, (kind, kc) in enumerate(itemsA):
            if ki == 14:
                def _(g, w=B_sl_total):
                    g.wait_ge(SEM["sl"], w)
                    g.collective_compute("AllReduce", ALU.add, replica_groups=GROUPS,
                                         ins=[kvk_l.tensor.ap().opt()],
                                         outs=[kvk_s.tensor.ap().opt()]).then_inc(SEM["cc"], 1)
                G(_); cc.inc()
                kv_cc = cc.n
            if ki == 32:
                def _(g, b=boundsA["g"]):
                    for s4 in range(4):
                        g.wait_ge(SEM[stNames[s4]], b[s4])
                    g.collective_compute("AllGather", ALU.bypass, replica_groups=ALL8,
                                         ins=[wg_l.tensor.ap().opt()],
                                         outs=[wg_all.tensor.ap().opt()]).then_inc(SEM["cc"], 1)
                G(_); cc.inc()
            if ki == 64:
                def _(g, b=boundsA["u"]):
                    for s4 in range(4):
                        g.wait_ge(SEM[stNames[s4]], b[s4])
                    g.collective_compute("AllGather", ALU.bypass, replica_groups=ALL8,
                                         ins=[wu_l.tensor.ap().opt()],
                                         outs=[wu_all.tensor.ap().opt()]).then_inc(SEM["cc"], 1)
                G(_); cc.inc()
            nw = 176 if kind != "d" else 64
            need = (U + item_last_ci[ki - 2] + 1) if ki >= 2 else 0
            def _(g, kc=kc, ki=ki, need=need, nw=nw, src=QSRC[kind]):
                if need:
                    g.wait_ge(SEM["qn"], need)
                g.dma_start(out=q_pp[:, ki % 2, 0:nw],
                            in_=src[kc * 128:(kc + 1) * 128, :]).then_inc(
                    SEM["wl0"] if ki % 2 == 0 else SEM["wl1"], 16)
            G(_); qld = qlp[ki % 2].inc(16)
            zs_x = {"g": zs_g, "u": zs_u, "d": zs_d}[kind]
            for ch in range(3 if kind != "d" else 1):
                cw = 384 if (kind != "d" and ch == 2) else 512
                c0 = ch * 512
                niw = cw // 8
                if kind == "d":
                    rhs_z = zs_x[:, 0, kc // 32, :]
                    rhs_s = zs_x[:, 1, kc // 32, :]
                else:
                    rhs_z = zs_x[:, 0, c0:c0 + cw]
                    rhs_s = zs_x[:, 1, c0:c0 + cw]
                sel_l = selA[:, (kc % 32) * 128:(kc % 32 + 1) * 128]
                bc_need = (U + ci) if ci >= 1 else U
                def _(t, cw=cw, rhs_z=rhs_z, rhs_s=rhs_s, sel_l=sel_l, bc_need=bc_need):
                    t.wait_ge(SEM["qn"], bc_need)
                    t.matmul(ps_zs[:, 0, 0:cw], sel_l, rhs_z,
                             start=True, stop=True, skip_group_check=True)
                    mm = t.matmul(ps_zs[:, 1, 0:cw], sel_l, rhs_s,
                                  start=True, stop=True, skip_group_check=True)
                    mm.then_inc(SEM["xg"], 1)
                T(_); bc_done = xgA.inc()
                wsb_guard = sthist[ci - 4] if ci >= 4 else 0
                def _(v, ci=ci, ki=ki, ch=ch, cw=cw, niw=niw, qld=qld,
                      bc_done=bc_done, wsb_guard=wsb_guard):
                    v.wait_ge(SEM["wl0"] if ki % 2 == 0 else SEM["wl1"], qld)
                    qi = q_pp[:, ki % 2, ch * 64:ch * 64 + niw].rearrange("p (b w) -> p b w", w=16)
                    for j2 in range(8):
                        qo = nibA[:, 0:cw].rearrange("p (b n) -> p b n", n=128)[:, :, j2::8]
                        last = v.tensor_scalar(qo, qi, 28 - 4 * j2, 28,
                                               ALU.logical_shift_left, ALU.logical_shift_right)
                    last.then_inc(SEM["ksl"], 1)
                    v.wait_ge(SEM["ksl"], vr.inc())
                    v.wait_ge(SEM["xg"], bc_done)
                    v.tensor_tensor(tmpA[:, 0:cw], nibA[:, 0:cw], ps_zs[:, 0, 0:cw],
                                    ALU.subtract).then_inc(SEM["ksl"], 1)
                    v.wait_ge(SEM["ksl"], vr.inc())
                    if wsb_guard:
                        v.wait_ge(SEM[stNames[ci % 4]], wsb_guard)
                    v.tensor_tensor(w_sb[:, ci % 4, 0:cw], tmpA[:, 0:cw], ps_zs[:, 1, 0:cw],
                                    ALU.mult).then_inc(SEM["qn"], 1)
                V(_); ch_done = qnA.inc()
                if kind == "d":
                    def _(s, kc=kc, ci=ci, need2=ch_done):
                        s.wait_ge(SEM["qn"], need2)
                        s.dma_start(out=wd_l.tensor.ap()[kc * 128:(kc + 1) * 128, :],
                                    in_=w_sb[:, ci % 4, 0:512]).then_inc(SEM[stNames[ci % 4]], 16)
                    Sy(_)
                else:
                    dstw = wg_l if kind == "g" else wu_l
                    nb0 = ch * 4
                    nbc = cw // 128
                    def _(s, dstw=dstw, kc=kc, ci=ci, nb0=nb0, nbc=nbc, cw=cw, need2=ch_done):
                        s.wait_ge(SEM["qn"], need2)
                        s.dma_start(out=dstw.tensor.ap()[nb0:nb0 + nbc, kc * 128:(kc + 1) * 128, :]
                                    .rearrange("b p n -> p b n"),
                                    in_=w_sb[:, ci % 4, 0:cw].rearrange("p (b n) -> p b n", n=128)).then_inc(
                            SEM[stNames[ci % 4]], 16)
                    Sy(_)
                assert stS[ci % 4].inc(16) == sthist[ci]
                ci += 1
        assert ci == NCHA

        def _(g, b=boundsA["d"]):
            for s4 in range(4):
                g.wait_ge(SEM[stNames[s4]], b[s4])
            g.collective_compute("AllGather", ALU.bypass, replica_groups=ALL8,
                                 ins=[wd_l.tensor.ap().opt()], outs=[wd_all.tensor.ap().opt()]).then_inc(SEM["cc"], 1)
        G(_); cc.inc()
        wts_cc = cc.n
        _pas.close()
        _pb.close()
        snB = snap()

        with ExitStack() as _pc:
            def sb(*a): return _pc.enter_context(nc.sbuf_tensor(*a))
            def psm(*a): return _pc.enter_context(nc.psum_tensor(*a))
            kvt_bf = sb("kvt_bfC", [128, 1024], BF16)
            ksr = sb("ksrC", [1, 1024], F32)
            krec = sb("krecC", [1, 1024], F32)
            recip_pp = sb("recipC", [128, 8], F32)
            bo_sb = sb("bo_sbC", [128, 32], F32)
            xts = sb("xtsC", [128, 2, 32, 128], BF16)
            wrow = sb("wrowC", [128, 2, D], BF16)
            wt_slab = sb("wt_slabC", [128, 32, 512], BF16)
            bq_bc = sb("bq_bcC", [128, 512], F32)
            qf = sb("qfC", [128, 512], F32)
            q2 = sb("q2C", [128, 512], F32)
            q3 = sb("q3C", [128, 512], F32)
            qphi = sb("qphiC", [128, 2, 512], BF16)
            qphiT = sb("qphiTC", [128, 4, 8, 128], BF16)
            attnT = sb("attnTC", [128, 32, 1024], BF16)
            s_stc = sb("s_stC", [128, 2, 512], F32)
            ps_t = psm("ps_tC", [128, 512], BF16)
            ps_t2 = psm("ps_t2C", [128, 512], BF16)
            psT = [ps_t, ps_t2]
            ps_q = psm("ps_qC", [128, 512], F32)
            ps_q2 = psm("ps_q2C", [128, 512], F32)
            ps_at = psm("ps_atC", [128, 512], F32)
            ps_s = psm("ps_sC", [128, 512], F32)
            def _(g):
                g.wait_ge(SEM["cc"], kv_cc)
                g.dma_start(out=kvt_bf[:], in_=kvk_s.tensor.ap()[0:128, :]).then_inc(SEM["ld"], 16)
                g.dma_start(out=ksr[:], in_=kvk_s.tensor.ap()[128:129, :]).then_inc(SEM["ld"], 16)
                g.dma_start(out=bo_sb[:], in_=bo_pp).then_inc(SEM["ld"], 16)
            G(_); ld.inc(48)
            kv_ld = ld.n
            def _(v, kv_ld=kv_ld):
                v.wait_ge(SEM["ld"], kv_ld)
                v.tensor_scalar(ksr[:], ksr[:], 1.0e-10, None, ALU.add).then_inc(SEM["vv"], 1)
                v.wait_ge(SEM["vv"], vv.n + 1)
                v.reciprocal(krec[:], ksr[:]).then_inc(SEM["vv"], 1)
            V(_); rec_rdy = vv.inc(2)
            def _(t, rec_rdy=rec_rdy):
                t.wait_ge(SEM["vv"], rec_rdy)
                for h in range(8):
                    mm = t.matmul(ps_at[:, h:h + 1], krec[0:1, h * 128:(h + 1) * 128],
                                  g_eyef[0:1, 0:1], start=(h == 0), stop=True,
                                  skip_group_check=True)
                mm.then_inc(SEM["pe"], 1)
                t.drain()
            T(_); rt = pe.inc()
            def _(v, rt=rt):
                v.wait_ge(SEM["pe"], rt)
                v.tensor_copy(recip_pp[:], ps_at[:, 0:8]).then_inc(SEM["vv"], 1)
            V(_); vv.inc()

            wr_state = {"i": 0, "h": {-2: 0, -1: 0}}
            def emit_build_slabC(w_src, slab):
                ev_h = {-2: vv.n, -1: vv.n}
                gg = 0
                for sub in range(4):
                    ws = wr_state["i"]
                    wfree = wr_state["h"][ws - 2]
                    wsem = "wll" if ws % 2 == 0 else "ql"
                    def _(g, slab=slab, sub=sub, wfree=wfree, w_src=w_src, wsl=ws % 2, wsem=wsem):
                        if wfree:
                            g.wait_ge(SEM["pe"], wfree)
                        g.dma_start(out=wrow[:, wsl], in_=w_src[slab * 512 + sub * 128: slab * 512 + (sub + 1) * 128, :]).then_inc(SEM[wsem], 16)
                    G(_); wld = C[wsem].inc(16)
                    for grp in range(8):
                        prev_evac = ev_h[gg - 2]
                        def _(t, wld=wld, grp=grp, prev_evac=prev_evac, sl2=gg % 2, wsl=ws % 2, wsem=wsem):
                            t.wait_ge(SEM[wsem], wld)
                            t.wait_ge(SEM["vv"], prev_evac)
                            for j in range(4):
                                ccj = grp * 4 + j
                                mm = t.matmul(psT[sl2][:, j * 128:(j + 1) * 128],
                                              wrow[:, wsl, ccj * 128:(ccj + 1) * 128], g_eye[:],
                                              is_transpose=True, start=(j == 0), stop=True,
                                              skip_group_check=True)
                            mm.then_inc(SEM["pe"], 1)
                        T(_); tr = pe.inc()
                        def _(v, tr=tr, grp=grp, sub=sub, sl2=gg % 2):
                            v.wait_ge(SEM["pe"], tr)
                            v.tensor_copy(wt_slab[:, grp * 4:grp * 4 + 4, sub * 128:(sub + 1) * 128],
                                          psT[sl2].ap().rearrange("p (j n) -> p j n", n=128)).then_inc(SEM["vv"], 1)
                        V(_); ev_h[gg] = vv.inc()
                        gg += 1
                    wr_state["h"][ws] = pe.n
                    wr_state["i"] = ws + 1
                return vv.n

            ph1_sl = C["sl"].n
            tev = {-2: vv.n, -1: vv.n}
            trv = {-2: 0, -1: 0}
            tstate = {"tg": 0}
            def emit_qtr(a5, tt):
                tg = tstate["tg"]
                prev_evac = tev[tg - 2]
                def _(t, a5=a5, prev_evac=prev_evac, sl2=tg % 2, qsl=tt % 2):
                    t.wait_ge(SEM["vv"], a5)
                    t.wait_ge(SEM["vv"], prev_evac)
                    for j in range(4):
                        mm = t.matmul(psT[sl2][:, j * 128:(j + 1) * 128],
                                      qphi[:, qsl, j * 128:(j + 1) * 128], g_eye[:],
                                      is_transpose=True, start=(j == 0), stop=True,
                                      skip_group_check=True)
                    mm.then_inc(SEM["pe"], 1)
                T(_); trv[tt] = pe.inc()
                def _(v, qt_=trv[tt], i=tt % 8, sl2=tg % 2):
                    v.wait_ge(SEM["pe"], qt_)
                    v.tensor_copy(qphiT[:, 0:4, i, :],
                                  psT[sl2].ap().rearrange("p (j n) -> p j n", n=128)).then_inc(SEM["vv"], 1)
                V(_); tev[tg] = vv.inc()
                tstate["tg"] = tg + 1
            for slab in range(8):
                slab_rdy = emit_build_slabC(wq0, slab)
                def _(g, slab=slab):
                    g.dma_start(out=bq_bc[:], in_=bq0[:, slab * 512:(slab + 1) * 512].partition_broadcast(128)).then_inc(SEM["bqs"], 16)
                G(_); bq_ld = C["bqs"].inc(16)
                psq = [ps_q, ps_q2]
                psq_free = [vv.n, vv.n]
                xts_used = [pe.n, pe.n]
                pend = None
                for i in range(8):
                    tt = slab * 8 + i
                    slot = i % 2
                    def _(g, i=i, slot=slot, used=xts_used[slot], first=(slab == 0)):
                        if first and i < 2:
                            g.wait_ge(SEM["sl"], ph1_sl)
                        g.wait_ge(SEM["pe"], used)
                        g.dma_start(out=xts[:, slot].rearrange("p a b -> p (a b)"), in_=xt_scr[i]).then_inc(SEM["xl0"] if slot == 0 else SEM["xl1"], 16)
                    G(_); xtv = C["xl0" if slot == 0 else "xl1"].inc(16)
                    def _(t, i=i, slab_rdy=slab_rdy, slot=slot, free=psq_free[slot], xtv=xtv):
                        t.wait_ge(SEM["vv"], max(slab_rdy, free))
                        t.wait_ge(SEM["xl0"] if slot == 0 else SEM["xl1"], xtv)
                        for ccj in range(32):
                            mm = t.matmul(psq[slot][:], xts[:, slot, ccj], wt_slab[:, ccj],
                                          start=(ccj == 0), stop=(ccj == 31))
                        mm.then_inc(SEM["pe"], 1)
                    T(_); qd = pe.inc()
                    xts_used[slot] = qd
                    def _(v, qd=qd, slot=slot, bq_ld=bq_ld):
                        v.wait_ge(SEM["pe"], qd)
                        v.wait_ge(SEM["bqs"], bq_ld)
                        v.tensor_tensor(qf[:], psq[slot][:], bq_bc[:], ALU.add).then_inc(SEM["vv"], 1)
                    V(_); a1 = vv.inc()
                    psq_free[slot] = a1
                    def _(v, a1=a1):
                        v.wait_ge(SEM["vv"], a1)
                        v.tensor_scalar(q2[:], qf[:], 0.0, None, ALU.min).then_inc(SEM["vv"], 1)
                    V(_); a2 = vv.inc()
                    def _(s, a2=a2):
                        s.wait_ge(SEM["vv"], a2)
                        s.activation(q3[:], q2[:], AF.Exp).then_inc(SEM["ss"], 1)
                    Sc(_); a3 = ss.inc()
                    def _(v, a3=a3):
                        v.wait_ge(SEM["ss"], a3)
                        v.tensor_scalar(q2[:], qf[:], 0.0, None, ALU.max).then_inc(SEM["vv"], 1)
                    V(_); a4 = vv.inc()
                    def _(v, a3=a3, a4=a4, qsl=tt % 2, trfree=trv[tt - 2]):
                        v.wait_ge(SEM["ss"], a3)
                        v.wait_ge(SEM["vv"], a4)
                        if trfree:
                            v.wait_ge(SEM["pe"], trfree)
                        v.tensor_tensor(qphi[:, qsl], q3[:], q2[:], ALU.add).then_inc(SEM["vv"], 1)
                    V(_); a5 = vv.inc()
                    if pend is not None:
                        emit_qtr(*pend)
                    pend = (a5, tt)
                emit_qtr(*pend)
                q_rdy = vv.n
                ps_att = [ps_at, ps_q]
                at_free = [vv.n, vv.n]
                for j in range(4):
                    h = slab * 4 + j
                    hkv = h // 4
                    for tq in range(2):
                        ai = (j * 2 + tq) % 2
                        def _(t, q_rdy=q_rdy, j=j, hkv=hkv, tq=tq, ai=ai, attn_free=at_free[ai]):
                            t.wait_ge(SEM["vv"], max(q_rdy, attn_free))
                            t.matmul(ps_att[ai][:], kvt_bf[:, hkv * 128:(hkv + 1) * 128],
                                     qphiT[:, j, tq * 4:(tq + 1) * 4, :].rearrange("p a b -> p (a b)"),
                                     start=True, stop=True).then_inc(SEM["pe"], 1)
                        T(_); adn = pe.inc()
                        def _(v, adn=adn, h=h, hkv=hkv, tq=tq, ai=ai):
                            v.wait_ge(SEM["pe"], adn)
                            v.tensor_scalar(attnT[:, h, tq * 512:(tq + 1) * 512], ps_att[ai][:],
                                            recip_pp[:, hkv:hkv + 1], None, ALU.mult).then_inc(SEM["vv"], 1)
                        V(_); at_free[ai] = vv.inc()
                psq_free[0] = max(psq_free[0], at_free[1])
            attn_rdy = vv.n
            sto = [C["so0"], C["so1"]]
            for slab in range(8):
                slab_rdy = emit_build_slabC(wo0, slab)
                pss = [ps_s, ps_q]
                pss_free = [vv.n, vv.n]
                for sub in range(4):
                    o_chunk = slab * 4 + sub
                    for tq in range(2):
                        k3i = slab * 8 + sub * 2 + tq
                        slot = (sub * 2 + tq) % 2
                        def _(t, slab_rdy=slab_rdy, sub=sub, tq=tq, slot=slot, free=pss_free[slot]):
                            t.wait_ge(SEM["vv"], max(slab_rdy, attn_rdy, free))
                            for ccj in range(32):
                                mm = t.matmul(pss[slot][:], wt_slab[:, ccj, sub * 128:(sub + 1) * 128],
                                              attnT[:, ccj, tq * 512:(tq + 1) * 512],
                                              start=(ccj == 0), stop=(ccj == 31))
                            mm.then_inc(SEM["pe"], 1)
                        T(_); sd = pe.inc()
                        s_free = sto[k3i % 2].n
                        def _(v, sd=sd, o_chunk=o_chunk, slot=slot, s_free=s_free, k3i=k3i):
                            v.wait_ge(SEM["pe"], sd)
                            if k3i >= 2:
                                v.wait_ge(SEM["so0"] if k3i % 2 == 0 else SEM["so1"], s_free)
                            v.tensor_scalar(s_stc[:, k3i % 2], pss[slot][:], bo_sb[:, o_chunk:o_chunk + 1],
                                            None, ALU.add).then_inc(SEM["vv"], 1)
                        V(_); evv = vv.inc()
                        pss_free[slot] = evv
                        def _(s, evv=evv, o_chunk=o_chunk, tq=tq, k3i=k3i):
                            s.wait_ge(SEM["vv"], evv)
                            s.dma_start(out=sT_scr.tensor.ap()[o_chunk * 128:(o_chunk + 1) * 128, tq * 512:(tq + 1) * 512],
                                        in_=s_stc[:, k3i % 2]).then_inc(SEM["so0"] if k3i % 2 == 0 else SEM["so1"], 16)
                        Sy(_); sto[k3i % 2].inc(16)
        snC = snap()
        _pa.close()

        with ExitStack() as _pd:
            def sb(*a): return _pd.enter_context(nc.sbuf_tensor(*a))
            def psm(*a): return _pd.enter_context(nc.psum_tensor(*a))
            SLABS = [list(range(s * 8, min(s * 8 + 8, 86))) for s in range(11)]
            x_t = sb("x_tD", [128, 2048], F32)
            xtg = sb("xtgD", [128, 4, 32, 128], BF16)
            acc = sb("accD", [128, 4, 4096], F16)
            hsl = sb("hslD", [128, 8, 512], BF16)
            gw = sb("gwD", [128, 2, 32, 128], BF16)
            uw = sb("uwD", [128, 2, 32, 128], BF16)
            dw = sb("dwD", [128, 2, 8, 512], BF16)
            hgD = sb("hgD", [128, 8, 512], F32)
            s_t = sb("s_tD", [128, 2048], F32)
            g_t = sb("g_tD", [128, 2048], F32)
            tmpD = sb("tmpD", [128, 2048], F32)
            s_ld = sb("s_ldD", [128, 16, 128], F32)
            ps_tf = psm("ps_tfD", [128, 512], F32)
            ps_g = psm("ps_gD", [128, 512], F32)
            ps_g2 = psm("ps_g2D", [128, 512], F32)
            ps_u = psm("ps_uD", [128, 512], F32)
            ps_u2 = psm("ps_u2D", [128, 512], F32)
            ps_d = psm("ps_dD", [128, 512], F32)
            ps_d2 = psm("ps_d2D", [128, 512], F32)
            psgD = [ps_g, ps_g2]
            psuD = [ps_u, ps_u2]
            wl = [C["wl0"], C["wl1"]]
            ul = [C["ul0"], C["ul1"]]
            dl = [C["dl0"], C["dl1"]]
            xg = C["xg"]
            snC_ = snC
            sto_done0 = C["so0"].n; sto_done1 = C["so1"].n
            first_d = [True]
            for g2 in range(2):
                xg_prev = pe.n
                def _(g, g2=g2, xg_prev=xg_prev, snC_=snC_):
                    if g2 == 0:
                        wait_all(g, snC_)
                        g.wait_ge(SEM["cc"], wts_cc)
                    g.wait_ge(SEM["pe"], xg_prev)
                    for i in range(4):
                        g.dma_start(out=xtg[:, i].rearrange("p a b -> p (a b)"),
                                    in_=xt_scr[g2 * 4 + i]).then_inc(SEM["qtl"], 16)
                G(_); xg_ld = C["qtl"].inc(64)
                def _(v, w=vv.n, snC_=snC_, g2=g2):
                    if g2 == 0:
                        wait_all(v, snC_)
                    v.memset(acc.ap().rearrange("p a b -> p (a b)"), 0.0).then_inc(SEM["vv"], 1)
                V(_); acc_zero = vv.inc()
                prev_down_pe = pe.n
                sil_h = {-2: ss.n, -1: ss.n}
                mul_h = {-2: vv.n, -1: vv.n}
                for si, slab in enumerate(SLABS):
                    NS = len(slab)
                    pe_slab0 = pe.n
                    ss_slab0 = ss.n
                    def _(s, NS=NS, pe_slab0=pe_slab0, si8=si * 8):
                        for sj in range(NS):
                            s.wait_ge(SEM["pe"], pe_slab0 + sj + 1)
                            s.activation(hgD[:, sj], psgD[(si8 + sj) % 2][:],
                                         AF.Silu).then_inc(SEM["ss"], 1)
                    Sc(_); ss.inc(NS)
                    for sj, ncg in enumerate(slab):
                        dch = si * 8 + sj
                        p = dch % 2
                        wfree = pe.n if dch >= 2 or g2 > 0 else 0
                        def _(g, ncg=ncg, p=p, wfree=wfree):
                            if wfree > 0:
                                g.wait_ge(SEM["pe"], wfree)
                            g.dma_start(out=gw[:, p],
                                        in_=wg_all[ncg].rearrange("(c p) n -> p c n", p=128)).then_inc(SEM["wl0"] if p == 0 else SEM["wl1"], 16)
                        G(_); wv_t = wl[p].inc(16)
                        def _(s, ncg=ncg, p=p, wfree=wfree):
                            if wfree > 0:
                                s.wait_ge(SEM["pe"], wfree)
                            s.dma_start(out=uw[:, p],
                                        in_=wu_all[ncg].rearrange("(c p) n -> p c n", p=128)).then_inc(SEM["ul0"] if p == 0 else SEM["ul1"], 16)
                        Sy(_); uv_t = ul[p].inc(16)
                        sil = ss_slab0 + sj + 1
                        ps_free_s = sil_h.get(dch - 2, 0)
                        ps_free_v = mul_h.get(dch - 2, 0)
                        def _(t, p=p, wv_t=wv_t, uv_t=uv_t, xg_ld=xg_ld, ps_free_s=ps_free_s,
                              ps_free_v=ps_free_v, fd=first_d[0], snC_=snC_):
                            if fd:
                                wait_all(t, snC_)
                            t.wait_ge(SEM["wl0"] if p == 0 else SEM["wl1"], wv_t)
                            t.wait_ge(SEM["ul0"] if p == 0 else SEM["ul1"], uv_t)
                            t.wait_ge(SEM["qtl"], xg_ld)
                            if ps_free_s:
                                t.wait_ge(SEM["ss"], ps_free_s)
                            if ps_free_v:
                                t.wait_ge(SEM["vv"], ps_free_v)
                            for ccj in range(32):
                                t.matmul(psgD[p][:], gw[:, p, ccj], xtg[:, 0:4, ccj, :],
                                         start=(ccj == 0), stop=(ccj == 31))
                            for ccj in range(32):
                                mm = t.matmul(psuD[p][:], uw[:, p, ccj], xtg[:, 0:4, ccj, :],
                                              start=(ccj == 0), stop=(ccj == 31))
                            mm.then_inc(SEM["pe"], 1)
                        T(_); gu_done = pe.inc()
                        first_d[0] = False
                        sil_h[dch] = sil
                        def _(v, sil=sil, sj=sj, p=p, prev_down_pe=prev_down_pe):
                            v.wait_ge(SEM["ss"], sil)
                            v.wait_ge(SEM["pe"], prev_down_pe)
                            v.tensor_tensor(hsl[:, sj], hgD[:, sj], psuD[p][:], ALU.mult).then_inc(SEM["vv"], 1)
                        V(_); mul_h[dch] = vv.inc()
                    h_rdy = vv.n
                    S = len(slab)
                    psd = [ps_d, ps_d2]
                    psd_free = [vv.n, vv.n]
                    for o2 in range(8):
                        p = o2 % 2
                        dfree = pe.n if o2 >= 2 or si > 0 or g2 > 0 else 0
                        def _(s, slab=slab, o2=o2, p=p, dfree=dfree):
                            if dfree > 0:
                                s.wait_ge(SEM["pe"], dfree)
                            for sj, ncg in enumerate(slab):
                                s.dma_start(out=dw[:, p, sj],
                                            in_=wd_all.tensor.ap()[o2 * DI + ncg * 128: o2 * DI + (ncg + 1) * 128, :]).then_inc(SEM["dl0"] if p == 0 else SEM["dl1"], 16)
                        (G if p == 0 else Sy)(_); dv_t = dl[p].inc(16 * len(slab))
                        for t4 in range(4):
                            slot = (o2 * 4 + t4) % 2
                            def _(t, p=p, dv_t=dv_t, t4=t4, S=S, slot=slot, h_rdy=h_rdy, free=psd_free[slot]):
                                t.wait_ge(SEM["dl0"] if p == 0 else SEM["dl1"], dv_t)
                                t.wait_ge(SEM["vv"], max(h_rdy, free))
                                for sj in range(S):
                                    mm = t.matmul(psd[slot][:], hsl[:, sj, t4 * 128:(t4 + 1) * 128],
                                                  dw[:, p, sj], start=(sj == 0), stop=(sj == S - 1))
                                mm.then_inc(SEM["pe"], 1)
                            T(_); dd = pe.inc()
                            def _(v, dd=dd, t4=t4, o2=o2, slot=slot):
                                v.wait_ge(SEM["pe"], dd)
                                v.tensor_tensor(acc[:, t4, o2 * 512:(o2 + 1) * 512],
                                                acc[:, t4, o2 * 512:(o2 + 1) * 512],
                                                psd[slot][:], ALU.add).then_inc(SEM["vv"], 1)
                            V(_); psd_free[slot] = vv.inc()
                    prev_down_pe = pe.n
                ffn_done = vv.n
                for t4 in range(4):
                    i = g2 * 4 + t4
                    for half in range(2):
                        w0 = vv.n
                        def _(g, i=i, half=half, w0=w0, sd0=sto_done0, sd1=sto_done1):
                            g.wait_ge(SEM["vv"], w0)
                            if i == 0 and half == 0:
                                g.wait_ge(SEM["so0"], sd0)
                                g.wait_ge(SEM["so1"], sd1)
                            g.dma_start(out=x_t[:], in_=x0[i * 128:(i + 1) * 128, half * 2048:(half + 1) * 2048]).then_inc(SEM["xl0"], 16)
                            g.dma_start(out=s_ld.ap(),
                                        in_=sT_scr.tensor.ap()[half * 2048:(half + 1) * 2048, i * 128:(i + 1) * 128]
                                        .rearrange("(a p) t -> p a t", p=128)).then_inc(SEM["xl0"], 16)
                        G(_); xl = C["xl0"].inc(32)
                        for grp in range(4):
                            prev_evac = vv.n
                            def _(t, grp=grp, prev_evac=prev_evac, ffn_done=ffn_done, xl=xl):
                                t.wait_ge(SEM["vv"], max(prev_evac, ffn_done))
                                t.wait_ge(SEM["xl0"], xl)
                                for j in range(4):
                                    oc = grp * 4 + j
                                    mm = t.matmul(ps_tf[:, j * 128:(j + 1) * 128], s_ld[:, oc, :],
                                                  g_eyef[:], is_transpose=True, start=(j == 0), stop=True,
                                                  skip_group_check=True)
                                mm.then_inc(SEM["pe"], 1)
                            T(_); tr = pe.inc()
                            def _(v, tr=tr, grp=grp):
                                v.wait_ge(SEM["pe"], tr)
                                v.tensor_copy(s_t[:, grp * 512:(grp + 1) * 512], ps_tf[:]).then_inc(SEM["vv"], 1)
                            V(_); vv.inc()
                        s_rdy = vv.n
                        def _(s, t4=t4, half=half, w=s_rdy):
                            s.wait_ge(SEM["vv"], w)
                            s.activation(g_t[:], acc[:, t4, half * 2048:(half + 1) * 2048], AF.Silu).then_inc(SEM["ss"], 1)
                        Sc(_); gsil = ss.inc()
                        def _(v, gsil=gsil, xl=xl):
                            v.wait_ge(SEM["ss"], gsil)
                            v.wait_ge(SEM["xl0"], xl)
                            v.tensor_tensor(tmpD[:], s_t[:], g_t[:], ALU.mult).then_inc(SEM["vv"], 1)
                            v.wait_ge(SEM["vv"], vv.n + 1)
                            v.tensor_tensor(tmpD[:], x_t[:], tmpD[:], ALU.add).then_inc(SEM["vv"], 1)
                        V(_); cmb = vv.inc(2)
                        def _(s, cmb=cmb, i=i, half=half, g2=g2):
                            s.wait_ge(SEM["vv"], cmb)
                            dstx = x1_la if g2 == 0 else x1_lb
                            s.dma_start(out=dstx.tensor.ap()[(i % 4) * 128:(i % 4 + 1) * 128, half * 2048:(half + 1) * 2048],
                                        in_=tmpD[:]).then_inc(SEM["sl"], 16)
                        Sy(_); C["sl"].inc(16)
                        def _(v, w=C["sl"].n):
                            v.wait_ge(SEM["sl"], w)
                        V(_)
                def _(g, w=C["sl"].n, g2=g2):
                    g.wait_ge(SEM["sl"], w)
                    ins_, outs_ = (x1_la, x1_h1) if g2 == 0 else (x1_lb, x1_h2)
                    g.collective_compute("AllGather", ALU.bypass, replica_groups=GROUPS,
                                         ins=[ins_.tensor.ap().opt()],
                                         outs=[outs_.tensor.ap().opt()]).then_inc(SEM["cc"], 1)
                G(_); cc.inc()
            x1_cc = cc.n
        snD = snap()

        from concourse.bass import IndirectOffsetOnAxis
        with ExitStack() as _pe_:
            def sb(*a): return _pe_.enter_context(nc.sbuf_tensor(*a))
            def psm(*a): return _pe_.enter_context(nc.psum_tensor(*a))
            idx_sb = sb("idxE", [128, 8], mybir.dt.uint32)
            x_t = sb("x_tE", [128, D], F32)
            stats = sb("statsE", [128, 48], F32)
            mve = sb("mvE", [128, 2], F32)
            sc1 = sb("sc1E", [128, 6], F32)
            xn_bf = sb("xn_bfE", [128, 2048], BF16)
            xn_f = sb("xn_fE", [128, 2048], F32)
            xT_t = sb("xT_tE", [128, 32, 128], BF16)
            xts = sb("xtsE", [128, 2, 32, 128], BF16)
            wsl = sb("wslE", [128, 2, 32, 512], BF16)
            cos_t = sb("cos_tE", [128, 2, 256], F32)
            sin_t = sb("sin_tE", [128, 2, 256], F32)
            qfe = sb("qfE", [128, 512], F32)
            qre = sb("qrE", [128, 512], F32)
            t1b = sb("t1bE", [128, 256], F32)
            t2b = sb("t2bE", [128, 256], F32)
            sqb = sb("sqbE", [128, 512], F32)
            nrm = sb("nrmE", [128, 4], F32)
            knacc = sb("knaccE", [128, 8], F32)
            kn1 = sb("kn1E", [128, 1], F32)
            out_bf = sb("out_bfE", [128, 2, 512], BF16)
            ps_t = psm("ps_tE", [128, 512], BF16)
            ps_tf = psm("ps_tfE", [128, 512], F32)
            ps_a = psm("ps_aE", [128, 512], F32)
            ps_b = psm("ps_bE", [128, 512], F32)
            snD_ = snD
            def _(g, snD_=snD_):
                wait_all(g, snD_)
                g.wait_ge(SEM["cc"], x1_cc)
                g.dma_start(out=idx_sb.ap(), in_=xidx).then_inc(SEM["ld"], 16)
            G(_); ld.inc(16)
            idx_ld = ld.n
            def _(v, snD_=snD_):
                wait_all(v, snD_)
                v.memset(knacc[:], 0.0).then_inc(SEM["vv"], 1)
            V(_); kn_zero = vv.inc()
            for i in range(8):
                prev = vv.n
                def _(g, i=i, prev=prev, idx_ld=idx_ld):
                    g.wait_ge(SEM["vv"], prev)
                    g.wait_ge(SEM["ld"], idx_ld)
                    g.indirect_dma_start(out=x_t[:], out_offset=None,
                                         in_=(x1_h1 if i % 2 == 0 else x1_h2).tensor.ap(),
                                         in_offset=IndirectOffsetOnAxis(ap=idx_sb[:, i:i + 1], axis=0)).then_inc(SEM["xl0"], 16)
                G(_); xl1 = C["xl0"].inc(16)
                def _(v, xl1=xl1):
                    v.wait_ge(SEM["xl0"], xl1)
                    for cch in range(8):
                        op = v.bn_stats(stats[:, cch * 6:(cch + 1) * 6], x_t[:, cch * 512:(cch + 1) * 512])
                    op.then_inc(SEM["vv"], 1)
                    v.wait_ge(SEM["vv"], vv.n + 1)
                    v.bn_aggr(mve[:], stats[:]).then_inc(SEM["vv"], 1)
                    v.wait_ge(SEM["vv"], vv.n + 2)
                    v.tensor_scalar(sc1[:, 0:1], mve[:, 1:2], 1.0e-5, None, ALU.add).then_inc(SEM["vv"], 1)
                V(_); h2 = vv.inc(3)
                def _(s, h2=h2):
                    s.wait_ge(SEM["vv"], h2)
                    s.activation(sc1[:, 3:4], sc1[:, 0:1], AF.Ln).then_inc(SEM["ss"], 1)
                    s.wait_ge(SEM["ss"], ss.n + 1)
                    s.activation(sc1[:, 1:2], sc1[:, 3:4], AF.Exp, scale=-0.5).then_inc(SEM["ss"], 1)
                Sc(_); s2v = ss.inc(2)
                def _(v, s2v=s2v):
                    v.wait_ge(SEM["ss"], s2v)
                    v.tensor_scalar(sc1[:, 2:3], mve[:, 0:1], sc1[:, 1:2], -1.0, ALU.mult, ALU.mult).then_inc(SEM["vv"], 1)
                    v.wait_ge(SEM["vv"], vv.n + 1)
                    v.reciprocal(sc1[:, 4:5], sc1[:, 0:1]).then_inc(SEM["vv"], 1)
                    v.wait_ge(SEM["vv"], vv.n + 2)
                    v.tensor_scalar(sc1[:, 4:5], sc1[:, 4:5], mve[:, 1:2], 1.1920929e-07, ALU.mult, ALU.add).then_inc(SEM["vv"], 1)
                V(_); w4a = vv.inc(3)
                def _(s, w4a=w4a):
                    s.wait_ge(SEM["vv"], w4a)
                    s.activation(sc1[:, 5:6], sc1[:, 4:5], AF.Ln).then_inc(SEM["ss"], 1)
                    s.wait_ge(SEM["ss"], ss.n + 1)
                    s.activation(sc1[:, 4:5], sc1[:, 5:6], AF.Exp, scale=-0.5).then_inc(SEM["ss"], 1)
                Sc(_); rm2 = ss.inc(2)
                def _(s, rm2=rm2, i=i):
                    s.wait_ge(SEM["ss"], rm2)
                    s.dma_start(out=rms_scr[i], in_=sc1[:, 4:5]).then_inc(SEM["sl"], 16)
                Sy(_); C["sl"].inc(16)
                w4 = w4a
                for half in (1, 0):
                    def _(v, half=half, w4=w4, w=C["sl"].n):
                        v.wait_ge(SEM["vv"], w4)
                        v.wait_ge(SEM["sl"], w)
                        v.tensor_scalar(xn_f[:], x_t[:, half * 2048:(half + 1) * 2048],
                                        sc1[:, 1:2], sc1[:, 2:3], ALU.mult, ALU.add).then_inc(SEM["vv"], 1)
                        v.wait_ge(SEM["vv"], vv.n + 1)
                        v.tensor_copy(xn_bf[:], xn_f[:]).then_inc(SEM["vv"], 1)
                    V(_); xn_rdy = vv.inc(2)
                    def _(s, xn_rdy=xn_rdy, i=i, half=half):
                        s.wait_ge(SEM["vv"], xn_rdy)
                        s.dma_start(out=xn1_scr[i][:, half * 2048:(half + 1) * 2048], in_=xn_f.ap()).then_inc(SEM["sl"], 16)
                    Sy(_); C["sl"].inc(16)
                    cbase = 16 * half
                    for grp in range(4):
                        prev_evac = vv.n
                        def _(t, xn_rdy=xn_rdy, grp=grp, prev_evac=prev_evac):
                            t.wait_ge(SEM["vv"], xn_rdy)
                            t.wait_ge(SEM["vv"], prev_evac)
                            for j in range(4):
                                ccj = grp * 4 + j
                                mm = t.matmul(ps_t[:, j * 128:(j + 1) * 128],
                                              xn_bf[:, ccj * 128:(ccj + 1) * 128], g_eye[:],
                                              is_transpose=True, start=(j == 0), stop=True,
                                              skip_group_check=True)
                            mm.then_inc(SEM["pe"], 1)
                        T(_); tr = pe.inc()
                        def _(v, tr=tr, grp=grp, cbase=cbase):
                            v.wait_ge(SEM["pe"], tr)
                            v.tensor_copy(xT_t[:, cbase + grp * 4:cbase + grp * 4 + 4, :],
                                          ps_t.ap().rearrange("p (j n) -> p j n", n=128)).then_inc(SEM["vv"], 1)
                        V(_); vv.inc()
                def _(s, i=i, w=vv.n):
                    s.wait_ge(SEM["vv"], w)
                    s.dma_start(out=xt1_scr[i], in_=xT_t.ap().rearrange("p a b -> p (a b)")).then_inc(SEM["sl"], 16)
                Sy(_); C["sl"].inc(16)
                def _(v, w=C["sl"].n):
                    v.wait_ge(SEM["sl"], w)
                V(_)
            ph1_sl = C["sl"].n
            slabs = [("q", wq1, s) for s in range(8)] + [("k", wk1, s) for s in range(2)] + [("v", wv1, s) for s in range(2)]
            psq = [ps_a, ps_b]
            psq_free = [0, 0]
            xts_used = [pe.n, pe.n]
            sop = [C["so0"], C["so1"]]
            first_iter = True
            cl = [C["cl0"], C["cl1"]]
            qn = C["qn"]
            for si_, (kind, wt, sfb) in enumerate(slabs):
                sp = si_ % 2
                wfree = pe.n
                def _(g, wt=wt, sfb=sfb, sp=sp, wfree=wfree, first=first_iter):
                    if not first:
                        g.wait_ge(SEM["pe"], wfree)
                    for ccj in range(32):
                        g.dma_start(out=wsl[:, sp, ccj],
                                    in_=wt[ccj * 128:(ccj + 1) * 128, sfb * 512:(sfb + 1) * 512]).then_inc(SEM["wl0"] if sp == 0 else SEM["wl1"], 16)
                G(_); wv_t = C["wl0" if sp == 0 else "wl1"].inc(16 * 32)
                for i in range(8):
                    slot = i % 2
                    def _(g, i=i, slot=slot, used=xts_used[slot], first=first_iter, ph1_sl=ph1_sl):
                        if first and i < 2:
                            g.wait_ge(SEM["sl"], ph1_sl)
                        g.wait_ge(SEM["pe"], used)
                        g.dma_start(out=xts[:, slot].rearrange("p a b -> p (a b)"), in_=xt1_scr[i]).then_inc(SEM["xl0"] if slot == 0 else SEM["xl1"], 16)
                    G(_); xtv = C["xl0" if slot == 0 else "xl1"].inc(16)
                    if kind in ("q", "k"):
                        cfree = vv.n
                        def _(g, i=i, slot=slot, cfree=cfree):
                            g.wait_ge(SEM["vv"], cfree)
                            g.dma_start(out=cos_t[:, slot], in_=cosr[i * 128:(i + 1) * 128, :]).then_inc(SEM["cl0"] if slot == 0 else SEM["cl1"], 16)
                            g.dma_start(out=sin_t[:, slot], in_=sinr[i * 128:(i + 1) * 128, :]).then_inc(SEM["cl0"] if slot == 0 else SEM["cl1"], 16)
                        G(_); clv = cl[slot].inc(32)
                    else:
                        clv = cl[slot].n
                    def _(t, sp=sp, wv_t=wv_t, slot=slot, xtv=xtv, free=psq_free[slot]):
                        t.wait_ge(SEM["wl0"] if sp == 0 else SEM["wl1"], wv_t)
                        t.wait_ge(SEM["xl0"] if slot == 0 else SEM["xl1"], xtv)
                        if free:
                            t.wait_ge(SEM["vv"], free)
                        for ccj in range(32):
                            mm = t.matmul(psq[slot][:], xts[:, slot, ccj], wsl[:, sp, ccj],
                                          start=(ccj == 0), stop=(ccj == 31))
                        mm.then_inc(SEM["pe"], 1)
                    T(_); pd = pe.inc()
                    xts_used[slot] = pd
                    if kind == "v":
                        def _(v, pd=pd, slot=slot, sofree=sop[slot].n):
                            v.wait_ge(SEM["pe"], pd)
                            v.wait_ge(SEM["so0"] if slot == 0 else SEM["so1"], sofree)
                            v.tensor_copy(out_bf[:, slot], psq[slot][:]).then_inc(SEM["vv"], 1)
                        V(_); psq_free[slot] = vv.inc()
                        evx = psq_free[slot]
                        def _(s, evx=evx, i=i, sfb=sfb, slot=slot):
                            s.wait_ge(SEM["vv"], evx)
                            s.dma_start(out=v_l.tensor.ap()[i * 128:(i + 1) * 128, sfb * 512:(sfb + 1) * 512],
                                        in_=out_bf[:, slot]).then_inc(SEM["so0"] if slot == 0 else SEM["so1"], 16)
                        Sy(_); sop[slot].inc(16)
                    else:
                        def _(v, pd=pd, slot=slot, clv=clv):
                            v.wait_ge(SEM["pe"], pd)
                            v.wait_ge(SEM["cl0"] if slot == 0 else SEM["cl1"], clv)
                            v.tensor_copy(qfe[:], psq[slot][:]).then_inc(SEM["vv"], 1)
                        V(_); e0 = vv.inc()
                        psq_free[slot] = e0
                        def _(v, e0=e0, slot=slot):
                            v.wait_ge(SEM["vv"], e0)
                            qe = qfe[:, 0::2]
                            qo = qfe[:, 1::2]
                            v.tensor_tensor(t1b[:], qe, cos_t[:, slot], ALU.mult)
                            v.tensor_tensor(t2b[:], qo, sin_t[:, slot], ALU.mult).then_inc(SEM["vv"], 1)
                            v.wait_ge(SEM["vv"], vv.n + 1)
                            v.tensor_tensor(qre[:, 0::2], t1b[:], t2b[:], ALU.subtract).then_inc(SEM["vv"], 1)
                            v.wait_ge(SEM["vv"], vv.n + 2)
                            v.tensor_tensor(t1b[:], qe, sin_t[:, slot], ALU.mult)
                            v.tensor_tensor(t2b[:], qo, cos_t[:, slot], ALU.mult).then_inc(SEM["vv"], 1)
                            v.wait_ge(SEM["vv"], vv.n + 3)
                            v.tensor_tensor(qre[:, 1::2], t1b[:], t2b[:], ALU.add).then_inc(SEM["vv"], 1)
                        V(_); rope_done = vv.inc(4)
                        def _(v, rope_done=rope_done, slot=slot, sofree=sop[slot].n, qnfree=C["st"].n):
                            v.wait_ge(SEM["vv"], rope_done)
                            v.wait_ge(SEM["so0"] if slot == 0 else SEM["so1"], sofree)
                            v.tensor_copy(out_bf[:, slot], qre[:]).then_inc(SEM["vv"], 1)
                            v.tensor_tensor(sqb[:], qre[:], qre[:], ALU.mult).then_inc(SEM["vv"], 1)
                            v.wait_ge(SEM["vv"], vv.n + 2)
                            v.wait_ge(SEM["st"], qnfree)
                            v.tensor_reduce(nrm[:], sqb.ap().rearrange("p (h d) -> p h d", d=128),
                                            mybir.AxisListType.X, ALU.add).then_inc(SEM["vv"], 1)
                        V(_); nrm_done = vv.inc(3)
                        if kind == "q":
                            def _(s2_, nrm_done=nrm_done, i=i, sfb=sfb):
                                s2_.wait_ge(SEM["vv"], nrm_done)
                                s2_.dma_start(out=qn2_l.tensor.ap()[i * 128:(i + 1) * 128, sfb * 4:(sfb + 1) * 4],
                                              in_=nrm[:]).then_inc(SEM["st"], 16)
                            Sy(_); C["st"].inc(16)
                        else:
                            def _(v, nrm_done=nrm_done, sfb=sfb):
                                v.wait_ge(SEM["vv"], nrm_done)
                                v.tensor_tensor(knacc[:, sfb * 4:(sfb + 1) * 4],
                                                knacc[:, sfb * 4:(sfb + 1) * 4], nrm[:], ALU.max).then_inc(SEM["vv"], 1)
                            V(_); vv.inc()
                        out_dst = q_scr if kind == "q" else k_l
                        ob_done = vv.n
                        def _(s, ob_done=ob_done, i=i, sfb=sfb, slot=slot, out_dst=out_dst):
                            s.wait_ge(SEM["vv"], ob_done)
                            s.dma_start(out=out_dst.tensor.ap()[i * 128:(i + 1) * 128, sfb * 512:(sfb + 1) * 512],
                                        in_=out_bf[:, slot]).then_inc(SEM["so0"] if slot == 0 else SEM["so1"], 16)
                        Sy(_); sop[slot].inc(16)
                first_iter = False
            def _(v, w=vv.n):
                v.wait_ge(SEM["vv"], w)
                v.reduce_max(kn1[:], knacc[:], mybir.AxisListType.X).then_inc(SEM["vv"], 1)
            V(_); r1 = vv.inc()
            def _(t, r1=r1):
                t.wait_ge(SEM["vv"], r1)
                t.drain()
                t.matmul(ps_tf[0:1, 0:128], kn1[:], g_eyef[:], is_transpose=True,
                         start=True, stop=True, skip_group_check=True).then_inc(SEM["pe"], 1)
                t.drain()
            T(_); r2 = pe.inc()
            def _(v, r2=r2):
                v.wait_ge(SEM["pe"], r2)
                v.reduce_max(kn1[0:1, 0:1], ps_tf[0:1, 0:128], mybir.AxisListType.X).then_inc(SEM["vv"], 1)
            V(_); r3 = vv.inc()
            def _(s, r3=r3):
                s.wait_ge(SEM["vv"], r3)
                s.dma_start(out=knm_l.tensor.ap(), in_=kn1[0:1, 0:1]).then_inc(SEM["sl"], 16)
            Sy(_); C["sl"].inc(16)
            def _(g, w=C["sl"].n, w0=sop[0].n, w1=sop[1].n):
                g.wait_ge(SEM["sl"], w)
                g.wait_ge(SEM["so0"], w0)
                g.wait_ge(SEM["so1"], w1)
                g.collective_compute("AllGather", ALU.bypass, replica_groups=GROUPS,
                                     ins=[k_l.tensor.ap().opt()], outs=[k_all.tensor.ap().opt()]).then_inc(SEM["cc"], 1)
                g.collective_compute("AllGather", ALU.bypass, replica_groups=GROUPS,
                                     ins=[v_l.tensor.ap().opt()], outs=[v_all.tensor.ap().opt()]).then_inc(SEM["cc"], 1)
                g.collective_compute("AllReduce", ALU.max, replica_groups=GROUPS,
                                     ins=[knm_l.tensor.ap().opt()], outs=[knm_s.tensor.ap().opt()]).then_inc(SEM["cc"], 1)
            G(_); cc.inc(3)
            kv1_cc = cc.n
        snE = snap()

        with ExitStack() as _pf:
            def sb(*a): return _pf.enter_context(nc.sbuf_tensor(*a))
            def psm(*a): return _pf.enter_context(nc.psum_tensor(*a))
            mask_sb = sb("mask_sbF", [128, 512], F32)
            kn4 = sb("kn4F", [1, 4], F32)
            cbias = sb("cbiasF", [128, 1], F32)
            knm2c = sb("knm2cF", [128, 1], F32)
            qn2_t = sb("qn2_tF", [128, 32], F32)
            t1f = sb("t1F", [128, 32], F32)
            nb_all = sb("nb_allF", [128, 8, 32], F32)
            kT_h = sb("kT_hF", [128, 4096], BF16)
            kstg = sb("kstgF", [128, 128], BF16)
            v_aug = sb("v_augF", [128, 32, 132], BF16)
            probs = sb("probsF", [128, 8, 512], BF16)
            probsT = sb("probsTF", [128, 8, 512], BF16)
            rlf = sb("rlF", [128, 1], F32)
            ctx_bf = sb("ctx_bfF", [128, 128], BF16)
            ctxT_all = sb("ctxT_allF", [128, 32, 1024], BF16)
            ps_sc = [psm("ps_sc0F", [128, 512], F32), psm("ps_sc1F", [128, 512], F32),
                     psm("ps_sc2F", [128, 512], F32), psm("ps_sc3F", [128, 512], F32)]
            ps_pt = [psm("ps_pt0F", [128, 512], BF16), psm("ps_pt1F", [128, 512], BF16)]
            ps_ctx = [psm("ps_ctx0F", [128, 132], F32), psm("ps_ctx1F", [128, 132], F32)]
            snE_ = snE
            def _(g, snE_=snE_):
                wait_all(g, snE_)
                g.wait_ge(SEM["cc"], kv1_cc)
                g.dma_start(out=mask_sb[:], in_=mask1).then_inc(SEM["ld"], 16)
                g.dma_start(out=kn4[0:1, 0:1], in_=knm_s.tensor.ap()).then_inc(SEM["ld"], 16)
            G(_); ld.inc(32)
            f_ld = ld.n
            def _(v, snE_=snE_, f_ld=f_ld):
                wait_all(v, snE_)
                v.wait_ge(SEM["ld"], f_ld)
                v.memset(cbias[:], -2.4260151319598084).then_inc(SEM["vv"], 1)
            V(_); km = vv.inc()
            def _(t, km=km, snE_=snE_):
                wait_all(t, snE_)
                t.wait_ge(SEM["vv"], km)
                t.matmul(ps_ctx[0][:, 0:1], g_onesf[:], kn4[0:1, 0:1], start=True, stop=True,
                         skip_group_check=True).then_inc(SEM["pe"], 1)
                t.drain()
            T(_); kb = pe.inc()
            def _(v, kb=kb):
                v.wait_ge(SEM["pe"], kb)
                v.tensor_copy(knm2c[:], ps_ctx[0][:, 0:1]).then_inc(SEM["vv"], 1)
            V(_); kc_rdy = vv.inc()
            for qt in range(8):
                prev = vv.n
                def _(g, qt=qt, prev=prev):
                    g.wait_ge(SEM["vv"], prev)
                    g.dma_start(out=qn2_t[:], in_=qn2_l.tensor.ap()[qt * 128:(qt + 1) * 128, :]).then_inc(SEM["ql"], 16)
                G(_); qv = C["ql"].inc(16)
                def _(v, qv=qv, kc_rdy=kc_rdy):
                    v.wait_ge(SEM["ql"], qv)
                    v.wait_ge(SEM["vv"], kc_rdy)
                    v.tensor_scalar(t1f[:], qn2_t[:], knm2c[:, 0:1], None, ALU.mult).then_inc(SEM["vv"], 1)
                V(_); b1 = vv.inc()
                def _(s, b1=b1):
                    s.wait_ge(SEM["vv"], b1)
                    s.activation(t1f[:], t1f[:], AF.Ln).then_inc(SEM["ss"], 1)
                    s.wait_ge(SEM["ss"], ss.n + 1)
                    s.activation(t1f[:], t1f[:], AF.Exp, scale=0.5, bias=cbias[:, 0:1]).then_inc(SEM["ss"], 1)
                Sc(_); b2 = ss.inc(2)
                def _(v, b2=b2, qt=qt):
                    v.wait_ge(SEM["ss"], b2)
                    v.tensor_scalar(nb_all[:, qt, :], t1f[:], -1.0, None, ALU.mult).then_inc(SEM["vv"], 1)
                V(_); vv.inc()
            qt_cm = nc.sbuf_tensor("q_tF", [128, D], BF16)
            q_t = qt_cm.__enter__()
            qta_cm = nc.sbuf_tensor("qT_allF", [128, 32, 8, 128], BF16)
            qT_all = qta_cm.__enter__()
            for i in range(8):
                qfree = pe.n
                def _(g, i=i, qfree=qfree):
                    g.wait_ge(SEM["pe"], qfree)
                    g.dma_start(out=q_t[:], in_=q_scr.tensor.ap()[i * 128:(i + 1) * 128, :]).then_inc(SEM["qtl"], 16)
                G(_); qv = C["qtl"].inc(16)
                for grp in range(8):
                    prev_evac = vv.n
                    def _(t, qv=qv, grp=grp, prev_evac=prev_evac):
                        t.wait_ge(SEM["qtl"], qv)
                        t.wait_ge(SEM["vv"], prev_evac)
                        for j in range(4):
                            h = grp * 4 + j
                            mm = t.matmul(ps_pt[0][:, j * 128:(j + 1) * 128],
                                          q_t[:, h * 128:(h + 1) * 128], g_eye[:],
                                          is_transpose=True, start=(j == 0), stop=True,
                                          skip_group_check=True)
                        mm.then_inc(SEM["pe"], 1)
                    T(_); tr = pe.inc()
                    def _(v, tr=tr, grp=grp, i=i):
                        v.wait_ge(SEM["pe"], tr)
                        v.tensor_copy(qT_all[:, grp * 4:grp * 4 + 4, i, :],
                                      ps_pt[0].ap().rearrange("p (j n) -> p j n", n=128)).then_inc(SEM["vv"], 1)
                    V(_); vv.inc()
            prologue_done = vv.n
            ones_set = False
            ctx_free = {0: 0, 1: 0}
            tcount = [0]
            sf2 = {i2: 0 for i2 in range(4)}
            tf4 = {i2: 0 for i2 in range(8)}
            pf2 = {0: 0, 1: 0}
            ctf4 = {i2: 0 for i2 in range(8)}
            vload_free = pe.n
            for h in range(8):
                def _(g, h=h, vload_free=vload_free):
                    g.wait_ge(SEM["pe"], vload_free)
                    for gtile in range(32):
                        src_row = (gtile % 4) * 1024 + (gtile // 4) * 128
                        g.dma_start(out=v_aug[:, gtile, 0:128],
                                    in_=v_all.tensor.ap()[src_row:src_row + 128, h * 128:(h + 1) * 128]).then_inc(SEM["kl"], 16)
                G(_); vl = C["kl"].inc(16 * 32)
                if not ones_set:
                    def _(v):
                        v.memset(v_aug[:, 0:32, 128:129], 1.0).then_inc(SEM["vv"], 1)
                    V(_); vv.inc()
                    ones_set = True
                for gtile in range(32):
                    ks_free = pe.n
                    def _(g, h=h, gtile=gtile, ks_free=ks_free):
                        g.wait_ge(SEM["pe"], ks_free)
                        src_row = (gtile % 4) * 1024 + (gtile // 4) * 128
                        g.dma_start(out=kstg[:], in_=k_all.tensor.ap()[src_row:src_row + 128, h * 128:(h + 1) * 128]).then_inc(SEM["kl"], 16)
                    G(_); klv = C["kl"].inc(16)
                    prev_evac = vv.n
                    pfg = max(pf2[0], pf2[1])
                    def _(t, klv=klv, gtile=gtile, prev_evac=prev_evac, pfg=pfg):
                        t.wait_ge(SEM["kl"], klv)
                        if gtile % 4 == 0:
                            t.wait_ge(SEM["vv"], max(prev_evac, pfg))
                        mm = t.matmul(ps_pt[0][:, (gtile % 4) * 128:(gtile % 4 + 1) * 128], kstg[:], g_eye[:],
                                      is_transpose=True, start=(gtile % 4 == 0), stop=True,
                                      skip_group_check=True)
                        mm.then_inc(SEM["pe"], 1)
                    T(_); tr = pe.inc()
                    if gtile % 4 == 3:
                        def _(v, tr=tr, gtile=gtile):
                            v.wait_ge(SEM["pe"], tr)
                            v.tensor_copy(kT_h[:, (gtile - 3) * 128:(gtile + 1) * 128], ps_pt[0].ap()).then_inc(SEM["vv"], 1)
                        V(_); vv.inc()
                kvh_rdy = vv.n
                for qh in range(4):
                    h_q = h * 4 + qh
                    for qt in range(8):
                        nkc = qt + 1
                        tp = tcount[0] % 2
                        tcount[0] += 1
                        pe0 = pe.n; vv0 = vv.n; ss0 = ss.n
                        plan = []
                        for k0 in range(min(4, nkc)):
                            plan.append(("sc", k0))
                        for k in range(nkc):
                            if k + 4 < nkc:
                                plan.append(("sc", k + 4))
                            plan.append(("tr", k))
                            plan.append(("cx", k))
                        plan.append(("ct", None))
                        sc_d = {}; tr_d = {}; cx_d = {}; ct_d = 0
                        for idx, (op, k) in enumerate(plan):
                            val = pe0 + idx + 1
                            if op == "sc":
                                sc_d[k] = val
                            elif op == "tr":
                                tr_d[k] = val
                            elif op == "cx":
                                cx_d[k] = val
                            else:
                                ct_d = val
                        def evac_done(k):
                            return vv0 + k + 1 + (1 if k == nkc - 1 else 0)
                        mask_done = vv0 + nkc
                        def exp_done(k):
                            return ss0 + k + 1
                        sf_ = dict(sf2); tf_ = dict(tf4); pf_ = dict(pf2); ctf_ = dict(ctf4)
                        def _(s, h_q=h_q, qt=qt, nkc=nkc, sc_d=dict(sc_d), tr_d=dict(tr_d),
                              tf_=tf_, mask_done=mask_done):
                            for k in range(nkc):
                                s.wait_ge(SEM["pe"], sc_d[k])
                                if k == nkc - 1:
                                    s.wait_ge(SEM["vv"], mask_done)
                                if k >= 8:
                                    s.wait_ge(SEM["pe"], tr_d[k - 8])
                                elif tf_.get(k % 8):
                                    s.wait_ge(SEM["pe"], tf_[k % 8])
                                s.activation(probs[:, k % 8, :], ps_sc[k % 4][:], AF.Exp,
                                             scale=0.08838834764831845,
                                             bias=nb_all[:, qt, h_q:h_q + 1]).then_inc(SEM["ss"], 1)
                        Sc(_); ss.inc(nkc)
                        def _(t, h_q=h_q, qt=qt, nkc=nkc, kvh=kvh_rdy, pro=prologue_done,
                              plan=list(plan), sf_=sf_, pf_=pf_, exp_done=exp_done,
                              evac_done=evac_done, vv0=vv0, tp=tp, cfree=ctx_free[tp]):
                            t.wait_ge(SEM["vv"], max(kvh, pro))
                            for op, k in plan:
                                if op == "sc":
                                    if k >= 4:
                                        t.wait_ge(SEM["ss"], exp_done(k - 4))
                                    elif sf_.get(k % 4):
                                        t.wait_ge(SEM["ss"], sf_[k % 4])
                                    t.matmul(ps_sc[k % 4][:], qT_all[:, h_q, qt, :],
                                             kT_h[:, k * 512:(k + 1) * 512],
                                             start=True, stop=True).then_inc(SEM["pe"], 1)
                                elif op == "tr":
                                    t.wait_ge(SEM["ss"], exp_done(k))
                                    if k >= 2:
                                        t.wait_ge(SEM["vv"], evac_done(k - 2))
                                    elif pf_.get(k % 2):
                                        t.wait_ge(SEM["vv"], pf_[k % 2])
                                    for j in range(4):
                                        mm = t.matmul(ps_pt[k % 2][:, j * 128:(j + 1) * 128],
                                                      probs[:, k % 8, j * 128:(j + 1) * 128], g_eye[:],
                                                      is_transpose=True, start=(j == 0), stop=True,
                                                      skip_group_check=True)
                                    mm.then_inc(SEM["pe"], 1)
                                elif op == "cx":
                                    t.wait_ge(SEM["vv"], evac_done(k))
                                    if k == 0 and cfree:
                                        t.wait_ge(SEM["vv"], cfree)
                                    for j in range(4):
                                        mm = t.matmul(ps_ctx[tp][:, 0:129], probsT[:, k % 8, j * 128:(j + 1) * 128],
                                                      v_aug[:, k * 4 + j, 0:129],
                                                      start=(k == 0 and j == 0), stop=False,
                                                      skip_group_check=True)
                                    mm.then_inc(SEM["pe"], 1)
                                else:
                                    t.wait_ge(SEM["vv"], vv0 + nkc + 3)
                                    t.matmul(ps_pt[nkc % 2][:, 0:128], ctx_bf[:], g_eye[:], is_transpose=True,
                                             start=True, stop=True, skip_group_check=True).then_inc(SEM["pe"], 1)
                        T(_); pe.inc(len(plan))
                        def _(v, h_q=h_q, qt=qt, nkc=nkc, sc_d=dict(sc_d), tr_d=dict(tr_d),
                              cx_d=dict(cx_d), ct_d=ct_d, ctf_=ctf_, vv0=vv0, tp=tp):
                            for k in range(nkc):
                                if k == nkc - 1:
                                    v.wait_ge(SEM["pe"], sc_d[k])
                                    v.tensor_tensor(ps_sc[k % 4][:], ps_sc[k % 4][:],
                                                    mask_sb[:], ALU.add).then_inc(SEM["vv"], 1)
                                v.wait_ge(SEM["pe"], tr_d[k])
                                if k >= 8:
                                    v.wait_ge(SEM["pe"], cx_d[k - 8])
                                elif ctf_.get(k % 8):
                                    v.wait_ge(SEM["pe"], ctf_[k % 8])
                                v.tensor_copy(probsT[:, k % 8, :], ps_pt[k % 2][:]).then_inc(SEM["vv"], 1)
                            v.wait_ge(SEM["pe"], cx_d[nkc - 1])
                            v.reciprocal(rlf[:], ps_ctx[tp][:, 128:129]).then_inc(SEM["vv"], 1)
                            v.wait_ge(SEM["vv"], vv0 + nkc + 2)
                            v.tensor_scalar(ctx_bf[:], ps_ctx[tp][:, 0:128], rlf[:, 0:1], None, ALU.mult).then_inc(SEM["vv"], 1)
                            v.wait_ge(SEM["pe"], ct_d)
                            v.tensor_copy(ctxT_all[:, h_q, qt * 128:(qt + 1) * 128],
                                          ps_pt[nkc % 2][:, 0:128]).then_inc(SEM["vv"], 1)
                        V(_); vv.inc(nkc + 1 + 3)
                        for p in range(4):
                            ks = [k for k in range(nkc) if k % 4 == p]
                            if ks:
                                sf2[p] = exp_done(max(ks))
                        for s4_ in range(8):
                            ks = [k for k in range(nkc) if k % 8 == s4_]
                            if ks:
                                tf4[s4_] = tr_d[max(ks)]
                                ctf4[s4_] = cx_d[max(ks)]
                        for p in range(2):
                            ks = [k for k in range(nkc) if k % 2 == p]
                            if ks:
                                pf2[p] = evac_done(max(ks))
                        pf2[nkc % 2] = max(pf2[nkc % 2], vv0 + nkc + 4)
                        ctx_free[tp] = vv0 + nkc + 3
                vload_free = pe.n
            attn_done = vv.n
            qta_cm.__exit__(None, None, None)
            qt_cm.__exit__(None, None, None)
            wsl = sb("wslF", [128, 2, 32, 512], BF16)
            s_stf = sb("s_stF", [128, 2, 512], F32)
            ps_o = [ps_sc[0], ps_sc[1]]
            sop = [C["so0"], C["so1"]]
            for slab in range(8):
                sp = slab % 2
                wfree = pe.n
                def _(g, slab=slab, sp=sp, wfree=wfree, first=(slab < 2)):
                    if not first:
                        g.wait_ge(SEM["pe"], wfree)
                    for ccj in range(32):
                        g.dma_start(out=wsl[:, sp, ccj],
                                    in_=wo1[ccj * 128:(ccj + 1) * 128, slab * 512:(slab + 1) * 512]).then_inc(SEM["wl0"] if sp == 0 else SEM["wl1"], 16)
                G(_); wv_t = C["wl0" if sp == 0 else "wl1"].inc(16 * 32)
                for sub in range(4):
                    o_chunk = slab * 4 + sub
                    for tq in range(2):
                        k3i = slab * 8 + sub * 2 + tq
                        slot = k3i % 2
                        ofree = vv.n if k3i >= 2 else 0
                        def _(t, sp=sp, wv_t=wv_t, sub=sub, tq=tq, slot=slot, ofree=ofree, attn_done=attn_done):
                            t.wait_ge(SEM["wl0"] if sp == 0 else SEM["wl1"], wv_t)
                            t.wait_ge(SEM["vv"], max(attn_done, ofree))
                            for ccj in range(32):
                                mm = t.matmul(ps_o[slot][:], wsl[:, sp, ccj, sub * 128:(sub + 1) * 128],
                                              ctxT_all[:, ccj, tq * 512:(tq + 1) * 512],
                                              start=(ccj == 0), stop=(ccj == 31))
                            mm.then_inc(SEM["pe"], 1)
                        T(_); sd = pe.inc()
                        s_free = sop[k3i % 2].n
                        def _(v, sd=sd, slot=slot, k3i=k3i, s_free=s_free):
                            v.wait_ge(SEM["pe"], sd)
                            if k3i >= 2:
                                v.wait_ge(SEM["so0"] if k3i % 2 == 0 else SEM["so1"], s_free)
                            v.tensor_copy(s_stf[:, k3i % 2], ps_o[slot][:]).then_inc(SEM["vv"], 1)
                        V(_); evf = vv.inc()
                        def _(s, evf=evf, o_chunk=o_chunk, tq=tq, k3i=k3i):
                            s.wait_ge(SEM["vv"], evf)
                            s.dma_start(out=s1T_scr.tensor.ap()[o_chunk * 128:(o_chunk + 1) * 128, tq * 512:(tq + 1) * 512],
                                        in_=s_stf[:, k3i % 2]).then_inc(SEM["so0"] if k3i % 2 == 0 else SEM["so1"], 16)
                        Sy(_); sop[k3i % 2].inc(16)
        snF = snap()

        with ExitStack() as _pg:
            def sb(*a): return _pg.enter_context(nc.sbuf_tensor(*a))
            def psm(*a): return _pg.enter_context(nc.psum_tensor(*a))
            idx_sb = sb("idxG", [128, 8], mybir.dt.uint32)
            wrow = sb("wrowG", [16, D], F32)
            wallT = sb("wallTG", [128, 32, 128], BF16)
            wallf = sb("wallfG", [128, 24], F32)
            wb_bf = sb("wb_bfG", [128, 8, 1024], BF16)
            small = sb("smallG", [1, 32], F32)
            g_bcG = sb("g_bcG", [128, 2], F32)
            base_bc = sb("base_bcG", [128, 24], F32)
            base16r = sb("base16rG", [128, 128], F32)
            base4r = sb("base4rG", [128, 2, 32], F32)
            ig_rep = sb("ig_repG", [128, 128], F32)
            EG = sb("EG", [128, 128], F32)
            xts = sb("xtsG", [128, 2, 32, 128], BF16)
            rmsv = sb("rmsG", [128, 2, 1], F32)
            a_sb = sb("a_sbG", [32, 128], F32)
            z_all = sb("z_allG", [128, 8, 24], F32)
            MG = sb("MG", [128, 128], F32)
            SG = sb("SG", [128, 32], F32)
            LG = sb("LG", [128, 32], F32)
            PlG = sb("PlG", [128, 2, 32], F32)
            HrG = sb("HrG", [128, 128], F32)
            HpreG = sb("HpreG", [128, 2, 32], F32)
            xn_t = sb("xn_tG", [128, D], F32)
            xpG = sb("xpG", [128, 2, 1024], F32)
            xpTG = sb("xpTG", [128, 8, 128], BF16)
            accmG = sb("accmG", [128, 2, 1024], F32)
            g1f = sb("g1fG", [128, D], F32)
            s_ld = sb("s_ldG", [128, 16, 128], F32)
            s_tG = sb("s_tG", [128, 2048], F32)
            gtG = sb("gtG", [128, 2048], F32)
            tmpG = sb("tmpG", [128, 2048], F32)
            x_tG = sb("x_tG", [128, D], F32)
            xr_bfG = sb("xr_bfG", [128, 1024], BF16)
            ps_t = psm("ps_tG", [128, 512], BF16)
            ps_tf = psm("ps_tfG", [128, 512], F32)
            ps_a = psm("ps_aG", [128, 128], F32)
            ps_y = [psm("ps_y0G", [128, 512], F32), psm("ps_y1G", [128, 512], F32)]
            snF_ = snF
            sof_done0 = C["so0"].n; sof_done1 = C["so1"].n
            def _(g, snF_=snF_):
                wait_all(g, snF_)
                g.wait_ge(SEM["so0"], sof_done0)
                g.wait_ge(SEM["so1"], sof_done1)
                g.dma_start(out=idx_sb.ap(), in_=xidx).then_inc(SEM["ld"], 16)
                g.dma_start(out=small[0:1, 0:24], in_=bases).then_inc(SEM["ld"], 16)
                g.dma_start(out=small[0:1, 24:25], in_=gate).then_inc(SEM["ld"], 16)
                g.dma_start(out=EG[0:1, 0:16], in_=eye4).then_inc(SEM["ld"], 16)
            G(_); ld.inc(64)
            g_ld = ld.n
            def _(s, g_ld=g_ld, snF_=snF_):
                wait_all(s, snF_)
                s.wait_ge(SEM["ld"], g_ld)
                s.activation(small[0:1, 30:31], small[0:1, 24:25], AF.Exp, scale=-1.0).then_inc(SEM["ss"], 1)
            Sc(_); e1 = ss.inc()
            def _(v, e1=e1, snF_=snF_):
                wait_all(v, snF_)
                v.wait_ge(SEM["ss"], e1)
                v.tensor_scalar(small[0:1, 30:31], small[0:1, 30:31], 1.0, None, ALU.add).then_inc(SEM["vv"], 1)
                v.wait_ge(SEM["vv"], vv.n + 1)
                v.reciprocal(small[0:1, 31:32], small[0:1, 30:31]).then_inc(SEM["vv"], 1)
            V(_); gs_rdy = vv.inc(2)
            def _(t, gs_rdy=gs_rdy, snF_=snF_):
                wait_all(t, snF_)
                t.wait_ge(SEM["vv"], gs_rdy)
                t.matmul(ps_tf[:, 0:1], g_onesf[:], small[0:1, 31:32], start=True, stop=True, skip_group_check=True)
                t.matmul(ps_tf[:, 8:32], g_onesf[:], small[0:1, 0:24], start=False, stop=True, skip_group_check=True)
                t.matmul(ps_tf[:, 32:48], g_onesf[:], EG[0:1, 0:16], start=False, stop=True, skip_group_check=True).then_inc(SEM["pe"], 1)
                t.drain()
            T(_); bc1 = pe.inc()
            def _(v, bc1=bc1):
                v.wait_ge(SEM["pe"], bc1)
                v.tensor_copy(g_bcG[:, 0:1], ps_tf[:, 0:1])
                v.tensor_copy(base_bc[:], ps_tf[:, 8:32]).then_inc(SEM["vv"], 1)
                v.wait_ge(SEM["vv"], vv.n + 1)
                v.tensor_scalar(g_bcG[:, 1:2], g_bcG[:, 0:1], -1.0, 1.0, ALU.mult, ALU.add).then_inc(SEM["vv"], 1)
                v.wait_ge(SEM["vv"], vv.n + 2)
                for r in range(8):
                    op = v.tensor_scalar(ig_rep[:, r * 16:(r + 1) * 16], ps_tf[:, 32:48], g_bcG[:, 1:2], None, ALU.mult)
                op.then_inc(SEM["vv"], 1)
                for r in range(8):
                    op = v.tensor_copy(base16r[:, r * 16:(r + 1) * 16], base_bc[:, 0:16])
                for r in range(8):
                    op2 = v.tensor_copy(base4r[:, 0, r * 4:(r + 1) * 4], base_bc[:, 16:20])
                    op2 = v.tensor_copy(base4r[:, 1, r * 4:(r + 1) * 4], base_bc[:, 20:24])
                op2.then_inc(SEM["vv"], 1)
            V(_); bc_rdy = vv.inc(4)
            def _(v):
                v.memset(wallT.ap().rearrange("p a b -> p (a b)"), 0.0).then_inc(SEM["vv"], 1)
            V(_); wz = vv.inc()
            srcs = [(wres, 16, 0), (wpre, 4, 16), (wpost, 4, 20)]
            for wsrc, nr, coff in srcs:
                wfree = pe.n
                def _(g, wsrc=wsrc, nr=nr, wfree=wfree):
                    if wfree:
                        g.wait_ge(SEM["pe"], wfree)
                    g.dma_start(out=wrow[0:nr, :], in_=wsrc).then_inc(SEM["wll"], 16)
                G(_); wld = C["wll"].inc(16)
                for ccj in range(32):
                    prev = vv.n
                    def _(t, ccj=ccj, nr=nr, wld=wld, prev=prev, wz=wz):
                        t.wait_ge(SEM["wll"], wld)
                        t.wait_ge(SEM["vv"], max(prev, wz))
                        t.matmul(ps_tf[:, 0:nr], wrow[0:nr, ccj * 128:(ccj + 1) * 128],
                                 g_eyef[0:nr, 0:nr], is_transpose=True, start=True, stop=True,
                                 skip_group_check=True).then_inc(SEM["pe"], 1)
                    T(_); tr = pe.inc()
                    def _(v, tr=tr, ccj=ccj, coff=coff, nr=nr):
                        v.wait_ge(SEM["pe"], tr)
                        v.tensor_copy(wallf[:, 0:nr], ps_tf[:, 0:nr]).then_inc(SEM["vv"], 1)
                        v.wait_ge(SEM["vv"], vv.n + 1)
                        v.tensor_copy(wallT[:, ccj, coff:coff + nr], wallf[:, 0:nr]).then_inc(SEM["vv"], 1)
                    V(_); vv.inc(2)
            def _(t):
                t.drain()
                t.nop().then_inc(SEM["pe"], 1)
            T(_); pe.inc()
            wall_rdy = vv.n
            def _(g):
                for ccj in range(8):
                    g.dma_start(out=wb_bf[:, ccj], in_=wb[ccj * 128:(ccj + 1) * 128, :]).then_inc(SEM["wbl"], 16)
            G(_); wb_ld = C["wbl"].inc(16 * 8)
            xts_used = [pe.n, pe.n]
            rms_used = [0, 0]
            for i in range(8):
                slot = i % 2
                def _(g, i=i, slot=slot, used=xts_used[slot], rused=rms_used[slot]):
                    g.wait_ge(SEM["pe"], used)
                    if rused:
                        g.wait_ge(SEM["vv"], rused)
                    g.dma_start(out=xts[:, slot].rearrange("p a b -> p (a b)"), in_=xt1_scr[i]).then_inc(SEM["xl0"] if slot == 0 else SEM["xl1"], 16)
                    g.dma_start(out=rmsv[:, slot], in_=rms_scr[i]).then_inc(SEM["xl0"] if slot == 0 else SEM["xl1"], 16)
                G(_); xtv = C["xl0" if slot == 0 else "xl1"].inc(32)
                afree = vv.n
                def _(t, slot=slot, xtv=xtv, wall_rdy=wall_rdy, afree=afree):
                    t.wait_ge(SEM["xl0"] if slot == 0 else SEM["xl1"], xtv)
                    t.wait_ge(SEM["vv"], max(wall_rdy, afree))
                    for ccj in range(32):
                        mm = t.matmul(ps_a[:], wallT[:, ccj, :], xts[:, slot, ccj, :],
                                      start=(ccj == 0), stop=(ccj == 31))
                    mm.then_inc(SEM["pe"], 1)
                T(_); adn = pe.inc()
                xts_used[slot] = adn
                def _(v, adn=adn):
                    v.wait_ge(SEM["pe"], adn)
                    v.tensor_copy(a_sb[0:24, :], ps_a[0:24, :]).then_inc(SEM["vv"], 1)
                V(_); aev = vv.inc()
                def _(t, aev=aev):
                    t.wait_ge(SEM["vv"], aev)
                    t.matmul(ps_tf[:, 0:24], a_sb[0:24, :], g_eyef[0:24, 0:24], is_transpose=True,
                             start=True, stop=True, skip_group_check=True).then_inc(SEM["pe"], 1)
                T(_); atr = pe.inc()
                def _(v, atr=atr, i=i, slot=slot):
                    v.wait_ge(SEM["pe"], atr)
                    v.tensor_scalar(z_all[:, i, :], ps_tf[:, 0:24], rmsv[:, slot, 0:1], None, ALU.mult).then_inc(SEM["vv"], 1)
                V(_); rms_used[slot] = vv.inc()
            z_rdy = vv.n
            def _(v, z_rdy=z_rdy, bc_rdy=bc_rdy):
                v.wait_ge(SEM["vv"], max(z_rdy, bc_rdy))
                v.tensor_tensor(MG.ap().rearrange("p (g j) -> p g j", j=16), z_all[:, :, 0:16],
                                base16r.ap().rearrange("p (g j) -> p g j", j=16), ALU.add).then_inc(SEM["vv"], 1)
                v.wait_ge(SEM["vv"], vv.n + 1)
                v.tensor_scalar(MG[:], MG[:], 10.0, None, ALU.mult).then_inc(SEM["vv"], 1)
                v.tensor_tensor(PlG.ap().rearrange("p a (g j) -> p a g j", j=4)[:, 0],
                                z_all[:, :, 16:20],
                                base4r.ap().rearrange("p a (g j) -> p a g j", j=4)[:, 0], ALU.add)
                v.tensor_tensor(PlG.ap().rearrange("p a (g j) -> p a g j", j=4)[:, 1],
                                z_all[:, :, 20:24],
                                base4r.ap().rearrange("p a (g j) -> p a g j", j=4)[:, 1], ALU.add).then_inc(SEM["vv"], 1)
            V(_); m0 = vv.inc(3)
            vv0s = vv.n; ss0s = ss.n
            def _(s, vv0s=vv0s):
                for q in range(100):
                    s.wait_ge(SEM["vv"], vv0s + 2 * q)
                    s.activation(EG[:], MG[:], AF.Exp).then_inc(SEM["ss"], 1)
                    s.wait_ge(SEM["vv"], vv0s + 2 * q + 1)
                    s.activation(LG[:], SG[:], AF.Ln).then_inc(SEM["ss"], 1)
            Sc(_); ss.inc(200)
            def _(v, ss0s=ss0s):
                for q in range(100):
                    phase = q % 2
                    v.wait_ge(SEM["ss"], ss0s + 2 * q + 1)
                    if phase == 0:
                        v.tensor_reduce(SG[:], EG.ap().rearrange("p (a j) -> p a j", j=4),
                                        mybir.AxisListType.X, ALU.add).then_inc(SEM["vv"], 1)
                    else:
                        evw = EG.ap().rearrange("p (g i j) -> p g j i", i=4, j=4)
                        v.tensor_reduce(SG[:], evw, mybir.AxisListType.X, ALU.add).then_inc(SEM["vv"], 1)
                    v.wait_ge(SEM["ss"], ss0s + 2 * q + 2)
                    if phase == 0:
                        mview = MG.ap().rearrange("p (a j) -> p j a", j=4)
                        for j in range(4):
                            op = v.tensor_tensor(mview[:, j], mview[:, j], LG[:], ALU.subtract)
                    else:
                        mview = MG.ap().rearrange("p (g i j) -> p i g j", i=4, j=4)
                        lview = LG.ap().rearrange("p (g j) -> p g j", j=4)
                        for j in range(4):
                            op = v.tensor_tensor(mview[:, j], mview[:, j], lview, ALU.subtract)
                    op.then_inc(SEM["vv"], 1)
            V(_); vv.inc(200)
            cur_v = vv.n
            def _(s, cur_v=cur_v):
                s.wait_ge(SEM["vv"], cur_v)
                s.activation(HrG[:], MG[:], AF.Exp).then_inc(SEM["ss"], 1)
            Sc(_); hx = ss.inc()
            def _(v, hx=hx):
                v.wait_ge(SEM["ss"], hx)
                v.tensor_scalar(HrG[:], HrG[:], g_bcG[:, 0:1], None, ALU.mult).then_inc(SEM["vv"], 1)
                v.wait_ge(SEM["vv"], vv.n + 1)
                v.tensor_tensor(HrG[:], HrG[:], ig_rep[:], ALU.add).then_inc(SEM["vv"], 1)
            V(_); hr_rdy = vv.inc(2)
            def _(s, m0=m0):
                s.wait_ge(SEM["vv"], m0)
                s.activation(PlG[:], PlG[:], AF.Exp).then_inc(SEM["ss"], 1)
            Sc(_); px = ss.inc()
            def _(v, px=px):
                v.wait_ge(SEM["ss"], px)
                v.tensor_reduce(SG[:, 0:16], PlG.ap().rearrange("p a (g j) -> p (a g) j", j=4),
                                mybir.AxisListType.X, ALU.add).then_inc(SEM["vv"], 1)
                v.wait_ge(SEM["vv"], vv.n + 1)
                v.reciprocal(SG[:, 16:32], SG[:, 0:16]).then_inc(SEM["vv"], 1)
                v.wait_ge(SEM["vv"], vv.n + 2)
                for j in range(4):
                    pv = HpreG.ap().rearrange("p a (g j) -> p j (a g)", j=4)
                    ev2 = PlG.ap().rearrange("p a (g j) -> p j (a g)", j=4)
                    op = v.tensor_tensor(pv[:, j], ev2[:, j], SG[:, 16:32], ALU.mult)
                op.then_inc(SEM["vv"], 1)
            V(_); hp_rdy = vv.inc(3)
            xn_sl_done = C["sl"].n
            for i in range(8):
                w0 = vv.n
                def _(g, i=i, w0=w0, st0=xn_sl_done):
                    g.wait_ge(SEM["vv"], w0)
                    g.wait_ge(SEM["sl"], st0)
                    g.dma_start(out=xn_t[:], in_=xn1_scr[i]).then_inc(SEM["xl0"], 16)
                G(_); xnl = C["xl0"].inc(16)
                def _(v, xnl=xnl, i=i, hp_rdy=hp_rdy):
                    v.wait_ge(SEM["xl0"], xnl)
                    v.wait_ge(SEM["vv"], hp_rdy)
                    v.tensor_scalar(xpG[:, 0], xn_t[:, 0:1024], HpreG[:, 0, i * 4:i * 4 + 1], None, ALU.mult).then_inc(SEM["vv"], 1)
                    for j in range(1, 4):
                        v.wait_ge(SEM["vv"], vv.n + j)
                        v.scalar_tensor_tensor(xpG[:, j % 2], xn_t[:, j * 1024:(j + 1) * 1024],
                                               HpreG[:, 0, i * 4 + j:i * 4 + j + 1], xpG[:, (j + 1) % 2],
                                               ALU.mult, ALU.add).then_inc(SEM["vv"], 1)
                    v.wait_ge(SEM["vv"], vv.n + 4)
                    v.tensor_copy(xr_bfG[:], xpG[:, 1]).then_inc(SEM["vv"], 1)
                V(_); xpb = vv.inc(5)
                for grp in range(2):
                    prev_evac = vv.n
                    def _(t, xpb=xpb, grp=grp, prev_evac=prev_evac):
                        t.wait_ge(SEM["vv"], xpb)
                        t.wait_ge(SEM["vv"], prev_evac)
                        for j in range(4):
                            ccj = grp * 4 + j
                            mm = t.matmul(ps_t[:, j * 128:(j + 1) * 128],
                                          xr_bfG[:, ccj * 128:(ccj + 1) * 128], g_eye[:],
                                          is_transpose=True, start=(j == 0), stop=True,
                                          skip_group_check=True)
                        mm.then_inc(SEM["pe"], 1)
                    T(_); tr = pe.inc()
                    def _(v, tr=tr, grp=grp):
                        v.wait_ge(SEM["pe"], tr)
                        v.tensor_copy(xpTG[:, grp * 4:grp * 4 + 4, :],
                                      ps_t.ap().rearrange("p (j n) -> p j n", n=128)).then_inc(SEM["vv"], 1)
                    V(_); vv.inc()
                xpt_rdy = vv.n
                ydone = []
                for half in range(2):
                    def _(t, half=half, xpt_rdy=xpt_rdy, wb_ld=wb_ld, yfree=vv.n):
                        t.wait_ge(SEM["vv"], xpt_rdy)
                        t.wait_ge(SEM["wbl"], wb_ld)
                        t.wait_ge(SEM["vv"], yfree)
                        for ccj in range(8):
                            mm = t.matmul(ps_y[half][:], xpTG[:, ccj, :],
                                          wb_bf[:, ccj, half * 512:(half + 1) * 512],
                                          start=(ccj == 0), stop=(ccj == 7))
                        mm.then_inc(SEM["pe"], 1)
                    T(_); ydone.append(pe.inc())
                mix_last = 0
                for istr in range(4):
                    def _(v, istr=istr, i=i, ydone=ydone, hr_rdy=hr_rdy):
                        v.wait_ge(SEM["pe"], ydone[1])
                        v.wait_ge(SEM["vv"], hr_rdy)
                        v.tensor_scalar(accmG[:, 0], xn_t[:, 0:1024],
                                        HrG[:, i * 16 + istr * 4:i * 16 + istr * 4 + 1], None, ALU.mult).then_inc(SEM["vv"], 1)
                        for j in range(1, 4):
                            v.wait_ge(SEM["vv"], vv.n + j)
                            v.scalar_tensor_tensor(accmG[:, j % 2], xn_t[:, j * 1024:(j + 1) * 1024],
                                                   HrG[:, i * 16 + istr * 4 + j:i * 16 + istr * 4 + j + 1],
                                                   accmG[:, (j + 1) % 2], ALU.mult, ALU.add).then_inc(SEM["vv"], 1)
                        for half in range(2):
                            v.wait_ge(SEM["vv"], vv.n + 4 + half)
                            v.scalar_tensor_tensor(g1f[:, istr * 1024 + half * 512: istr * 1024 + (half + 1) * 512],
                                                   ps_y[half][:],
                                                   HpreG[:, 1, i * 4 + istr:i * 4 + istr + 1],
                                                   accmG[:, 1, half * 512:(half + 1) * 512], ALU.mult, ALU.add).then_inc(SEM["vv"], 1)
                    V(_); mix_last = vv.inc(6)
                xg_prev = vv.n
                def _(g, i=i, w1=xg_prev):
                    g.wait_ge(SEM["vv"], w1)
                    g.indirect_dma_start(out=x_tG[:], out_offset=None,
                                         in_=(x1_h1 if i % 2 == 0 else x1_h2).tensor.ap(),
                                         in_offset=IndirectOffsetOnAxis(ap=idx_sb[:, i:i + 1], axis=0)).then_inc(SEM["xl1"], 16)
                G(_); xfull = C["xl1"].inc(16)
                for half in range(2):
                    w1 = vv.n
                    def _(g, i=i, half=half, w1=w1):
                        g.wait_ge(SEM["vv"], w1)
                        g.dma_start(out=s_ld.ap(),
                                    in_=s1T_scr.tensor.ap()[half * 2048:(half + 1) * 2048, i * 128:(i + 1) * 128]
                                    .rearrange("(a p) t -> p a t", p=128)).then_inc(SEM["xl1"], 16)
                    G(_); xl4 = C["xl1"].inc(16)
                    for grp in range(4):
                        prev_evac = vv.n
                        def _(t, grp=grp, prev_evac=prev_evac, xl4=xl4):
                            t.wait_ge(SEM["xl1"], xl4)
                            t.wait_ge(SEM["vv"], prev_evac)
                            for j in range(4):
                                oc = grp * 4 + j
                                mm = t.matmul(ps_tf[:, j * 128:(j + 1) * 128], s_ld[:, oc, :],
                                              g_eyef[:], is_transpose=True, start=(j == 0), stop=True,
                                              skip_group_check=True)
                            mm.then_inc(SEM["pe"], 1)
                        T(_); tr = pe.inc()
                        def _(v, tr=tr, grp=grp):
                            v.wait_ge(SEM["pe"], tr)
                            v.tensor_copy(s_tG[:, grp * 512:(grp + 1) * 512], ps_tf[:]).then_inc(SEM["vv"], 1)
                        V(_); vv.inc()
                    s_rdy = vv.n
                    def _(s, half=half, w=mix_last, s_rdy=s_rdy):
                        s.wait_ge(SEM["vv"], max(w, s_rdy))
                        s.activation(gtG[:], g1f[:, half * 2048:(half + 1) * 2048], AF.Silu).then_inc(SEM["ss"], 1)
                    Sc(_); gsil = ss.inc()
                    def _(v, gsil=gsil, xl4=xl4, xfull=xfull, half=half):
                        v.wait_ge(SEM["ss"], gsil)
                        v.wait_ge(SEM["xl1"], max(xl4, xfull))
                        v.tensor_tensor(tmpG[:], s_tG[:], gtG[:], ALU.mult).then_inc(SEM["vv"], 1)
                        v.wait_ge(SEM["vv"], vv.n + 1)
                        v.tensor_tensor(tmpG[:], x_tG[:, half * 2048:(half + 1) * 2048], tmpG[:], ALU.add).then_inc(SEM["vv"], 1)
                    V(_); cmb = vv.inc(2)
                    def _(s, cmb=cmb, i=i, half=half):
                        s.wait_ge(SEM["vv"], cmb)
                        s.dma_start(out=out_q[i * 128:(i + 1) * 128, half * 2048:(half + 1) * 2048],
                                    in_=tmpG[:]).then_inc(SEM["st"], 16)
                    Sy(_); C["st"].inc(16)
                    def _(v, w=C["st"].n):
                        v.wait_ge(SEM["st"], w)
                    V(_)
            def _(s):
                s.wait_ge(SEM["st"], C["st"].n)
            Sy(_)
        if _dbg:
            scr_map = {"wg_l": wg_l, "wu_l": wu_l, "wd_l": wd_l, "sT_scr": sT_scr,
                       "x1_la": x1_la, "x1_lb": x1_lb, "q_scr": q_scr, "k_l": k_l,
                       "v_l": v_l, "qn2_l": qn2_l, "s1T_scr": s1T_scr, "xt_scr": xt_scr,
                       "xn1_scr": xn1_scr, "kvk_s": kvk_s, "kvk_l": kvk_l,
                       "k_all": k_all, "v_all": v_all}
            fin = {n: C[n].n for n in names}
            def _(g, fin=fin):
                for n2, v2 in fin.items():
                    if v2:
                        g.wait_ge(SEM[n2], v2)
                for n2 in _dbg:
                    if n2 in scr_map:
                        src2 = scr_map[n2]
                        dbg_t = nc.dram_tensor("dbg_" + n2, list(src2.tensor.shape),
                                               src2.tensor.dtype, kind="ExternalOutput").ap()
                        g.dma_start(out=dbg_t, in_=src2).then_inc(SEM["ld"], 16)
                        C["ld"].inc(16)
                g.wait_ge(SEM["ld"], C["ld"].n)
            G(_)
    return nc


def fused_in_maps(inp):
    eye = np.eye(128, dtype=BF16NP)
    eyef = np.eye(128, dtype=np.float32)
    selv = np.zeros((32, 4096), dtype=BF16NP)
    for kc in range(32):
        selv[kc, kc * 128:(kc + 1) * 128] = 1

    eye4v = np.eye(4, dtype=np.float32).reshape(1, 16)
    onesf = np.ones((1, 128), dtype=np.float32)
    basesv = np.concatenate([inp["L1_ffn_H_res_base"].reshape(-1),
                             inp["L1_ffn_H_pre_base"].reshape(-1),
                             inp["L1_ffn_H_post_base"].reshape(-1)]).reshape(1, 24)
    x = inp["L0_attn_hidden_states"].reshape(8192, D)
    qmaps = _dequant_inputs(inp)
    maps = []
    for c in range(NCORE):
        cq = c % 4
        m = np.zeros((128, 512), np.float32)
        for tic in range(4):
            if tic > cq:
                m[:, tic * 128:(tic + 1) * 128] = -1.0e30
            elif tic == cq:
                m[:, tic * 128:(tic + 1) * 128] = np.triu(np.full((128, 128), -1.0e30, np.float32), k=1)
        tiles = [4 * k + cq for k in range(8)]
        pos = np.concatenate([np.arange(t * 128, (t + 1) * 128) for t in tiles])
        half = 64
        inv = 1.0 / (10000.0 ** (np.arange(half, dtype=np.float64) / half))
        ang = pos[:, None].astype(np.float64) * inv[None, :]
        cosv = np.tile(np.cos(ang).astype(np.float32), (1, 4))
        sinv = np.tile(np.sin(ang).astype(np.float32), (1, 4))
        def _x1row(t):
            tok = np.arange(t * 128, (t + 1) * 128, dtype=np.uint32)
            cpr = tok // 1024
            r = tok % 1024
            return (cpr * 512 + (r % 512)).astype(np.uint32)
        xidxv = np.stack([_x1row(t) for t in tiles])
        mm_ = dict(
            x0=np.ascontiguousarray(x[c * TPC:(c + 1) * TPC]),
            wq0=inp["L0_attn_Wq"], bq0=inp["L0_attn_bq"].reshape(1, D),
            wk0=inp["L0_attn_Wk"], wv0=inp["L0_attn_Wv"],
            bk0=inp["L0_attn_bk"].reshape(1, KVD), bv0=inp["L0_attn_bv"].reshape(1, KVD),
            wo0=inp["L0_attn_Wo"],
            bo_pp=np.ascontiguousarray(inp["L0_attn_bo"].reshape(32, 128).T),
            wq1=inp["L1_attn_wq"], wk1=inp["L1_attn_wk"], wv1=inp["L1_attn_wv"],
            wo1=inp["L1_attn_wo"],
            wres=inp["L1_ffn_W_res"], wpre=inp["L1_ffn_W_pre"], wpost=inp["L1_ffn_W_post"],
            bases=np.ascontiguousarray(basesv), gate=inp["L1_ffn_gate"].reshape(1, 1),
            wb=inp["L1_ffn_W_branch"],
            eye=eye, eyef=eyef, onesf_in=onesf, eye4=eye4v, sel=selv,
            mask1=np.ascontiguousarray(m),
            cosr=np.ascontiguousarray(cosv), sinr=np.ascontiguousarray(sinv),
            xidx=np.ascontiguousarray(xidxv.T),
        )
        mm_.update(qmaps[c])
        mm_["dz"] = np.ascontiguousarray(np.concatenate(
            [mm_["dz"], np.zeros((10, 64), mm_["dz"].dtype)], axis=0))
        mm_["ds"] = np.ascontiguousarray(np.concatenate(
            [mm_["ds"], np.zeros((10, 512), mm_["ds"].dtype)], axis=0))
        maps.append(mm_)
    return maps


def kernel_fused(**inputs):
    nc = build_fused()
    maps = fused_in_maps(inputs)
    res = run_bass_kernel_spmd(nc, maps, core_ids=list(range(NCORE)))
    out = np.zeros((2, 4096, D), np.float32)
    for c in range(NCORE):
        b, cq = c // 4, c % 4
        o = res.results[c]["out_q"]
        for k in range(8):
            t = 4 * k + cq
            out[b, t * 128:(t + 1) * 128, :] = o[k * 128:(k + 1) * 128, :]
    return out
